# version 09
# baseline (speedup 1.0000x reference)
import contextlib
import numpy as np
import ml_dtypes

import concourse.bass as bass
import concourse.mybir as mybir
from concourse.bass_utils import run_bass_kernel_spmd

F32 = mybir.dt.float32
BF16 = mybir.dt.bfloat16
AF = mybir.ActivationFunctionType
ALU = mybir.AluOpType
AX = mybir.AxisListType
BF = ml_dtypes.bfloat16


class Ctr:
    def __init__(self, sem, step=1):
        self.sem = sem
        self.n = 0
        self.step = step

    def inc(self, instr, k=None):
        k = self.step if k is None else k
        instr.then_inc(self.sem, k)
        self.n += k
        return self.n


def build_core_program(n_cores, T_loc, D):
    NT = T_loc // 128
    NCH = D // 128
    NO = D // 512
    NQ = NT * NO
    NR = n_cores * NT

    nc = bass.Bass("TRN2", target_bir_lowering=False)

    dr_x = nc.dram_tensor("x", [T_loc, D], F32, kind="ExternalInput").ap()
    dr_wfT = nc.dram_tensor("wfT", [D, D], F32, kind="ExternalInput").ap()
    dr_wgT = nc.dram_tensor("wgT", [D, D], F32, kind="ExternalInput").ap()
    dr_va = nc.dram_tensor("va", [1, D], F32, kind="ExternalInput").ap()
    dr_vb = nc.dram_tensor("vb", [1, D], F32, kind="ExternalInput").ap()
    dr_vc = nc.dram_tensor("vc", [1, D], F32, kind="ExternalInput").ap()
    dr_vd = nc.dram_tensor("vd", [1, D], F32, kind="ExternalInput").ap()
    dr_vgb = nc.dram_tensor("vgb", [1, D], F32, kind="ExternalInput").ap()
    dr_vfb = nc.dram_tensor("vfb", [1, D], F32, kind="ExternalInput").ap()
    dr_vdlrv = nc.dram_tensor("vdlrv", [4, D], F32, kind="ExternalInput").ap()
    dr_vdlru = nc.dram_tensor("vdlru", [4, D], F32, kind="ExternalInput").ap()
    dr_tri = nc.dram_tensor("tri16", [128, 128], BF16, kind="ExternalInput").ap()
    dr_id = nc.dram_tensor("id16", [128, 128], BF16, kind="ExternalInput").ap()
    dr_ones2 = nc.dram_tensor("ones2_16", [2, 128], BF16, kind="ExternalInput").ap()
    dr_embZ = nc.dram_tensor("embZ16", [128, 16 * NT], BF16, kind="ExternalInput").ap()
    dr_embU = nc.dram_tensor("embU32", [128, 16 * NT], F32, kind="ExternalInput").ap()
    dr_tri9 = nc.dram_tensor("tri9", [NT, NT + 1], F32, kind="ExternalInput").ap()
    dr_onescol = nc.dram_tensor("onescol", [1, 128], F32, kind="ExternalInput").ap()
    dr_ones128 = nc.dram_tensor("ones128", [128, 1], F32, kind="ExternalInput").ap()
    dr_iotaP = nc.dram_tensor("iotaP", [1, 128], F32, kind="ExternalInput").ap()
    dr_iotaM16 = nc.dram_tensor("iotaM16", [1, 16], F32, kind="ExternalInput").ap()
    dr_iota9 = nc.dram_tensor("iota9", [1, NT + 1], F32, kind="ExternalInput").ap()
    dr_iota64 = nc.dram_tensor("iota64", [1, NR], F32, kind="ExternalInput").ap()
    dr_sel64 = nc.dram_tensor("sel64", [NR, 1], F32, kind="ExternalInput").ap()
    dr_pcs = nc.dram_tensor("pcs", [1, 4], F32, kind="ExternalInput").ap()

    dr_x1 = nc.dram_tensor("x1_sp", [T_loc, D], F32).ap()
    dr_u32 = nc.dram_tensor("u32_sp", [T_loc, D], F32).ap()
    dr_ug = nc.dram_tensor("ug_sp", [T_loc, D], F32).ap()
    dr_wf16 = nc.dram_tensor("wf16_sp", [128, NCH * D], BF16).ap()
    dr_cc_in = nc.dram_tensor("cc_in", [16, D], F32).ap()
    dr_cc_out = nc.dram_tensor("cc_out", [n_cores * 16, D], F32).ap()
    dr_y = nc.dram_tensor("y", [T_loc, D], F32, kind="ExternalOutput").ap()

    es = contextlib.ExitStack()
    sbt = lambda n, s, d: es.enter_context(nc.sbuf_tensor(n, s, d))
    pst = lambda n, s, d: es.enter_context(nc.psum_tensor(n, s, d))

    _open = {}

    def sopen(name, shape, dt):
        cm = nc.sbuf_tensor(name, shape, dt)
        t = cm.__enter__()
        _open[name] = cm
        return t

    def sclose(name):
        _open.pop(name).__exit__(None, None, None)

    W16 = sbt("W16", [128, NCH * D], BF16)
    xbuf = sbt("xbuf", [128, 3 * D], F32)
    ubuf = sbt("ubuf", [128, 3 * D], F32)
    h16b = sbt("h16b", [128, 2 * D], BF16)
    hTb = sbt("hTb", [128, 3 * D], BF16)
    zbuf32 = sbt("zbuf32", [128, 2 * 512], F32)
    zbufh = sbt("zbufh", [128, 2 * 512], BF16)
    zbufl = sbt("zbufl", [128, 2 * 512], BF16)
    Pt = sbt("Pt", [128, D], BF16)
    BRt = sbt("BRt", [128, D], BF16)
    row_lnaa = sbt("row_lnaa", [1, D], F32)
    row_br0 = sbt("row_br0", [1, D], BF16)
    biasF = sbt("biasF", [2, D], BF16)
    biasG = sbt("biasG", [2, D], BF16)
    statb = sbt("statb", [128, 3 * 6 * NO], F32)
    mvb = sbt("mvb", [128, 3 * 2], F32)
    rcol = sbt("rcol", [128, 3], F32)
    bcol = sbt("bcol", [128, 3], F32)
    abscol = sbt("abscol", [128, NCH + 1], F32)
    scol = sbt("scol", [128, 4], F32)
    row_m4 = sbt("row_m4", [4, 1], F32)
    mhalf = sbt("mhalf", [128, 1], F32)
    tri_sb = sbt("tri_sb", [128, 128], BF16)
    id_sb = sbt("id_sb", [128, 128], BF16)
    ones2_sb = sbt("ones2_sb", [2, 128], BF16)
    embZ_sb = sbt("embZ_sb", [128, 16 * NT], BF16)
    embU_sb = sbt("embU_sb", [128, 16 * NT], F32)
    tri9_sb = sbt("tri9_sb", [NT, NT + 1], F32)
    oc_sb = sbt("oc_sb", [1, 128], F32)
    o128_sb = sbt("o128_sb", [128, 1], F32)
    iotaP_sb = sbt("iotaP_sb", [1, 128], F32)
    iotaM16_sb = sbt("iotaM16_sb", [1, 16], F32)
    iota9_sb = sbt("iota9_sb", [1, NT + 1], F32)
    iota64_sb = sbt("iota64_sb", [1, NR], F32)
    sel64_sb = sbt("sel64_sb", [NR, 1], F32)
    pcs_sb = sbt("pcs_sb", [1, 4], F32)

    ps_mm = pst("ps_mm", [128, D], F32)
    ps_aux = pst("ps_aux", [128, 1024], F32)
    ps_tr = pst("ps_tr", [128, 512], BF16)
    ps_tr2 = pst("ps_tr2", [128, 512], BF16)

    sCst = Ctr(es.enter_context(nc.semaphore("sCst")), 16)
    sW = [Ctr(es.enter_context(nc.semaphore(f"sW{p}")), 16) for p in range(2)]
    sX = [Ctr(es.enter_context(nc.semaphore(f"sX{p}")), 16) for p in range(3)]
    sU = [Ctr(es.enter_context(nc.semaphore(f"sU{p}")), 16) for p in range(3)]
    sPx = [Ctr(es.enter_context(nc.semaphore(f"sPx{p}")), 16) for p in range(2)]
    sPg = [Ctr(es.enter_context(nc.semaphore(f"sPg{p}")), 16) for p in range(2)]
    sST = [Ctr(es.enter_context(nc.semaphore(f"sST{p}")), 16) for p in range(2)]
    sCY = [Ctr(es.enter_context(nc.semaphore(f"sCY{p}")), 16) for p in range(2)]
    sG = Ctr(es.enter_context(nc.semaphore("sG")), 16)
    sY = Ctr(es.enter_context(nc.semaphore("sY")), 16)
    sY2 = Ctr(es.enter_context(nc.semaphore("sY2")), 16)
    cT = Ctr(es.enter_context(nc.semaphore("cT")))
    cG = Ctr(es.enter_context(nc.semaphore("cG")))
    cV = Ctr(es.enter_context(nc.semaphore("cV")))
    cS = Ctr(es.enter_context(nc.semaphore("cS")))
    ccC = Ctr(es.enter_context(nc.semaphore("ccsem")))

    blk = es.enter_context(nc.Block())
    T, G, Y = blk.tensor, blk.gpsimd, blk.sync

    class _Chain:
        def __init__(self, eng, ctr):
            self._e = eng
            self._c = ctr

        def wait_ge(self, sem, n):
            self._e.wait_ge(sem, n)

        def __getattr__(self, name):
            fn = getattr(self._e, name)

            def call(*a, **k):
                self._e.wait_ge(self._c.sem, self._c.n)
                r = fn(*a, **k)
                self._c.inc(r)
                return r
            return call

    def V(f):
        blk.vector(lambda e: f(_Chain(e, cV)))

    def S(f):
        blk.scalar(lambda e: f(_Chain(e, cS)))

    def GC(f):
        blk.gpsimd(lambda e: f(_Chain(e, cG)))

    NG = max(1, NCH // 4)

    def emit_transpose_pass(j, s_src, s_dst, evac_ct_wait=None,
                            s_prefix=None, n_prefix=0, s_suffix=None, n_suffix=0):
        sbase = cS.n
        tbase = cT.n
        m_src = sbase + n_prefix
        ev_mark = lambda g: sbase + n_prefix + g + 1
        tr_mark = lambda g: tbase + min((g + 1) * 4, NCH)

        def tr_all(e):
            e.wait_ge(sCst.sem, m_consts)
            for g in range(NG):
                e.wait_ge(cS.sem, m_src if g < 2 else ev_mark(g - 2))
                pt = ps_tr if (g % 2) == 0 else ps_tr2
                for c4 in range(4):
                    c = g * 4 + c4
                    if c >= NCH:
                        break
                    cT.inc(e.transpose(out=pt[:, c4 * 128:(c4 + 1) * 128],
                                       in_=h16b[:, s_src + 128 * c:s_src + 128 * (c + 1)],
                                       identity=id_sb[:]))

        def merged(e):
            if s_prefix is not None:
                s_prefix(e)
            assert cS.n == m_src, f"prefix op count mismatch {cS.n} != {m_src}"
            if evac_ct_wait is not None:
                e.wait_ge(cT.sem, evac_ct_wait)
            for g in range(NG):
                e.wait_ge(cT.sem, tr_mark(g))
                pt = ps_tr if (g % 2) == 0 else ps_tr2
                n = min(4, NCH - g * 4) * 128
                e.activation(out=hTb[:, s_dst + g * 512:s_dst + g * 512 + n],
                             in_=pt[:, 0:n], func=AF.Identity)
            if s_suffix is not None:
                s_suffix(e)

        T(tr_all)
        assert cT.n == tbase + NCH
        S(merged)
        assert cS.n == sbase + n_prefix + NG + n_suffix, \
            f"scalar body count mismatch {cS.n} vs {sbase}+{n_prefix}+{NG}+{n_suffix}"
        return cT.n, sbase + n_prefix + NG

    ugob = sopen("ugob", [128, 2 * D], F32)
    wtmp = sopen("wtmp", [128, 2 * D], F32)
    r0 = ugob[0:1, 0:D]
    r1 = ugob[0:1, D:2 * D]
    rb16 = h16b[0:1, 0:D]

    def p_loads(e):
        for src, dst in [
            (dr_tri, tri_sb[:]), (dr_id, id_sb[:]), (dr_ones2, ones2_sb[:]),
            (dr_embZ, embZ_sb[:]), (dr_embU, embU_sb[:]), (dr_tri9, tri9_sb[:]),
            (dr_onescol, oc_sb[:]), (dr_ones128, o128_sb[:]),
            (dr_iotaP, iotaP_sb[:]),
            (dr_iotaM16, iotaM16_sb[:]), (dr_iota9, iota9_sb[:]),
            (dr_iota64, iota64_sb[:]), (dr_sel64, sel64_sb[:]), (dr_pcs, pcs_sb[:]),
            (dr_va, r0),
        ]:
            sCst.inc(e.dma_start(out=dst, in_=src))
    Y(p_loads)
    m_consts = sCst.n

    x_marks = [0] * NT

    def p_x01(e):
        sX[0].inc(e.dma_start(out=xbuf[:, 0:D], in_=dr_x[0:128, :]))
        x_marks[0] = sX[0].n
        sX[1].inc(e.dma_start(out=xbuf[:, D:2 * D], in_=dr_x[128:256, :]))
        x_marks[1] = sX[1].n
    Y(p_x01)

    def p_mh(e):
        e.memset(mhalf[:, :], -0.5)
    V(p_mh)
    m_mhalf = cV.n

    mk_x1_tt = [0] * NT
    mk_u32 = [0] * NT
    mk_rb = [0] * NT
    mk_rb2 = [0] * NT
    mk_spill = [0] * NT
    mk_hT = [0] * NT
    mk_trevac = [0] * NT
    mk_mmA = [0] * NT
    mk_mmAc = [[0] * NO for _ in range(NT)]
    mk_x1c = [[0] * NO for _ in range(NT)]
    mk_bcol = [0] * NT

    def a_pre(j):
        s = (j % 3) * D
        c3 = j % 3

        def a_stats(e, j=j, s=s, c3=c3):
            e.wait_ge(sX[j % 3].sem, x_marks[j])
            if j >= 3:
                e.wait_ge(cG.sem, mk_rb2[j - 3])
            so = c3 * 6 * NO
            for c in range(NO):
                e.bn_stats(out=statb[:, so + 6 * c:so + 6 * c + 6],
                           in_=xbuf[:, s + 512 * c:s + 512 * (c + 1)])
            e.bn_aggr(out=mvb[:, c3 * 2:c3 * 2 + 2],
                      in_=statb[:, so:so + 6 * NO])
        V(a_stats)
        m_st = cV.n

        def a_rb(e, j=j, c3=c3):
            e.wait_ge(cV.sem, m_st)
            e.wait_ge(cV.sem, m_mhalf)
            if j >= 3:
                e.wait_ge(cV.sem, mk_u32[j - 3])
            e.tensor_scalar(out=rcol[:, c3:c3 + 1], in0=mvb[:, 2 * c3 + 1:2 * c3 + 2],
                            scalar1=1e-5, scalar2=None, op0=ALU.add, op1=ALU.bypass)
            e.tensor_tensor(out=rcol[:, c3:c3 + 1], in0=rcol[:, c3:c3 + 1],
                            in1=mhalf[:, :], op=ALU.pow)
        GC(a_rb)
        mk_rb[j] = cG.n

        def a_b(e, j=j, c3=c3):
            e.wait_ge(cG.sem, mk_rb[j])
            e.tensor_scalar(out=bcol[:, c3:c3 + 1], in0=mvb[:, 2 * c3:2 * c3 + 1],
                            scalar1=rcol[:, c3:c3 + 1], scalar2=-1.0,
                            op0=ALU.mult, op1=ALU.mult)
        V(a_b)
        mk_bcol[j] = cV.n

    def a_pre_t(j):
        s = (j % 3) * D
        c3 = j % 3

        def h1_prefix(e, j=j, s=s, c3=c3):
            e.wait_ge(cV.sem, mk_bcol[j])
            if j >= 2:
                e.wait_ge(cT.sem, mk_hT[j - 2])
            e.activation(out=h16b[:, (j % 2) * D:(j % 2) * D + D],
                         in_=xbuf[:, s:s + D],
                         func=AF.Identity, bias=bcol[:, c3:c3 + 1],
                         scale=rcol[:, c3:c3 + 1])

        evac_ct = mk_mmAc[j - 3][NO - 1] if j >= 3 else None
        mk_hT[j], mk_trevac[j] = emit_transpose_pass(
            j, (j % 2) * D, (j % 3) * D, evac_ct_wait=evac_ct,
            s_prefix=h1_prefix, n_prefix=1)

    def a_mm(j):
        s2 = (j % 3) * D
        for o in range(NO):
            def a_mmo(e, j=j, s2=s2, o=o):
                if o == 0:
                    e.wait_ge(cS.sem, mk_trevac[j])
                    e.wait_ge(cV.sem, tern_marks[NCH - 1])
                    e.wait_ge(cV.sem, m_psmm_free_rows)
                    e.wait_ge(cS.sem, m_P)
                    e.wait_ge(sG.sem, m_fb5)
                if j >= 1:
                    e.wait_ge(cV.sem, mk_x1c[j - 1][o])
                for i in range(NCH):
                    e.matmul(out=ps_mm[:, o * 512:(o + 1) * 512],
                             lhsT=hTb[:, s2 + 128 * i:s2 + 128 * (i + 1)],
                             rhs=W16[:, i * D + o * 512:i * D + o * 512 + 512],
                             start=(i == 0), stop=False)
                cT.inc(e.matmul(out=ps_mm[:, o * 512:(o + 1) * 512], lhsT=ones2_sb[:],
                                rhs=biasF[:, o * 512:(o + 1) * 512], start=False, stop=True))
            T(a_mmo)
            mk_mmAc[j][o] = cT.n
        mk_mmA[j] = mk_mmAc[j][NO - 1]

    def a_post(j):
        s = (j % 3) * D
        c3 = j % 3
        for o in range(NO):
            def a_x1o(e, j=j, s=s, o=o):
                e.wait_ge(cT.sem, mk_mmAc[j][o])
                e.tensor_tensor(out=xbuf[:, s + o * 512:s + o * 512 + 512],
                                in0=xbuf[:, s + o * 512:s + o * 512 + 512],
                                in1=ps_mm[:, o * 512:(o + 1) * 512], op=ALU.add)
            V(a_x1o)
            mk_x1c[j][o] = cV.n
        mk_x1_tt[j] = mk_x1c[j][NO - 1]

        def a_stats2(e, j=j, s=s, c3=c3):
            e.wait_ge(cG.sem, mk_rb[j])
            so = c3 * 6 * NO
            for c in range(NO):
                e.bn_stats(out=statb[:, so + 6 * c:so + 6 * c + 6],
                           in_=xbuf[:, s + 512 * c:s + 512 * (c + 1)])
            e.bn_aggr(out=mvb[:, c3 * 2:c3 * 2 + 2],
                      in_=statb[:, so:so + 6 * NO])
        V(a_stats2)
        m_st2 = cV.n

        def a_rb2(e, j=j, c3=c3):
            e.wait_ge(cV.sem, m_st2)
            e.wait_ge(cS.sem, mk_h1_next[0])
            e.tensor_scalar(out=rcol[:, c3:c3 + 1], in0=mvb[:, 2 * c3 + 1:2 * c3 + 2],
                            scalar1=1e-5, scalar2=None, op0=ALU.add, op1=ALU.bypass)
            e.tensor_tensor(out=rcol[:, c3:c3 + 1], in0=rcol[:, c3:c3 + 1],
                            in1=mhalf[:, :], op=ALU.pow)
        GC(a_rb2)
        mk_rb2[j] = cG.n

        def a_b2(e, j=j, c3=c3):
            e.wait_ge(cG.sem, mk_rb2[j])
            e.tensor_scalar(out=bcol[:, c3:c3 + 1], in0=mvb[:, 2 * c3:2 * c3 + 1],
                            scalar1=rcol[:, c3:c3 + 1], scalar2=-1.0,
                            op0=ALU.mult, op1=ALU.mult)
        V(a_b2)
        m_bb2 = cV.n

        def a_u32(e, j=j, s=s, c3=c3):
            if j >= 2:
                e.wait_ge(sPx[j % 2].sem, mk_spill[j - 2])
            e.tensor_scalar(out=ubuf[:, (j % 2) * D:(j % 2) * D + D],
                            in0=xbuf[:, s:s + D],
                            scalar1=rcol[:, c3:c3 + 1], scalar2=bcol[:, c3:c3 + 1],
                            op0=ALU.mult, op1=ALU.add)
        V(a_u32)
        mk_u32[j] = cV.n

        def a_spill(e, j=j, s=s):
            e.wait_ge(cV.sem, mk_u32[j])
            sPx[j % 2].inc(e.dma_start(out=dr_x1[128 * j:128 * (j + 1), :],
                                       in_=xbuf[:, s:s + D]))
            sPx[j % 2].inc(e.dma_start(out=dr_u32[128 * j:128 * (j + 1), :],
                                       in_=ubuf[:, (j % 2) * D:(j % 2) * D + D]))
        G(a_spill)
        mk_spill[j] = sPx[j % 2].n

        if j + 3 < NT:
            def a_xpre(e, j=j):
                e.wait_ge(sPx[j % 2].sem, mk_spill[j])
                sX[(j + 3) % 3].inc(e.dma_start(
                    out=xbuf[:, ((j + 3) % 3) * D:((j + 3) % 3) * D + D],
                    in_=dr_x[128 * (j + 3):128 * (j + 4), :]))
                x_marks[j + 3] = sX[(j + 3) % 3].n
            Y(a_xpre)

    if NT > 2:
        def p_x2(e):
            sX[2].inc(e.dma_start(out=xbuf[:, 2 * D:3 * D], in_=dr_x[256:384, :]))
            x_marks[2] = sX[2].n
        Y(p_x2)

    mk_h1_next = [0]
    a_pre(0)
    if NT > 1:
        a_pre(1)
    a_pre_t(0)
    if NT > 1:
        a_pre_t(1)
    mk_h1_next[0] = cS.n

    m_wchunk = [0] * NCH
    absred_marks = [0] * NCH
    for k in range(NCH):
        def p_wld(e, k=k):
            if k >= 2:
                e.wait_ge(cV.sem, absred_marks[k - 2])
            sW[k % 2].inc(e.dma_start(out=wtmp[:, (k % 2) * D:(k % 2 + 1) * D],
                                      in_=dr_wfT[128 * k:128 * (k + 1), :]))
            m_wchunk[k] = sW[k % 2].n
        G(p_wld)

        def p_absk(e, k=k):
            e.wait_ge(sW[k % 2].sem, m_wchunk[k])
            e.tensor_reduce(out=abscol[:, k:k + 1],
                            in_=wtmp[:, (k % 2) * D:(k % 2 + 1) * D],
                            axis=AX.X, op=ALU.add, apply_absolute_value=True)
            absred_marks[k] = cV.n
        V(p_absk)

    def p_absred(e):
        e.tensor_reduce(out=abscol[:, NCH:NCH + 1], in_=abscol[:, 0:NCH],
                        axis=AX.X, op=ALU.add)
        e.memset(scol[0:1, 3:4], 1e-5)
    V(p_absred)
    m_absred = cV.n

    def p_scale_mm(e):
        e.wait_ge(sCst.sem, m_consts)
        e.wait_ge(cV.sem, m_absred)
        cT.inc(e.matmul(out=ps_aux[0:1, 0:1], lhsT=o128_sb[:], rhs=abscol[:, NCH:NCH + 1],
                        start=True, stop=True))
    T(p_scale_mm)
    m_scale_mm = cT.n

    def p_scale(e):
        e.wait_ge(cT.sem, m_scale_mm)
        e.activation(out=scol[0:1, 2:3], in_=ps_aux[0:1, 0:1], func=AF.Identity,
                     scale=0.5 / (D * D))
    S(p_scale)
    m_hs = cS.n

    def p_thr(e):
        e.wait_ge(cS.sem, m_hs)
        e.tensor_scalar(out=scol[0:1, 0:1], in0=scol[0:1, 2:3],
                        scalar1=5e-11, scalar2=None, op0=ALU.add, op1=ALU.bypass)
        e.tensor_scalar(out=scol[0:1, 1:2], in0=scol[0:1, 0:1],
                        scalar1=-1.0, scalar2=None, op0=ALU.mult, op1=ALU.bypass)
    V(p_thr)
    m_thr = cV.n

    def p_thr_b(e):
        e.wait_ge(cV.sem, m_thr)
        cT.inc(e.matmul(out=ps_aux[:, 4:8], lhsT=oc_sb[:], rhs=scol[0:1, 0:4],
                        start=True, stop=True))
    T(p_thr_b)
    m_thr_b = cT.n

    def p_thr_c(e):
        e.wait_ge(cT.sem, m_thr_b)
        e.activation(out=scol[:, 0:4], in_=ps_aux[:, 4:8], func=AF.Identity)
    S(p_thr_c)
    m_thr_c = cS.n

    def p_rows1(e):
        e.wait_ge(sCst.sem, m_consts)
        e.activation(out=r1, in_=r0, func=AF.Tanh)
        e.activation(out=row_lnaa[:], in_=r1, func=AF.Ln)
    S(p_rows1)
    m_rows1 = cS.n

    def p_br0a(e):
        e.wait_ge(cS.sem, m_rows1)
        e.tensor_scalar(out=r0, in0=r1, scalar1=pcs_sb[0:1, 1:2],
                        scalar2=None, op0=ALU.max, op1=ALU.bypass)
        e.reciprocal(out=r1, in_=r0)
    V(p_br0a)
    m_br0a = cV.n

    def p_br0b(e):
        e.wait_ge(cV.sem, m_br0a)
        sCst.inc(e.dma_start(out=r0, in_=dr_vb))
    Y(p_br0b)
    m_vb1 = sCst.n

    def p_br0c(e):
        e.wait_ge(sCst.sem, m_vb1)
        e.tensor_tensor(out=r1, in0=r1, in1=r0, op=ALU.mult)
    V(p_br0c)
    m_br0c = cV.n

    def p_br0d(e):
        e.wait_ge(cV.sem, m_br0c)
        sCst.inc(e.dma_start(out=r0, in_=dr_vc))
    Y(p_br0d)
    m_vc1 = sCst.n

    def p_br0e(e):
        e.wait_ge(sCst.sem, m_vc1)
        e.tensor_tensor(out=r1, in0=r1, in1=r0, op=ALU.mult)
    V(p_br0e)
    m_br0e = cV.n

    def p_br0f(e):
        e.wait_ge(cV.sem, m_br0e)
        e.activation(out=row_br0[:], in_=r1, func=AF.Identity)
    S(p_br0f)
    m_br0s = cS.n
    m_br0 = m_br0e

    def p_r1P(e):
        e.wait_ge(cS.sem, m_rows1)
        for o in range(NO):
            cT.inc(e.matmul(out=ps_mm[:, o * 512:(o + 1) * 512], lhsT=iotaP_sb[:],
                            rhs=row_lnaa[0:1, o * 512:(o + 1) * 512], start=True, stop=True))
    T(p_r1P)
    m_r1P = cT.n

    def p_P(e):
        e.wait_ge(cT.sem, m_r1P)
        e.activation(out=Pt[:, :], in_=ps_mm[:, :], func=AF.Exp)
    S(p_P)
    m_P = cS.n

    brstage = ubuf[:, 0:D]

    def p_BR1(e):
        e.wait_ge(cT.sem, m_r1P)
        e.activation(out=brstage, in_=ps_mm[:, :], func=AF.Exp, scale=-1.0)
    S(p_BR1)
    m_BR1 = cS.n

    def p_bload(e):
        e.wait_ge(cS.sem, m_br0s)
        sCst.inc(e.dma_start(out=r1, in_=dr_vb))
    Y(p_bload)
    m_vb2 = sCst.n

    def p_r1b2(e):
        e.wait_ge(cS.sem, m_BR1)
        e.wait_ge(sCst.sem, m_vb2)
        for o in range(NO):
            cT.inc(e.matmul(out=ps_mm[:, o * 512:(o + 1) * 512], lhsT=oc_sb[:],
                            rhs=r1[0:1, o * 512:(o + 1) * 512], start=True, stop=True))
    T(p_r1b2)
    m_r1b = cT.n

    def p_BR2(e):
        e.wait_ge(cT.sem, m_r1b)
        e.tensor_tensor(out=brstage, in0=brstage, in1=ps_mm[:, :], op=ALU.mult)
    V(p_BR2)
    m_BR2 = cV.n

    def p_r1c(e):
        e.wait_ge(cV.sem, m_BR2)
        for o in range(NO):
            cT.inc(e.matmul(out=ps_mm[:, o * 512:(o + 1) * 512], lhsT=oc_sb[:],
                            rhs=r0[0:1, o * 512:(o + 1) * 512], start=True, stop=True))
    T(p_r1c)
    m_r1c = cT.n

    def p_BR3(e):
        e.wait_ge(cT.sem, m_r1c)
        e.tensor_tensor(out=BRt[:, :], in0=brstage, in1=ps_mm[:, :], op=ALU.mult)
    V(p_BR3)
    m_BR3 = cV.n
    m_psmm_free_rows = m_BR3

    def p_fb1(e):
        e.wait_ge(cV.sem, m_BR3)
        sCst.inc(e.dma_start(out=r0, in_=dr_vfb))
    Y(p_fb1)
    m_fb1 = sCst.n

    def p_fb2(e):
        e.wait_ge(sCst.sem, m_fb1)
        e.tensor_scalar(out=r1, in0=r0, scalar1=0.5, scalar2=None,
                        op0=ALU.mult, op1=ALU.bypass)
    V(p_fb2)
    m_fb2 = cV.n

    def p_fb3(e):
        e.wait_ge(cV.sem, m_fb2)
        e.activation(out=biasF[0:1, :], in_=r1, func=AF.Identity)
    S(p_fb3)
    m_fb3 = cS.n

    def p_fb4(e):
        e.wait_ge(cS.sem, m_fb3)
        e.tensor_tensor(out=rb16, in0=r1, in1=biasF[0:1, :], op=ALU.subtract)
    V(p_fb4)
    m_fb4 = cV.n

    def p_fb5(e):
        e.wait_ge(cV.sem, m_fb4)
        sG.inc(e.dma_start(out=biasF[1:2, :], in_=rb16))
    G(p_fb5)
    m_fb5 = sG.n

    def p_gb1(e):
        e.wait_ge(sG.sem, m_fb5)
        sCst.inc(e.dma_start(out=r0, in_=dr_vgb))
    Y(p_gb1)
    m_gb1 = sCst.n

    def p_gb2(e):
        e.wait_ge(sCst.sem, m_gb1)
        e.activation(out=biasG[0:1, :], in_=r0, func=AF.Identity)
    S(p_gb2)
    m_gb2 = cS.n

    def p_gb3(e):
        e.wait_ge(cS.sem, m_gb2)
        e.tensor_tensor(out=rb16, in0=r0, in1=biasG[0:1, :], op=ALU.subtract)
    V(p_gb3)
    m_gb3 = cV.n

    def p_gb4(e):
        e.wait_ge(cV.sem, m_gb3)
        sG.inc(e.dma_start(out=biasG[1:2, :], in_=rb16))
    G(p_gb4)
    m_gb4 = sG.n
    m_rows_done_S = m_gb2
    m_rows_done_V = m_gb3

    m_wchunk2 = [0] * NCH
    tern_marks = [0] * NCH
    for k in range(NCH):
        def p_wld2(e, k=k):
            if k >= 2:
                e.wait_ge(cV.sem, tern_marks[k - 2])
            else:
                e.wait_ge(cV.sem, m_absred)
            sW[k % 2].inc(e.dma_start(out=wtmp[:, (k % 2) * D:(k % 2 + 1) * D],
                                      in_=dr_wfT[128 * k:128 * (k + 1), :]))
            m_wchunk2[k] = sW[k % 2].n
        G(p_wld2)

        def p_tern(e, k=k):
            sl = wtmp[:, (k % 2) * D:(k % 2 + 1) * D]
            e.wait_ge(sW[k % 2].sem, m_wchunk2[k])
            e.wait_ge(cS.sem, m_thr_c)
            if k == 0:
                e.wait_ge(cV.sem, m_rows_done_V)
                e.wait_ge(cS.sem, m_rows_done_S)
                e.wait_ge(sG.sem, m_gb4)
                e.wait_ge(sCst.sem, sCst.n)
            e.tensor_scalar(out=ugob[:, 0:D], in0=sl,
                            scalar1=scol[:, 0:1], scalar2=scol[:, 2:3],
                            op0=ALU.is_ge, op1=ALU.mult)
            e.tensor_scalar(out=ugob[:, D:2 * D], in0=sl,
                            scalar1=scol[:, 1:2], scalar2=scol[:, 2:3],
                            op0=ALU.is_le, op1=ALU.mult)
            e.tensor_tensor(out=W16[:, k * D:(k + 1) * D], in0=ugob[:, 0:D],
                            in1=ugob[:, D:2 * D], op=ALU.subtract)
            tern_marks[k] = cV.n
        V(p_tern)

    def p_wspill(e):
        e.wait_ge(cV.sem, tern_marks[NCH - 1])
        sG.inc(e.dma_start(out=dr_wf16, in_=W16[:, :]))
    G(p_wspill)
    m_wspill = sG.n

    for j in range(NT):
        if j + 2 < NT:
            a_pre(j + 2)
        a_mm(j)
        if j + 2 < NT:
            a_pre_t(j + 2)
            mk_h1_next[0] = cS.n
        a_post(j)

    m_gchunk = [0] * NCH
    gcast_marks = [0] * NCH
    for k in range(NCH):
        def i_wld(e, k=k):
            if k >= 2:
                e.wait_ge(cS.sem, gcast_marks[k - 2])
            else:
                e.wait_ge(cV.sem, tern_marks[NCH - 1])
            sW[k % 2].inc(e.dma_start(out=wtmp[:, (k % 2) * D:(k % 2 + 1) * D],
                                      in_=dr_wgT[128 * k:128 * (k + 1), :]))
            m_gchunk[k] = sW[k % 2].n
        G(i_wld)

        def i_gcast(e, k=k):
            e.wait_ge(sW[k % 2].sem, m_gchunk[k])
            e.wait_ge(sG.sem, m_wspill)
            e.wait_ge(cT.sem, mk_mmA[NT - 1])
            e.activation(out=W16[:, k * D:(k + 1) * D],
                         in_=wtmp[:, (k % 2) * D:(k % 2 + 1) * D], func=AF.Identity)
            gcast_marks[k] = cS.n
        S(i_gcast)

    sclose("wtmp")
    gbufB = sopen("gbufB", [128, 4 * 512], F32)
    T16sb = sopen("T16sb", [16, D], F32)
    nhalf = 2 if NO > 1 else 1

    def b_t16z(e):
        e.wait_ge(cS.sem, gcast_marks[NCH - 1])
        e.memset(T16sb[:, :], 0.0)
    V(b_t16z)
    m_t16z = cV.n

    mk_u32re = [0] * NT
    mk_ug = [0] * NT
    mk_ug_spill = [0] * NT
    mk_gmm = [0] * NT
    mk_gmmc = [[0] * NO for _ in range(NT)]
    mk_sig = [0] * NT
    mk_sigc = [[0] * NO for _ in range(NT)]
    mk_uT = [0] * NT
    mk_uTev = [0] * NT
    mk_u16 = [0] * NT
    zh_marks = [0] * NQ
    zl_marks = [0] * NQ
    zmm_marks = [0] * NQ
    tadd_marks = [0] * NQ

    def b_l01(e):
        e.wait_ge(sPx[0].sem, sPx[0].n)
        e.wait_ge(sPx[1].sem, sPx[1].n)
        for jj in range(min(3, NT)):
            sU[jj % 3].inc(e.dma_start(out=ubuf[:, jj * D:(jj + 1) * D],
                                       in_=dr_u32[128 * jj:128 * (jj + 1), :]))
            mk_u32re[jj] = sU[jj % 3].n
    Y(b_l01)

    def b_pre(j):
        s = (j % 2) * D
        su = (j % 3) * D

        def b_u16(e, j=j, s=s, su=su):
            e.wait_ge(sU[j % 3].sem, mk_u32re[j])
            if j <= 1:
                e.wait_ge(cT.sem, mk_hT[NT - 1])
            if j >= 2:
                e.wait_ge(cT.sem, mk_uT[j - 2])
            e.activation(out=h16b[:, s:s + D], in_=ubuf[:, su:su + D], func=AF.Identity)

        if j < 3:
            evac_ct = mk_mmA[NT - 1]
        else:
            evac_ct = mk_gmmc[j - 3][NO - 1]
        mk_uT[j], mk_uTev[j] = emit_transpose_pass(
            j, (j % 2) * D, (j % 3) * D, evac_ct_wait=evac_ct, s_prefix=b_u16, n_prefix=1)
        mk_u16[j] = mk_uTev[j]

    def b_mm(j):
        s = (j % 3) * D
        for o in range(NO):
            def b_mmo(e, j=j, s=s, o=o):
                if o == 0:
                    e.wait_ge(cS.sem, mk_uTev[j])
                    e.wait_ge(cS.sem, gcast_marks[NCH - 1])
                    e.wait_ge(sG.sem, max(m_gb4, m_wspill))
                if j == 0:
                    e.wait_ge(cV.sem, mk_x1c[NT - 1][o])
                else:
                    e.wait_ge(cS.sem, mk_sigc[j - 1][o])
                for i in range(NCH):
                    e.matmul(out=ps_mm[:, o * 512:(o + 1) * 512],
                             lhsT=hTb[:, s + 128 * i:s + 128 * (i + 1)],
                             rhs=W16[:, i * D + o * 512:i * D + o * 512 + 512],
                             start=(i == 0), stop=False)
                cT.inc(e.matmul(out=ps_mm[:, o * 512:(o + 1) * 512], lhsT=ones2_sb[:],
                                rhs=biasG[:, o * 512:(o + 1) * 512], start=False, stop=True))
            T(b_mmo)
            mk_gmmc[j][o] = cT.n
        mk_gmm[j] = mk_gmmc[j][NO - 1]

    def b_sig_f(j):
        s = (j % 2) * D
        cv_ug_base = cV.n

        def b_sig(e, j=j, s=s, cv_ug_base=cv_ug_base):
            if j >= 1:
                e.wait_ge(cV.sem, mk_ug[j - 1])
            if j == 0:
                e.wait_ge(cV.sem, tern_marks[NCH - 1])
            for o in range(NO):
                e.wait_ge(cT.sem, mk_gmmc[j][o])
                e.activation(out=gbufB[:, o * 512:o * 512 + 512],
                             in_=ps_mm[:, o * 512:(o + 1) * 512], func=AF.Sigmoid)
        S(b_sig)
        mk_sig[j] = cS.n
        for o in range(NO):
            mk_sigc[j][o] = mk_sig[j] - (NO - 1 - o)

    def b_post(j):
        s = (j % 2) * D

        def b_ug(e, j=j, s=s):
            if j >= 2:
                e.wait_ge(sPg[j % 2].sem, mk_ug_spill[j - 2])
                e.wait_ge(cT.sem, zmm_marks[(j - 2) * NO + NO - 1])
            su = (j % 3) * D
            for o in range(NO):
                e.wait_ge(cS.sem, mk_sigc[j][o])
                e.tensor_tensor(out=ugob[:, s + o * 512:s + o * 512 + 512],
                                in0=ubuf[:, su + o * 512:su + o * 512 + 512],
                                in1=gbufB[:, o * 512:o * 512 + 512],
                                op=ALU.mult)
        V(b_ug)
        mk_ug[j] = cV.n

        def b_sp(e, j=j, s=s):
            e.wait_ge(cV.sem, mk_ug[j])
            sPg[j % 2].inc(e.dma_start(out=dr_ug[128 * j:128 * (j + 1), :],
                                       in_=ugob[:, s:s + D]))
        G(b_sp)
        mk_ug_spill[j] = sPg[j % 2].n

        if j + 3 < NT:
            def b_upre(e, j=j):
                e.wait_ge(cV.sem, mk_ug[j])
                sU[(j + 3) % 3].inc(e.dma_start(
                    out=ubuf[:, (j % 3) * D:(j % 3) * D + D],
                    in_=dr_u32[128 * (j + 3):128 * (j + 4), :]))
                mk_u32re[j + 3] = sU[(j + 3) % 3].n
            Y(b_upre)



    def b_zchain(j):
        s = (j % 2) * D
        for o in range(NO):
            q = j * NO + o
            zs = (q % 2) * 512

            def b_z(e, j=j, s=s, o=o, q=q, zs=zs):
                e.wait_ge(cV.sem, mk_ug[j])
                if q >= 2:
                    e.wait_ge(cT.sem, zmm_marks[q - 2])
                e.tensor_tensor(out=zbuf32[:, zs:zs + 512],
                                in0=ugob[:, s + o * 512:s + o * 512 + 512],
                                in1=BRt[:, o * 512:o * 512 + 512], op=ALU.mult)
                if j == 0:
                    e.wait_ge(cS.sem, m_br0s)
                    e.tensor_tensor(out=zbuf32[0:1, zs:zs + 512],
                                    in0=ugob[0:1, s + o * 512:s + o * 512 + 512],
                                    in1=row_br0[0:1, o * 512:o * 512 + 512],
                                    op=ALU.mult)
            V(b_z)
            m_z32 = cV.n

            def b_zh(e, q=q, zs=zs, m_z32=m_z32):
                if q >= 2:
                    e.wait_ge(cT.sem, zmm_marks[q - 2])
                e.tensor_copy(out=zbufh[:, zs:zs + 512], in_=zbuf32[:, zs:zs + 512])
                e.tensor_tensor(out=zbufl[:, zs:zs + 512], in0=zbuf32[:, zs:zs + 512],
                                in1=zbufh[:, zs:zs + 512], op=ALU.subtract)
            V(b_zh)
            zh_marks[q] = cV.n
            zl_marks[q] = cV.n

            def b_zmm(e, j=j, s=s, o=o, q=q, zs=zs):
                e.wait_ge(cV.sem, zl_marks[q])
                if q >= 2:
                    e.wait_ge(cV.sem, tadd_marks[q - 2])
                else:
                    e.wait_ge(cS.sem, m_thr_c)
                pa = ps_aux[0:16, zs:zs + 512]
                e.matmul(out=pa, lhsT=embZ_sb[:, 16 * j:16 * j + 16],
                         rhs=zbufh[:, zs:zs + 512], start=True, stop=False)
                e.matmul(out=pa, lhsT=embZ_sb[:, 16 * j:16 * j + 16],
                         rhs=zbufl[:, zs:zs + 512], start=False, stop=False)
                cT.inc(e.matmul(out=pa, lhsT=embU_sb[:, 16 * j:16 * j + 16],
                                rhs=ugob[:, s + o * 512:s + o * 512 + 512],
                                start=False, stop=True))
            T(b_zmm)
            zmm_marks[q] = cT.n

            def b_tadd(e, j=j, o=o, q=q, zs=zs):
                e.wait_ge(cT.sem, zmm_marks[q])
                if q == 0:
                    e.wait_ge(cV.sem, m_t16z)
                e.tensor_tensor(out=T16sb[:, o * 512:(o + 1) * 512],
                                in0=T16sb[:, o * 512:(o + 1) * 512],
                                in1=ps_aux[0:16, zs:zs + 512], op=ALU.add)
            V(b_tadd)
            tadd_marks[q] = cV.n
    b_pre(0)
    if NT > 1:
        b_pre(1)
    for j in range(NT):
        b_mm(j)
        b_sig_f(j)
        if j >= 1:
            b_zchain(j - 1)
        if j + 2 < NT:
            b_pre(j + 2)
        b_post(j)
    b_zchain(NT - 1)

    m16mm_marks = [0] * NO
    m16exp_marks = [0] * NO
    m16tt_marks = [0] * NO
    cv_base_m16 = [0]

    def c_m16a(e):
        e.wait_ge(cV.sem, tadd_marks[NQ - 1])
        cv_base = cV.n
        cv_base_m16[0] = cv_base
        for o in range(NO):
            if o >= 2:
                e.wait_ge(cV.sem, cv_base + (o - 2) + 1)
            cT.inc(e.matmul(out=ps_aux[0:16, (o % 2) * 512:(o % 2) * 512 + 512],
                            lhsT=iotaM16_sb[:],
                            rhs=row_lnaa[0:1, o * 512:(o + 1) * 512],
                            start=True, stop=True))
            m16mm_marks[o] = cT.n
    T(c_m16a)

    def c_m16b(e):
        for o in range(NO):
            e.wait_ge(cT.sem, m16mm_marks[o])
            if o >= 2:
                e.wait_ge(cV.sem, cv_base_m16[0] + (o - 2) + 1)
            e.activation(out=ps_aux[0:16, (o % 2) * 512:(o % 2) * 512 + 512],
                         in_=ps_aux[0:16, (o % 2) * 512:(o % 2) * 512 + 512],
                         func=AF.Exp)
            m16exp_marks[o] = cS.n
    S(c_m16b)

    def c_f16(e):
        for o in range(NO):
            e.wait_ge(cS.sem, m16exp_marks[o])
            e.tensor_tensor(out=T16sb[:, o * 512:(o + 1) * 512],
                            in0=T16sb[:, o * 512:(o + 1) * 512],
                            in1=ps_aux[0:16, (o % 2) * 512:(o % 2) * 512 + 512],
                            op=ALU.mult)
            m16tt_marks[o] = cV.n
    V(c_f16)
    m_f16 = cV.n

    def c_din(e):
        e.wait_ge(cV.sem, m_f16)
        sG.inc(e.dma_start(out=dr_cc_in, in_=T16sb[:, :]))
    G(c_din)
    m_ccin = sG.n

    def c_cc(e):
        e.wait_ge(sG.sem, m_ccin)
        ccC.inc(e.collective_compute(
            "AllGather", ALU.bypass,
            replica_groups=[list(range(n_cores))],
            ins=[dr_cc_in.opt()],
            outs=[dr_cc_out.opt()],
        ))
    G(c_cc)
    m_cc = ccC.n

    sclose("T16sb")
    sclose("gbufB")
    sclose("ugob")
    bscr = sopen("bscr", [NT, 2 * D], BF16)
    deff_b = sopen("deff_b", [128, D], BF16)
    s1 = sopen("s1", [128, D], F32)
    s2 = sopen("s2", [128, D], F32)

    a64_marks = []
    for half in range(nhalf):
        os_ = tuple(range(half * (NO // nhalf), (half + 1) * (NO // nhalf)))
        def c_a64a(e, os_=os_):
            e.wait_ge(cV.sem, tadd_marks[NQ - 1])
            e.wait_ge(cV.sem, m_f16)
            if a64_marks:
                e.wait_ge(cS.sem, a64_marks[-1])
            for oi, o in enumerate(os_):
                cT.inc(e.matmul(out=ps_aux[0:NR, oi * 512:oi * 512 + 512],
                                lhsT=iota64_sb[:],
                                rhs=row_lnaa[0:1, o * 512:(o + 1) * 512],
                                start=True, stop=True))
        T(c_a64a)
        m_a64a = cT.n

        def c_a64b(e, os_=os_, m_a64a=m_a64a):
            e.wait_ge(cT.sem, m_a64a)
            e.wait_ge(sG.sem, m_ccin)
            e.wait_ge(cV.sem, m_f16)
            for oi, o in enumerate(os_):
                e.activation(out=s2[0:NR, o * 512:(o + 1) * 512],
                             in_=ps_aux[0:NR, oi * 512:oi * 512 + 512], func=AF.Exp)
        S(c_a64b)
        a64_marks.append(cS.n)
    m_a64 = a64_marks[-1]

    def c_dout(e):
        e.wait_ge(ccC.sem, m_cc)
        e.wait_ge(sPg[0].sem, sPg[0].n)
        e.wait_ge(sPg[1].sem, sPg[1].n)
        for r in range(n_cores):
            sG.inc(e.dma_start(out=s1[r * NT:(r + 1) * NT, 0:D],
                               in_=dr_cc_out[16 * r:16 * r + NT, :]))
            sG.inc(e.dma_start(out=s1[64 + r * NT:64 + (r + 1) * NT, 0:D],
                               in_=dr_cc_out[16 * r + 8:16 * r + 8 + NT, :]))
    G(c_dout)
    m_ccout = sG.n

    def c_fown(e):
        e.wait_ge(cT.sem, m_dlr7_h[0])
        sG.inc(e.dma_start(out=s1[0:NT, 0:D], in_=dr_cc_in[0:NT, :]))


    def c_w64(e):
        e.wait_ge(sG.sem, m_ccout)
        e.wait_ge(cS.sem, m_a64)
        e.tensor_tensor(out=s1[0:NR, :], in0=s1[0:NR, :], in1=s2[0:NR, :], op=ALU.mult)
    V(c_w64)
    m_w64 = cV.n
    m_dlr7_h = [0]

    def c_cpre(e):
        e.wait_ge(cV.sem, m_w64)
        e.wait_ge(cS.sem, mk_sig[NT - 1])
        for o in range(NO):
            cT.inc(e.matmul(out=ps_mm[0:1, o * 512:(o + 1) * 512], lhsT=sel64_sb[:],
                            rhs=s1[0:NR, o * 512:(o + 1) * 512], start=True, stop=True))
    T(c_cpre)
    m_cpre = cT.n

    def c_apow(e):
        e.wait_ge(cT.sem, m_cpre)
        e.wait_ge(cV.sem, m_w64)
        e.activation(out=s2[0:1, 0:D], in_=row_lnaa[:], func=AF.Exp,
                     scale=pcs_sb[0:1, 0:1])
    S(c_apow)
    m_apow = cS.n

    def c_crow(e):
        e.wait_ge(cS.sem, m_apow)
        e.wait_ge(cV.sem, mk_x1_tt[NT - 1])
        e.wait_ge(sPx[0].sem, sPx[0].n)
        e.wait_ge(sPx[1].sem, sPx[1].n)
        e.tensor_tensor(out=xbuf[0:1, 0:D], in0=ps_mm[0:1, 0:D], in1=s2[0:1, 0:D],
                        op=ALU.mult)
    V(c_crow)
    m_crow = cV.n

    def c_ummm(e):
        e.wait_ge(cV.sem, m_crow)
        for o in range(NO):
            cT.inc(e.matmul(out=ps_mm[0:1, o * 512:(o + 1) * 512],
                            lhsT=o128_sb[64:64 + NR, :],
                            rhs=s1[64:64 + NR, o * 512:(o + 1) * 512],
                            start=True, stop=True))
    T(c_ummm)
    m_ummm = cT.n

    def c_um(e):
        e.wait_ge(cT.sem, m_ummm)
        e.wait_ge(cV.sem, m_crow)
        e.activation(out=s2[0:1, 0:D], in_=ps_mm[0:1, 0:D], func=AF.Identity,
                     scale=1.0 / (n_cores * T_loc))
    S(c_um)
    m_um = cS.n

    def c_dlr0(e):
        e.wait_ge(cS.sem, m_um)
        e.wait_ge(cT.sem, m_cpre)
        e.wait_ge(cV.sem, m_crow)
        sY.inc(e.dma_start(out=s1[0:4, 0:D], in_=dr_vdlru))
    Y(c_dlr0)
    m_dlru = sY.n

    def c_dlr1(e):
        e.wait_ge(cS.sem, m_um)
        for o in range(NO):
            cT.inc(e.matmul(out=ps_mm[0:4, o * 512:(o + 1) * 512], lhsT=oc_sb[0:1, 0:4],
                            rhs=s2[0:1, o * 512:(o + 1) * 512], start=True, stop=True))
    T(c_dlr1)
    m_dlr1 = cT.n

    def c_dlr2(e):
        e.wait_ge(cT.sem, m_dlr1)
        e.wait_ge(sY.sem, m_dlru)
        e.tensor_tensor(out=s1[0:4, 0:D], in0=ps_mm[0:4, 0:D], in1=s1[0:4, 0:D],
                        op=ALU.mult)
        e.tensor_reduce(out=row_m4[:], in_=s1[0:4, 0:D], axis=AX.X, op=ALU.add)
    V(c_dlr2)
    m_dlr2 = cV.n

    def c_dlr5(e):
        e.wait_ge(cV.sem, m_dlr2)
        sY.inc(e.dma_start(out=s1[0:4, 0:D], in_=dr_vdlrv))
        sY.inc(e.dma_start(out=s2[0:1, 0:D], in_=dr_vd))
    Y(c_dlr5)
    m_dlrv2 = sY.n

    def c_dlr6(e):
        e.wait_ge(sY.sem, m_dlrv2)
        e.tensor_scalar(out=s1[0:4, 0:D], in0=s1[0:4, 0:D], scalar1=row_m4[:],
                        scalar2=0.25, op0=ALU.mult, op1=ALU.mult)
    V(c_dlr6)
    m_dlr6 = cV.n

    def c_dlr7(e):
        e.wait_ge(cV.sem, m_dlr6)
        for o in range(NO):
            cT.inc(e.matmul(out=ps_mm[0:1, o * 512:(o + 1) * 512], lhsT=o128_sb[0:4, :],
                            rhs=s1[0:4, o * 512:(o + 1) * 512], start=True, stop=True))
    T(c_dlr7)
    m_dlr7 = cT.n
    m_dlr7_h[0] = m_dlr7
    G(c_fown)
    m_fown = sG.n

    def c_deffrow(e):
        e.wait_ge(cT.sem, m_dlr7)
        e.wait_ge(sY.sem, m_dlrv2)
        e.tensor_tensor(out=s2[0:1, 0:D], in0=ps_mm[0:1, 0:D], in1=s2[0:1, 0:D],
                        op=ALU.add)
    V(c_deffrow)
    m_deffrow = cV.n

    def c_deffb(e):
        e.wait_ge(cV.sem, m_deffrow)
        for o in range(NO):
            cT.inc(e.matmul(out=ps_mm[:, o * 512:(o + 1) * 512], lhsT=oc_sb[:],
                            rhs=s2[0:1, o * 512:(o + 1) * 512], start=True, stop=True))
    T(c_deffb)
    m_deffb = cT.n

    def c_deffev(e):
        e.wait_ge(cT.sem, m_deffb)
        e.activation(out=deff_b[:, :], in_=ps_mm[:, :], func=AF.Identity)
    S(c_deffev)
    m_deffev = cS.n

    a9_marks = []
    for half in range(nhalf):
        os_ = tuple(range(half * (NO // nhalf), (half + 1) * (NO // nhalf)))
        def c_a9a(e, os_=os_):
            e.wait_ge(cV.sem, tadd_marks[NQ - 1])
            if a9_marks:
                e.wait_ge(cS.sem, a9_marks[-1])
            else:
                e.wait_ge(cS.sem, m_deffev)
            for oi, o in enumerate(os_):
                cT.inc(e.matmul(out=ps_aux[0:NT + 1, oi * 512:oi * 512 + 512],
                                lhsT=iota9_sb[:],
                                rhs=row_lnaa[0:1, o * 512:(o + 1) * 512],
                                start=True, stop=True))
        T(c_a9a)
        m_a9a = cT.n

        def c_a9b(e, os_=os_, m_a9a=m_a9a):
            e.wait_ge(cT.sem, m_a9a)
            e.wait_ge(cV.sem, m_deffrow)
            for oi, o in enumerate(os_):
                e.activation(out=s2[0:NT + 1, o * 512:(o + 1) * 512],
                             in_=ps_aux[0:NT + 1, oi * 512:oi * 512 + 512], func=AF.Exp)
        S(c_a9b)
        a9_marks.append(cS.n)
    m_a9 = a9_marks[-1]

    def c_q9(e):
        e.wait_ge(cS.sem, m_deffev)
        e.wait_ge(cV.sem, m_crow)
        e.wait_ge(sG.sem, m_fown)
        for o in range(NO):
            e.matmul(out=ps_mm[0:NT + 1, o * 512:(o + 1) * 512], lhsT=tri9_sb[:],
                     rhs=s1[0:NT, o * 512:(o + 1) * 512], start=True, stop=False)
            cT.inc(e.matmul(out=ps_mm[0:NT + 1, o * 512:(o + 1) * 512],
                            lhsT=oc_sb[0:1, 0:NT + 1],
                            rhs=xbuf[0:1, o * 512:(o + 1) * 512], start=False, stop=True))
    T(c_q9)
    m_q9 = cT.n

    def c_cfull(e):
        e.wait_ge(cT.sem, m_q9)
        e.wait_ge(cS.sem, m_a9)
        e.tensor_tensor(out=s1[0:NT, :], in0=ps_mm[0:NT, 0:D], in1=s2[0:NT, 0:D],
                        op=ALU.mult)
    V(c_cfull)
    m_cfull = cV.n

    def c_chl(e):
        e.wait_ge(cV.sem, m_cfull)
        e.activation(out=bscr[:, 0:D], in_=s1[0:NT, :], func=AF.Identity)
    S(c_chl)
    m_chl = cS.n

    def c_cll(e):
        e.wait_ge(cS.sem, m_chl)
        e.tensor_tensor(out=bscr[:, D:2 * D], in0=s1[0:NT, :], in1=bscr[:, 0:D],
                        op=ALU.subtract)
    V(c_cll)
    m_cll = cV.n

    def y_wf(e):
        e.wait_ge(cT.sem, mk_gmm[NT - 1])
        e.wait_ge(sG.sem, m_wspill)
        sY2.inc(e.dma_start(out=W16[:, :], in_=dr_wf16))
    Y(y_wf)
    m_wfre = sY2.n

    mk_x1re = [0] * NT
    mk_ugre = [0] * NT
    mk_carry = [0] * NT

    csb = None
    carryP = None

    def alloc_phase_c():
        nonlocal csb, carryP
        sclose("s2")
        sclose("s1")
        csb = sopen("csb", [128, 2 * D], F32)
        carryP = [sopen(f"carryP{p}", [2, D], BF16) for p in range(2)]
    alloc_phase_c()

    def y_carry01(e):
        e.wait_ge(cV.sem, m_cll)
        sCY[0].inc(e.dma_start(out=carryP[0][0:1, :], in_=bscr[0:1, 0:D]))
        sCY[0].inc(e.dma_start(out=carryP[0][1:2, :], in_=bscr[0:1, D:2 * D]))
        mk_carry[0] = sCY[0].n
        if NT > 1:
            sCY[1].inc(e.dma_start(out=carryP[1][0:1, :], in_=bscr[1:2, 0:D]))
            sCY[1].inc(e.dma_start(out=carryP[1][1:2, :], in_=bscr[1:2, D:2 * D]))
            mk_carry[1] = sCY[1].n
    Y(y_carry01)

    def y_l01(e):
        e.wait_ge(cV.sem, mk_ug[NT - 1])
        e.wait_ge(sPg[0].sem, sPg[0].n)
        e.wait_ge(sPg[1].sem, sPg[1].n)
        e.wait_ge(cT.sem, m_q9)
        sX[0].inc(e.dma_start(out=xbuf[:, 0:D], in_=dr_x1[0:128, :]))
        mk_x1re[0] = sX[0].n
        sU[0].inc(e.dma_start(out=ubuf[:, 0:D], in_=dr_ug[0:128, :]))
        mk_ugre[0] = sU[0].n
        sX[1].inc(e.dma_start(out=xbuf[:, D:2 * D], in_=dr_x1[128:256, :]))
        mk_x1re[1] = sX[1].n
        sU[1].inc(e.dma_start(out=ubuf[:, D:2 * D], in_=dr_ug[128:256, :]))
        mk_ugre[1] = sU[1].n
    Y(y_l01)

    mk_cs = [0] * NT
    mk_x2 = [0] * NT
    mk_rb3 = [0] * NT
    mk_h3act = [0] * NT
    mk_h3T = [0] * NT
    mk_h3Tev = [0] * NT
    mk_mmC = [0] * NT
    mk_mmCc = [[0] * NO for _ in range(NT)]
    mk_out = [0] * NT
    mk_outc = [[0] * NO for _ in range(NT)]
    mk_outdma = [0] * NT
    yz_marks = [0] * NQ
    ysh_marks = [0] * NQ
    ysl_marks = [0] * NQ
    ysmm_marks = [0] * NQ
    ycs_marks = [0] * NQ

    def y_pre_chunk(j, o):
        s = (j % 2) * D
        if True:
            q = j * NO + o
            zs = (q % 2) * 512

            def y_z(e, j=j, s=s, o=o, q=q, zs=zs):
                e.wait_ge(sU[j % 3].sem, mk_ugre[j])
                if q >= 2:
                    e.wait_ge(cG.sem, ysh_marks[q - 2])
                    e.wait_ge(cG.sem, ysl_marks[q - 2])
                else:
                    e.wait_ge(cV.sem, zl_marks[NQ - 1])
                e.tensor_tensor(out=zbuf32[:, zs:zs + 512],
                                in0=ubuf[:, s + o * 512:s + o * 512 + 512],
                                in1=BRt[:, o * 512:o * 512 + 512], op=ALU.mult)
                if j == 0:
                    e.tensor_tensor(out=zbuf32[0:1, zs:zs + 512],
                                    in0=ubuf[0:1, s + o * 512:s + o * 512 + 512],
                                    in1=row_br0[0:1, o * 512:o * 512 + 512],
                                    op=ALU.mult)
            V(y_z)
            m_z32 = cV.n
            yz_marks[q] = m_z32

            def y_zh(e, q=q, zs=zs, m_z32=m_z32):
                e.wait_ge(cV.sem, m_z32)
                if q >= 2:
                    e.wait_ge(cT.sem, ysmm_marks[q - 2])
                else:
                    e.wait_ge(cT.sem, zmm_marks[NQ - 1])
                e.tensor_copy(out=zbufh[:, zs:zs + 512], in_=zbuf32[:, zs:zs + 512])
            GC(y_zh)
            ysh_marks[q] = cG.n

            def y_zl(e, q=q, zs=zs):
                e.tensor_tensor(out=zbufl[:, zs:zs + 512], in0=zbuf32[:, zs:zs + 512],
                                in1=zbufh[:, zs:zs + 512], op=ALU.subtract)
            GC(y_zl)
            ysl_marks[q] = cG.n

            def y_smm(e, j=j, o=o, q=q, zs=zs):
                e.wait_ge(cG.sem, ysl_marks[q])
                e.wait_ge(sCY[j % 2].sem, mk_carry[j])
                if q >= 2:
                    e.wait_ge(cV.sem, ycs_marks[q - 2])
                else:
                    e.wait_ge(cS.sem, m_a9)
                pa = ps_aux[:, zs:zs + 512]
                e.matmul(out=pa, lhsT=tri_sb[:], rhs=zbufh[:, zs:zs + 512],
                         start=True, stop=False)
                e.matmul(out=pa, lhsT=tri_sb[:], rhs=zbufl[:, zs:zs + 512],
                         start=False, stop=False)
                cT.inc(e.matmul(out=pa, lhsT=ones2_sb[:],
                                rhs=carryP[j % 2][:, o * 512:(o + 1) * 512],
                                start=False, stop=True))
            T(y_smm)
            ysmm_marks[q] = cT.n

            def y_cs(e, j=j, s=s, o=o, q=q, zs=zs):
                e.wait_ge(cT.sem, ysmm_marks[q])
                if j >= 2 and o == 0:
                    e.wait_ge(cS.sem, mk_h3act[j - 2])
                e.tensor_tensor(out=csb[:, s + o * 512:s + o * 512 + 512],
                                in0=Pt[:, o * 512:(o + 1) * 512],
                                in1=ps_aux[:, zs:zs + 512], op=ALU.mult)
            V(y_cs)
            ycs_marks[q] = cV.n

    def y_pre_tail(j):
        s = (j % 2) * D
        mk_cs[j] = ycs_marks[j * NO + NO - 1]

        if j + 2 < NT:
            def y_cpre(e, j=j):
                e.wait_ge(cT.sem, ysmm_marks[j * NO + NO - 1])
                sCY[j % 2].inc(e.dma_start(out=carryP[j % 2][0:1, :],
                                           in_=bscr[j + 2:j + 3, 0:D]))
                sCY[j % 2].inc(e.dma_start(out=carryP[j % 2][1:2, :],
                                           in_=bscr[j + 2:j + 3, D:2 * D]))
                mk_carry[j + 2] = sCY[j % 2].n
            Y(y_cpre)

        def y_dg(e, j=j, s=s):
            e.wait_ge(cV.sem, yz_marks[j * NO + NO - 1])
            e.wait_ge(cS.sem, m_deffev)
            e.tensor_tensor(out=ubuf[:, s:s + D], in0=ubuf[:, s:s + D],
                            in1=deff_b[:, :], op=ALU.mult)
        GC(y_dg)
        mk_dg = cG.n

        def y_x2(e, j=j, s=s):
            e.wait_ge(cV.sem, mk_cs[j])
            e.wait_ge(sX[j % 2].sem, mk_x1re[j])
            e.tensor_tensor(out=csb[:, s:s + D], in0=csb[:, s:s + D],
                            in1=xbuf[:, s:s + D], op=ALU.add)
            e.wait_ge(cG.sem, mk_dg)
            e.tensor_tensor(out=csb[:, s:s + D], in0=csb[:, s:s + D],
                            in1=ubuf[:, s:s + D], op=ALU.add)
        V(y_x2)
        mk_x2[j] = cV.n

        def y_stats(e, j=j, s=s):
            if j >= 2:
                e.wait_ge(cG.sem, mk_rb3[j - 2])
            else:
                e.wait_ge(cG.sem, mk_rb2[NT - 1])
            so = (j % 2) * 6 * NO
            for c in range(NO):
                e.bn_stats(out=statb[:, so + 6 * c:so + 6 * c + 6],
                           in_=csb[:, s + 512 * c:s + 512 * (c + 1)])
            e.bn_aggr(out=mvb[:, (j % 2) * 2:(j % 2) * 2 + 2],
                      in_=statb[:, so:so + 6 * NO])
        V(y_stats)
        m_st3 = cV.n

        def y_rb(e, j=j):
            e.wait_ge(cV.sem, m_st3)
            if j >= 2:
                e.wait_ge(cS.sem, mk_h3act[j - 2])
            else:
                e.wait_ge(cV.sem, mk_u32[NT - 1])
            c2 = (j % 2)
            e.tensor_scalar(out=rcol[:, c2:c2 + 1], in0=mvb[:, 2 * c2 + 1:2 * c2 + 2],
                            scalar1=1e-5, scalar2=None, op0=ALU.add, op1=ALU.bypass)
            e.tensor_tensor(out=rcol[:, c2:c2 + 1], in0=rcol[:, c2:c2 + 1],
                            in1=mhalf[:, :], op=ALU.pow)
        GC(y_rb)
        mk_rb3[j] = cG.n

        def y_b(e, j=j):
            e.wait_ge(cG.sem, mk_rb3[j])
            c2 = (j % 2)
            e.tensor_scalar(out=bcol[:, c2:c2 + 1], in0=mvb[:, 2 * c2:2 * c2 + 1],
                            scalar1=rcol[:, c2:c2 + 1], scalar2=-1.0,
                            op0=ALU.mult, op1=ALU.mult)
        V(y_b)
        m_b3v = cV.n

        def y_h3(e, j=j, s=s):
            e.wait_ge(cV.sem, m_b3v)
            if j == 0:
                e.wait_ge(cT.sem, mk_uT[NT - 1])
            if j >= 2:
                e.wait_ge(cT.sem, mk_h3T[j - 2])
            c2 = (j % 2)
            e.activation(out=h16b[:, s:s + D], in_=csb[:, s:s + D],
                         func=AF.Identity, bias=bcol[:, c2:c2 + 1],
                         scale=rcol[:, c2:c2 + 1])

        evac_ct = (mk_gmm[NT - 1] if j < 3 else mk_mmCc[j - 3][NO - 1])
        mk_h3T[j], mk_h3Tev[j] = emit_transpose_pass(
            j, (j % 2) * D, (j % 3) * D, evac_ct_wait=evac_ct, s_prefix=y_h3, n_prefix=1)
        mk_h3act[j] = mk_h3Tev[j]

    def y_mmo_f(j, o):
        s = (j % 3) * D
        if True:
            def y_mmo(e, j=j, s=s, o=o):
                if o == 0:
                    e.wait_ge(cS.sem, mk_h3Tev[j])
                    e.wait_ge(sY2.sem, m_wfre)
                if j == 0:
                    e.wait_ge(cV.sem, m_cfull)
                    e.wait_ge(cS.sem, m_deffev)
                else:
                    e.wait_ge(cV.sem, mk_outc[j - 1][o])
                for i in range(NCH):
                    e.matmul(out=ps_mm[:, o * 512:(o + 1) * 512],
                             lhsT=hTb[:, s + 128 * i:s + 128 * (i + 1)],
                             rhs=W16[:, i * D + o * 512:i * D + o * 512 + 512],
                             start=(i == 0), stop=False)
                cT.inc(e.matmul(out=ps_mm[:, o * 512:(o + 1) * 512], lhsT=ones2_sb[:],
                                rhs=biasF[:, o * 512:(o + 1) * 512], start=False, stop=True))
            T(y_mmo)
            mk_mmCc[j][o] = cT.n

    def y_outo_f(j, o):
        s = (j % 2) * D

        def y_outo(e, j=j, s=s, o=o):
            e.wait_ge(cT.sem, mk_mmCc[j][o])
            if j >= 2 and o == 0:
                e.wait_ge(sST[j % 2].sem, mk_outdma[j - 2])
            e.tensor_tensor(out=xbuf[:, s + o * 512:s + o * 512 + 512],
                            in0=csb[:, s + o * 512:s + o * 512 + 512],
                            in1=ps_mm[:, o * 512:(o + 1) * 512], op=ALU.add)
        V(y_outo)
        mk_outc[j][o] = cV.n

    def y_post(j):
        s = (j % 2) * D
        mk_out[j] = mk_outc[j][NO - 1]

        def y_store(e, j=j, s=s):
            e.wait_ge(cV.sem, mk_out[j])
            sST[j % 2].inc(e.dma_start(out=dr_y[128 * j:128 * (j + 1), :],
                                       in_=xbuf[:, s:s + D]))
        Y(y_store)
        mk_outdma[j] = sST[j % 2].n

        if j + 2 < NT:
            def y_ldpre(e, j=j, s=s):
                e.wait_ge(sST[j % 2].sem, mk_outdma[j])
                sX[j % 2].inc(e.dma_start(out=xbuf[:, s:s + D],
                                          in_=dr_x1[128 * (j + 2):128 * (j + 3), :]))
                mk_x1re[j + 2] = sX[j % 2].n
                e.wait_ge(cV.sem, mk_x2[j])
                sU[(j + 2) % 3].inc(e.dma_start(out=ubuf[:, s:s + D],
                                                in_=dr_ug[128 * (j + 2):128 * (j + 3), :]))
                mk_ugre[j + 2] = sU[(j + 2) % 3].n
            Y(y_ldpre)

    for o in range(NO):
        y_pre_chunk(0, o)
    y_pre_tail(0)
    for j in range(NT):
        for o in range(NO):
            y_mmo_f(j, o)
            y_outo_f(j, o)
            if j + 1 < NT:
                y_pre_chunk(j + 1, o)
        mk_mmC[j] = mk_mmCc[j][NO - 1]
        if j + 1 < NT:
            y_pre_tail(j + 1)
        y_post(j)

    def fin(e):
        e.wait_ge(sST[0].sem, sST[0].n)
        e.wait_ge(sST[1].sem, sST[1].n)
    G(fin)

    for nm in reversed(list(_open)):
        sclose(nm)
    es.close()
    return nc


def _consts(n_cores, T_loc, D, core_i):
    NT = T_loc // 128
    NR = n_cores * NT
    f32 = np.float32
    c = {}
    c["tri16"] = np.triu(np.ones((128, 128), dtype=f32)).astype(BF)
    c["id16"] = np.eye(128, dtype=f32).astype(BF)
    c["ones2_16"] = np.ones((2, 128), dtype=f32).astype(BF)
    embZ = np.zeros((128, 16 * NT), dtype=f32)
    embU = np.zeros((128, 16 * NT), dtype=f32)
    for j in range(NT):
        embZ[:, 16 * j + j] = 1.0
        embU[:, 16 * j + 8 + j] = 1.0
    c["embZ16"] = embZ.astype(BF)
    c["embU32"] = embU
    t9 = np.zeros((NT, NT + 1), dtype=f32)
    for k in range(NT):
        for jj in range(NT + 1):
            if k < jj:
                t9[k, jj] = 1.0
    c["tri9"] = t9
    c["onescol"] = np.ones((1, 128), dtype=f32)
    c["ones128"] = np.ones((128, 1), dtype=f32)
    c["iotaP"] = np.arange(1, 129, dtype=f32).reshape(1, 128)
    im16 = np.zeros((1, 16), dtype=f32)
    im16[0, :8] = -128.0 * np.arange(8)
    c["iotaM16"] = im16
    c["iota9"] = (128.0 * np.arange(NT + 1, dtype=f32)).reshape(1, NT + 1)
    c["iota64"] = np.array([-float(T_loc) * (r // NT) for r in range(NR)],
                           dtype=f32).reshape(1, NR)
    sel = np.zeros((NR, 1), dtype=f32)
    for r in range(NR):
        if (r // NT) < core_i:
            sel[r, 0] = 1.0
    c["sel64"] = sel
    pcs = np.zeros((1, 4), dtype=f32)
    pcs[0, 0] = float(T_loc) * core_i
    pcs[0, 1] = 1e-6 if core_i == 0 else -1e30
    c["pcs"] = pcs
    return c


def _run(x_full, wfT, wgT, vd, n_cores, T_loc, D):
    nc = build_core_program(n_cores, T_loc, D)
    in_maps = []
    for i in range(n_cores):
        m = dict(_consts(n_cores, T_loc, D, i))
        m["x"] = np.ascontiguousarray(x_full[i * T_loc:(i + 1) * T_loc])
        m["wfT"] = wfT
        m["wgT"] = wgT
        m.update(vd)
        in_maps.append(m)
    res = run_bass_kernel_spmd(nc, in_maps, core_ids=list(range(n_cores)))
    return np.concatenate([np.asarray(r["y"]) for r in res.results], axis=0)


def kernel(L0_attn_u, L0_attn_a, L0_attn_b, L0_attn_c, L0_attn_d,
           L0_attn_gate_w, L0_attn_gate_b, L0_attn_d_lr_u, L0_attn_d_lr_v,
           L0_ffn_weight, L0_ffn_bias):
    n_cores = 8
    B, Tn, D = L0_attn_u.shape
    T_loc = Tn // n_cores
    x = np.ascontiguousarray(np.asarray(L0_attn_u, dtype=np.float32).reshape(Tn, D))
    wfT = np.ascontiguousarray(np.asarray(L0_ffn_weight, dtype=np.float32).T)
    wgT = np.ascontiguousarray(np.asarray(L0_attn_gate_w, dtype=np.float32).T)
    vd = {
        "va": np.asarray(L0_attn_a, dtype=np.float32).reshape(1, D),
        "vb": np.asarray(L0_attn_b, dtype=np.float32).reshape(1, D),
        "vc": np.asarray(L0_attn_c, dtype=np.float32).reshape(1, D),
        "vd": np.asarray(L0_attn_d, dtype=np.float32).reshape(1, D),
        "vgb": np.asarray(L0_attn_gate_b, dtype=np.float32).reshape(1, D),
        "vfb": np.asarray(L0_ffn_bias, dtype=np.float32).reshape(1, D),
        "vdlrv": np.ascontiguousarray(np.asarray(L0_attn_d_lr_v, dtype=np.float32)),
        "vdlru": np.ascontiguousarray(np.asarray(L0_attn_d_lr_u, dtype=np.float32).T),
    }
    out = _run(x, wfT, wgT, vd, n_cores, T_loc, D)
    return out.reshape(B, Tn, D).astype(np.float32)


# version 10
# speedup vs baseline: 1.0478x; 1.0478x over previous
import contextlib
import numpy as np
import ml_dtypes

import concourse.bass as bass
import concourse.mybir as mybir
from concourse.bass_utils import run_bass_kernel_spmd

F32 = mybir.dt.float32
BF16 = mybir.dt.bfloat16
AF = mybir.ActivationFunctionType
ALU = mybir.AluOpType
AX = mybir.AxisListType
BF = ml_dtypes.bfloat16


class Ctr:
    def __init__(self, sem, step=1):
        self.sem = sem
        self.n = 0
        self.step = step

    def inc(self, instr, k=None):
        k = self.step if k is None else k
        instr.then_inc(self.sem, k)
        self.n += k
        return self.n


def build_core_program(n_cores, T_loc, D):
    NT = T_loc // 128
    NCH = D // 128
    NO = D // 512
    NQ = NT * NO
    NR = n_cores * NT

    nc = bass.Bass("TRN2", target_bir_lowering=False)

    dr_x = nc.dram_tensor("x", [T_loc, D], F32, kind="ExternalInput").ap()
    dr_wfT = nc.dram_tensor("wfT", [D, D], F32, kind="ExternalInput").ap()
    dr_wgT = nc.dram_tensor("wgT", [D, D], F32, kind="ExternalInput").ap()
    dr_va = nc.dram_tensor("va", [1, D], F32, kind="ExternalInput").ap()
    dr_vb = nc.dram_tensor("vb", [1, D], F32, kind="ExternalInput").ap()
    dr_vc = nc.dram_tensor("vc", [1, D], F32, kind="ExternalInput").ap()
    dr_vd = nc.dram_tensor("vd", [1, D], F32, kind="ExternalInput").ap()
    dr_vgb = nc.dram_tensor("vgb", [1, D], F32, kind="ExternalInput").ap()
    dr_vfb = nc.dram_tensor("vfb", [1, D], F32, kind="ExternalInput").ap()
    dr_vdlrv = nc.dram_tensor("vdlrv", [4, D], F32, kind="ExternalInput").ap()
    dr_vdlru = nc.dram_tensor("vdlru", [4, D], F32, kind="ExternalInput").ap()
    dr_tri = nc.dram_tensor("tri16", [128, 128], BF16, kind="ExternalInput").ap()
    dr_id = nc.dram_tensor("id16", [128, 128], BF16, kind="ExternalInput").ap()
    dr_ones2 = nc.dram_tensor("ones2_16", [2, 128], BF16, kind="ExternalInput").ap()
    dr_embZ = nc.dram_tensor("embZ16", [128, 16 * NT], BF16, kind="ExternalInput").ap()
    dr_embU = nc.dram_tensor("embU32", [128, 16 * NT], F32, kind="ExternalInput").ap()
    dr_tri9 = nc.dram_tensor("tri9", [NT, NT + 1], F32, kind="ExternalInput").ap()
    dr_onescol = nc.dram_tensor("onescol", [1, 128], F32, kind="ExternalInput").ap()
    dr_ones128 = nc.dram_tensor("ones128", [128, 1], F32, kind="ExternalInput").ap()
    dr_iotaP = nc.dram_tensor("iotaP", [1, 128], F32, kind="ExternalInput").ap()
    dr_iotaM16 = nc.dram_tensor("iotaM16", [1, 16], F32, kind="ExternalInput").ap()
    dr_iota9 = nc.dram_tensor("iota9", [1, NT + 1], F32, kind="ExternalInput").ap()
    dr_iota64 = nc.dram_tensor("iota64", [1, NR], F32, kind="ExternalInput").ap()
    dr_sel64 = nc.dram_tensor("sel64", [NR, 1], F32, kind="ExternalInput").ap()
    dr_pcs = nc.dram_tensor("pcs", [1, 4], F32, kind="ExternalInput").ap()

    dr_x1 = nc.dram_tensor("x1_sp", [T_loc, D], F32).ap()
    dr_u32 = nc.dram_tensor("u32_sp", [T_loc, D], F32).ap()
    dr_ug = nc.dram_tensor("ug_sp", [T_loc, D], F32).ap()
    dr_wf16 = nc.dram_tensor("wf16_sp", [128, NCH * D], BF16).ap()
    dr_cc_in = nc.dram_tensor("cc_in", [16, D], F32).ap()
    dr_cc_out = nc.dram_tensor("cc_out", [n_cores * 16, D], F32).ap()
    dr_y = nc.dram_tensor("y", [T_loc, D], F32, kind="ExternalOutput").ap()

    es = contextlib.ExitStack()
    sbt = lambda n, s, d: es.enter_context(nc.sbuf_tensor(n, s, d))
    pst = lambda n, s, d: es.enter_context(nc.psum_tensor(n, s, d))

    _open = {}

    def sopen(name, shape, dt):
        cm = nc.sbuf_tensor(name, shape, dt)
        t = cm.__enter__()
        _open[name] = cm
        return t

    def sclose(name):
        _open.pop(name).__exit__(None, None, None)

    W16 = sbt("W16", [128, NCH * D], BF16)
    xbuf = sbt("xbuf", [128, 3 * D], F32)
    ubuf = sbt("ubuf", [128, 3 * D], F32)
    h16b = sbt("h16b", [128, 2 * D], BF16)
    hTb = sbt("hTb", [128, 3 * D], BF16)
    zbuf32 = sbt("zbuf32", [128, 2 * 512], F32)
    zbufh = sbt("zbufh", [128, 2 * 512], BF16)
    zbufl = sbt("zbufl", [128, 2 * 512], BF16)
    Pt = sbt("Pt", [128, D], BF16)
    BRt = sbt("BRt", [128, D], BF16)
    row_lnaa = sbt("row_lnaa", [1, D], F32)
    row_br0 = sbt("row_br0", [1, D], BF16)
    biasF = sbt("biasF", [2, D], BF16)
    biasG = sbt("biasG", [2, D], BF16)
    statb = sbt("statb", [128, 3 * 6 * NO], F32)
    mvb = sbt("mvb", [128, 3 * 2], F32)
    rcol = sbt("rcol", [128, 3], F32)
    bcol = sbt("bcol", [128, 3], F32)
    abscol = sbt("abscol", [128, NCH + 1], F32)
    scol = sbt("scol", [128, 4], F32)
    row_m4 = sbt("row_m4", [4, 1], F32)
    mhalf = sbt("mhalf", [128, 1], F32)
    tri_sb = sbt("tri_sb", [128, 128], BF16)
    id_sb = sbt("id_sb", [128, 128], BF16)
    ones2_sb = sbt("ones2_sb", [2, 128], BF16)
    embZ_sb = sbt("embZ_sb", [128, 16 * NT], BF16)
    embU_sb = sbt("embU_sb", [128, 16 * NT], F32)
    tri9_sb = sbt("tri9_sb", [NT, NT + 1], F32)
    oc_sb = sbt("oc_sb", [1, 128], F32)
    o128_sb = sbt("o128_sb", [128, 1], F32)
    iotaP_sb = sbt("iotaP_sb", [1, 128], F32)
    iotaM16_sb = sbt("iotaM16_sb", [1, 16], F32)
    iota9_sb = sbt("iota9_sb", [1, NT + 1], F32)
    iota64_sb = sbt("iota64_sb", [1, NR], F32)
    sel64_sb = sbt("sel64_sb", [NR, 1], F32)
    pcs_sb = sbt("pcs_sb", [1, 4], F32)

    ps_mm = pst("ps_mm", [128, D], F32)
    ps_aux = pst("ps_aux", [128, 1024], F32)
    ps_tr = pst("ps_tr", [128, 512], BF16)
    ps_tr2 = pst("ps_tr2", [128, 512], BF16)

    sCst = Ctr(es.enter_context(nc.semaphore("sCst")), 16)
    sW = [Ctr(es.enter_context(nc.semaphore(f"sW{p}")), 16) for p in range(2)]
    sX = [Ctr(es.enter_context(nc.semaphore(f"sX{p}")), 16) for p in range(3)]
    sU = [Ctr(es.enter_context(nc.semaphore(f"sU{p}")), 16) for p in range(3)]
    sPx = [Ctr(es.enter_context(nc.semaphore(f"sPx{p}")), 16) for p in range(2)]
    sPg = [Ctr(es.enter_context(nc.semaphore(f"sPg{p}")), 16) for p in range(2)]
    sST = [Ctr(es.enter_context(nc.semaphore(f"sST{p}")), 16) for p in range(2)]
    sCY = [Ctr(es.enter_context(nc.semaphore(f"sCY{p}")), 16) for p in range(2)]
    sG = Ctr(es.enter_context(nc.semaphore("sG")), 16)
    sY = Ctr(es.enter_context(nc.semaphore("sY")), 16)
    sY2 = Ctr(es.enter_context(nc.semaphore("sY2")), 16)
    cT = Ctr(es.enter_context(nc.semaphore("cT")))
    cG = Ctr(es.enter_context(nc.semaphore("cG")))
    cV = Ctr(es.enter_context(nc.semaphore("cV")))
    cS = Ctr(es.enter_context(nc.semaphore("cS")))
    ccC = Ctr(es.enter_context(nc.semaphore("ccsem")))

    blk = es.enter_context(nc.Block())
    T, G, Y = blk.tensor, blk.gpsimd, blk.sync

    class _Chain:
        def __init__(self, eng, ctr):
            self._e = eng
            self._c = ctr

        def wait_ge(self, sem, n):
            self._e.wait_ge(sem, n)

        def __getattr__(self, name):
            fn = getattr(self._e, name)

            def call(*a, **k):
                self._e.wait_ge(self._c.sem, self._c.n)
                r = fn(*a, **k)
                self._c.inc(r)
                return r
            return call

    def V(f):
        blk.vector(lambda e: f(_Chain(e, cV)))

    def S(f):
        blk.scalar(lambda e: f(_Chain(e, cS)))

    def GC(f):
        blk.gpsimd(lambda e: f(_Chain(e, cG)))

    NG = max(1, NCH // 4)

    def emit_transpose_pass(j, s_src, s_dst, evac_ct_wait=None,
                            s_prefix=None, n_prefix=0, s_suffix=None, n_suffix=0):
        sbase = cS.n
        tbase = cT.n
        m_src = sbase + n_prefix
        ev_mark = lambda g: sbase + n_prefix + g + 1
        tr_mark = lambda g: tbase + min((g + 1) * 4, NCH)

        def tr_all(e):
            e.wait_ge(sCst.sem, m_consts)
            for g in range(NG):
                e.wait_ge(cS.sem, m_src if g < 2 else ev_mark(g - 2))
                pt = ps_tr if (g % 2) == 0 else ps_tr2
                for c4 in range(4):
                    c = g * 4 + c4
                    if c >= NCH:
                        break
                    cT.inc(e.transpose(out=pt[:, c4 * 128:(c4 + 1) * 128],
                                       in_=h16b[:, s_src + 128 * c:s_src + 128 * (c + 1)],
                                       identity=id_sb[:]))

        def merged(e):
            if s_prefix is not None:
                s_prefix(e)
            assert cS.n == m_src, f"prefix op count mismatch {cS.n} != {m_src}"
            if evac_ct_wait is not None:
                e.wait_ge(cT.sem, evac_ct_wait)
            for g in range(NG):
                e.wait_ge(cT.sem, tr_mark(g))
                pt = ps_tr if (g % 2) == 0 else ps_tr2
                n = min(4, NCH - g * 4) * 128
                e.activation(out=hTb[:, s_dst + g * 512:s_dst + g * 512 + n],
                             in_=pt[:, 0:n], func=AF.Identity)
            if s_suffix is not None:
                s_suffix(e)

        T(tr_all)
        assert cT.n == tbase + NCH
        S(merged)
        assert cS.n == sbase + n_prefix + NG + n_suffix, \
            f"scalar body count mismatch {cS.n} vs {sbase}+{n_prefix}+{NG}+{n_suffix}"
        return cT.n, sbase + n_prefix + NG

    ugob = sopen("ugob", [128, 2 * D], F32)
    wtmp = sopen("wtmp", [128, 2 * D], F32)
    r0 = ugob[0:1, 0:D]
    r1 = ugob[0:1, D:2 * D]
    rb16 = h16b[0:1, 0:D]

    def p_loads(e):
        for src, dst in [
            (dr_tri, tri_sb[:]), (dr_id, id_sb[:]), (dr_ones2, ones2_sb[:]),
            (dr_embZ, embZ_sb[:]), (dr_embU, embU_sb[:]), (dr_tri9, tri9_sb[:]),
            (dr_onescol, oc_sb[:]), (dr_ones128, o128_sb[:]),
            (dr_iotaP, iotaP_sb[:]),
            (dr_iotaM16, iotaM16_sb[:]), (dr_iota9, iota9_sb[:]),
            (dr_iota64, iota64_sb[:]), (dr_sel64, sel64_sb[:]), (dr_pcs, pcs_sb[:]),
            (dr_va, r0),
        ]:
            sCst.inc(e.dma_start(out=dst, in_=src))
    Y(p_loads)
    m_consts = sCst.n

    x_marks = [0] * NT

    def p_x01(e):
        sX[0].inc(e.dma_start(out=xbuf[:, 0:D], in_=dr_x[0:128, :]))
        x_marks[0] = sX[0].n
        sX[1].inc(e.dma_start(out=xbuf[:, D:2 * D], in_=dr_x[128:256, :]))
        x_marks[1] = sX[1].n
    Y(p_x01)

    def p_mh(e):
        e.memset(mhalf[:, :], -0.5)
    V(p_mh)
    m_mhalf = cV.n

    mk_x1_tt = [0] * NT
    mk_u32 = [0] * NT
    mk_rb = [0] * NT
    mk_rb2 = [0] * NT
    mk_spill = [0] * NT
    mk_hT = [0] * NT
    mk_trevac = [0] * NT
    mk_mmA = [0] * NT
    mk_mmAc = [[0] * NO for _ in range(NT)]
    mk_x1c = [[0] * NO for _ in range(NT)]
    mk_bcol = [0] * NT

    def a_pre(j):
        s = (j % 3) * D
        c3 = j % 3

        def a_stats(e, j=j, s=s, c3=c3):
            e.wait_ge(sX[j % 3].sem, x_marks[j])
            if j >= 3:
                e.wait_ge(cG.sem, mk_rb2[j - 3])
            so = c3 * 6 * NO
            for c in range(NO):
                e.bn_stats(out=statb[:, so + 6 * c:so + 6 * c + 6],
                           in_=xbuf[:, s + 512 * c:s + 512 * (c + 1)])
            e.bn_aggr(out=mvb[:, c3 * 2:c3 * 2 + 2],
                      in_=statb[:, so:so + 6 * NO])
        V(a_stats)
        m_st = cV.n

        def a_rb(e, j=j, c3=c3):
            e.wait_ge(cV.sem, m_st)
            e.wait_ge(cV.sem, m_mhalf)
            if j >= 3:
                e.wait_ge(cV.sem, mk_u32[j - 3])
            e.tensor_scalar(out=rcol[:, c3:c3 + 1], in0=mvb[:, 2 * c3 + 1:2 * c3 + 2],
                            scalar1=1e-5, scalar2=None, op0=ALU.add, op1=ALU.bypass)
            e.tensor_tensor(out=rcol[:, c3:c3 + 1], in0=rcol[:, c3:c3 + 1],
                            in1=mhalf[:, :], op=ALU.pow)
        GC(a_rb)
        mk_rb[j] = cG.n

        def a_b(e, j=j, c3=c3):
            e.wait_ge(cG.sem, mk_rb[j])
            e.tensor_scalar(out=bcol[:, c3:c3 + 1], in0=mvb[:, 2 * c3:2 * c3 + 1],
                            scalar1=rcol[:, c3:c3 + 1], scalar2=-1.0,
                            op0=ALU.mult, op1=ALU.mult)
        V(a_b)
        mk_bcol[j] = cV.n

    def a_pre_t(j):
        s = (j % 3) * D
        c3 = j % 3

        def h1_prefix(e, j=j, s=s, c3=c3):
            e.wait_ge(cV.sem, mk_bcol[j])
            if j >= 2:
                e.wait_ge(cT.sem, mk_hT[j - 2])
            e.activation(out=h16b[:, (j % 2) * D:(j % 2) * D + D],
                         in_=xbuf[:, s:s + D],
                         func=AF.Identity, bias=bcol[:, c3:c3 + 1],
                         scale=rcol[:, c3:c3 + 1])

        evac_ct = mk_mmAc[j - 3][NO - 1] if j >= 3 else None
        mk_hT[j], mk_trevac[j] = emit_transpose_pass(
            j, (j % 2) * D, (j % 3) * D, evac_ct_wait=evac_ct,
            s_prefix=h1_prefix, n_prefix=1)

    def a_mm(j):
        s2 = (j % 3) * D
        for o in range(NO):
            def a_mmo(e, j=j, s2=s2, o=o):
                if o == 0:
                    e.wait_ge(cS.sem, mk_trevac[j])
                    e.wait_ge(cV.sem, tern_marks[NCH - 1])
                    e.wait_ge(cV.sem, m_psmm_free_rows)
                    e.wait_ge(cS.sem, m_P)
                    e.wait_ge(sG.sem, m_fb5)
                if j >= 1:
                    e.wait_ge(cV.sem, mk_x1c[j - 1][o])
                for i in range(NCH):
                    e.matmul(out=ps_mm[:, o * 512:(o + 1) * 512],
                             lhsT=hTb[:, s2 + 128 * i:s2 + 128 * (i + 1)],
                             rhs=W16[:, i * D + o * 512:i * D + o * 512 + 512],
                             start=(i == 0), stop=False)
                cT.inc(e.matmul(out=ps_mm[:, o * 512:(o + 1) * 512], lhsT=ones2_sb[:],
                                rhs=biasF[:, o * 512:(o + 1) * 512], start=False, stop=True))
            T(a_mmo)
            mk_mmAc[j][o] = cT.n
        mk_mmA[j] = mk_mmAc[j][NO - 1]

    def a_post(j):
        s = (j % 3) * D
        c3 = j % 3
        for o in range(NO):
            def a_x1o(e, j=j, s=s, o=o):
                e.wait_ge(cT.sem, mk_mmAc[j][o])
                e.tensor_tensor(out=xbuf[:, s + o * 512:s + o * 512 + 512],
                                in0=xbuf[:, s + o * 512:s + o * 512 + 512],
                                in1=ps_mm[:, o * 512:(o + 1) * 512], op=ALU.add)
            V(a_x1o)
            mk_x1c[j][o] = cV.n
        mk_x1_tt[j] = mk_x1c[j][NO - 1]

        def a_stats2(e, j=j, s=s, c3=c3):
            e.wait_ge(cG.sem, mk_rb[j])
            so = c3 * 6 * NO
            for c in range(NO):
                e.bn_stats(out=statb[:, so + 6 * c:so + 6 * c + 6],
                           in_=xbuf[:, s + 512 * c:s + 512 * (c + 1)])
            e.bn_aggr(out=mvb[:, c3 * 2:c3 * 2 + 2],
                      in_=statb[:, so:so + 6 * NO])
        V(a_stats2)
        m_st2 = cV.n

        def a_rb2(e, j=j, c3=c3):
            e.wait_ge(cV.sem, m_st2)
            e.wait_ge(cS.sem, mk_h1_next[0])
            e.tensor_scalar(out=rcol[:, c3:c3 + 1], in0=mvb[:, 2 * c3 + 1:2 * c3 + 2],
                            scalar1=1e-5, scalar2=None, op0=ALU.add, op1=ALU.bypass)
            e.tensor_tensor(out=rcol[:, c3:c3 + 1], in0=rcol[:, c3:c3 + 1],
                            in1=mhalf[:, :], op=ALU.pow)
        GC(a_rb2)
        mk_rb2[j] = cG.n

        def a_b2(e, j=j, c3=c3):
            e.wait_ge(cG.sem, mk_rb2[j])
            e.tensor_scalar(out=bcol[:, c3:c3 + 1], in0=mvb[:, 2 * c3:2 * c3 + 1],
                            scalar1=rcol[:, c3:c3 + 1], scalar2=-1.0,
                            op0=ALU.mult, op1=ALU.mult)
        V(a_b2)
        m_bb2 = cV.n

        def a_u32(e, j=j, s=s, c3=c3):
            if j >= 2:
                e.wait_ge(sPx[j % 2].sem, mk_spill[j - 2])
            e.tensor_scalar(out=ubuf[:, (j % 2) * D:(j % 2) * D + D],
                            in0=xbuf[:, s:s + D],
                            scalar1=rcol[:, c3:c3 + 1], scalar2=bcol[:, c3:c3 + 1],
                            op0=ALU.mult, op1=ALU.add)
        V(a_u32)
        mk_u32[j] = cV.n

        def a_spill(e, j=j, s=s):
            e.wait_ge(cV.sem, mk_u32[j])
            sPx[j % 2].inc(e.dma_start(out=dr_x1[128 * j:128 * (j + 1), :],
                                       in_=xbuf[:, s:s + D]))
            sPx[j % 2].inc(e.dma_start(out=dr_u32[128 * j:128 * (j + 1), :],
                                       in_=ubuf[:, (j % 2) * D:(j % 2) * D + D]))
        G(a_spill)
        mk_spill[j] = sPx[j % 2].n

        if j + 3 < NT:
            def a_xpre(e, j=j):
                e.wait_ge(sPx[j % 2].sem, mk_spill[j])
                sX[(j + 3) % 3].inc(e.dma_start(
                    out=xbuf[:, ((j + 3) % 3) * D:((j + 3) % 3) * D + D],
                    in_=dr_x[128 * (j + 3):128 * (j + 4), :]))
                x_marks[j + 3] = sX[(j + 3) % 3].n
            Y(a_xpre)

    if NT > 2:
        def p_x2(e):
            sX[2].inc(e.dma_start(out=xbuf[:, 2 * D:3 * D], in_=dr_x[256:384, :]))
            x_marks[2] = sX[2].n
        Y(p_x2)

    mk_h1_next = [0]
    a_pre(0)
    if NT > 1:
        a_pre(1)
    a_pre_t(0)
    if NT > 1:
        a_pre_t(1)
    mk_h1_next[0] = cS.n

    m_wchunk = [0] * NCH
    absred_marks = [0] * NCH
    for k in range(NCH):
        def p_wld(e, k=k):
            if k >= 2:
                e.wait_ge(cV.sem, absred_marks[k - 2])
            sW[k % 2].inc(e.dma_start(out=wtmp[:, (k % 2) * D:(k % 2 + 1) * D],
                                      in_=dr_wfT[128 * k:128 * (k + 1), :]))
            m_wchunk[k] = sW[k % 2].n
        G(p_wld)

        def p_absk(e, k=k):
            e.wait_ge(sW[k % 2].sem, m_wchunk[k])
            e.tensor_reduce(out=abscol[:, k:k + 1],
                            in_=wtmp[:, (k % 2) * D:(k % 2 + 1) * D],
                            axis=AX.X, op=ALU.add, apply_absolute_value=True)
            absred_marks[k] = cV.n
        V(p_absk)

    def p_absred(e):
        e.tensor_reduce(out=abscol[:, NCH:NCH + 1], in_=abscol[:, 0:NCH],
                        axis=AX.X, op=ALU.add)
        e.memset(scol[0:1, 3:4], 1e-5)
    V(p_absred)
    m_absred = cV.n

    def p_scale_mm(e):
        e.wait_ge(sCst.sem, m_consts)
        e.wait_ge(cV.sem, m_absred)
        cT.inc(e.matmul(out=ps_aux[0:1, 0:1], lhsT=o128_sb[:], rhs=abscol[:, NCH:NCH + 1],
                        start=True, stop=True))
    T(p_scale_mm)
    m_scale_mm = cT.n

    def p_scale(e):
        e.wait_ge(cT.sem, m_scale_mm)
        e.activation(out=scol[0:1, 2:3], in_=ps_aux[0:1, 0:1], func=AF.Identity,
                     scale=0.5 / (D * D))
    S(p_scale)
    m_hs = cS.n

    def p_thr(e):
        e.wait_ge(cS.sem, m_hs)
        e.tensor_scalar(out=scol[0:1, 0:1], in0=scol[0:1, 2:3],
                        scalar1=5e-11, scalar2=None, op0=ALU.add, op1=ALU.bypass)
        e.tensor_scalar(out=scol[0:1, 1:2], in0=scol[0:1, 0:1],
                        scalar1=-1.0, scalar2=None, op0=ALU.mult, op1=ALU.bypass)
    V(p_thr)
    m_thr = cV.n

    def p_thr_b(e):
        e.wait_ge(cV.sem, m_thr)
        cT.inc(e.matmul(out=ps_aux[:, 4:8], lhsT=oc_sb[:], rhs=scol[0:1, 0:4],
                        start=True, stop=True))
    T(p_thr_b)
    m_thr_b = cT.n

    def p_thr_c(e):
        e.wait_ge(cT.sem, m_thr_b)
        e.activation(out=scol[:, 0:4], in_=ps_aux[:, 4:8], func=AF.Identity)
    S(p_thr_c)
    m_thr_c = cS.n

    def p_rows1(e):
        e.wait_ge(sCst.sem, m_consts)
        e.activation(out=r1, in_=r0, func=AF.Tanh)
        e.activation(out=row_lnaa[:], in_=r1, func=AF.Ln)
    S(p_rows1)
    m_rows1 = cS.n

    def p_br0a(e):
        e.wait_ge(cS.sem, m_rows1)
        e.tensor_scalar(out=r0, in0=r1, scalar1=pcs_sb[0:1, 1:2],
                        scalar2=None, op0=ALU.max, op1=ALU.bypass)
        e.reciprocal(out=r1, in_=r0)
    V(p_br0a)
    m_br0a = cV.n

    def p_br0b(e):
        e.wait_ge(cV.sem, m_br0a)
        sCst.inc(e.dma_start(out=r0, in_=dr_vb))
    Y(p_br0b)
    m_vb1 = sCst.n

    def p_br0c(e):
        e.wait_ge(sCst.sem, m_vb1)
        e.tensor_tensor(out=r1, in0=r1, in1=r0, op=ALU.mult)
    V(p_br0c)
    m_br0c = cV.n

    def p_br0d(e):
        e.wait_ge(cV.sem, m_br0c)
        sCst.inc(e.dma_start(out=r0, in_=dr_vc))
    Y(p_br0d)
    m_vc1 = sCst.n

    def p_br0e(e):
        e.wait_ge(sCst.sem, m_vc1)
        e.tensor_tensor(out=r1, in0=r1, in1=r0, op=ALU.mult)
    V(p_br0e)
    m_br0e = cV.n

    def p_br0f(e):
        e.wait_ge(cV.sem, m_br0e)
        e.activation(out=row_br0[:], in_=r1, func=AF.Identity)
    S(p_br0f)
    m_br0s = cS.n
    m_br0 = m_br0e

    def p_r1P(e):
        e.wait_ge(cS.sem, m_rows1)
        for o in range(NO):
            cT.inc(e.matmul(out=ps_mm[:, o * 512:(o + 1) * 512], lhsT=iotaP_sb[:],
                            rhs=row_lnaa[0:1, o * 512:(o + 1) * 512], start=True, stop=True))
    T(p_r1P)
    m_r1P = cT.n

    def p_P(e):
        e.wait_ge(cT.sem, m_r1P)
        e.activation(out=Pt[:, :], in_=ps_mm[:, :], func=AF.Exp)
    S(p_P)
    m_P = cS.n

    brstage = ubuf[:, 0:D]

    def p_BR1(e):
        e.wait_ge(cT.sem, m_r1P)
        e.activation(out=brstage, in_=ps_mm[:, :], func=AF.Exp, scale=-1.0)
    S(p_BR1)
    m_BR1 = cS.n

    def p_bload(e):
        e.wait_ge(cS.sem, m_br0s)
        sCst.inc(e.dma_start(out=r1, in_=dr_vb))
    Y(p_bload)
    m_vb2 = sCst.n

    def p_r1b2(e):
        e.wait_ge(cS.sem, m_BR1)
        e.wait_ge(sCst.sem, m_vb2)
        for o in range(NO):
            cT.inc(e.matmul(out=ps_mm[:, o * 512:(o + 1) * 512], lhsT=oc_sb[:],
                            rhs=r1[0:1, o * 512:(o + 1) * 512], start=True, stop=True))
    T(p_r1b2)
    m_r1b = cT.n

    def p_BR2(e):
        e.wait_ge(cT.sem, m_r1b)
        e.tensor_tensor(out=brstage, in0=brstage, in1=ps_mm[:, :], op=ALU.mult)
    V(p_BR2)
    m_BR2 = cV.n

    def p_r1c(e):
        e.wait_ge(cV.sem, m_BR2)
        for o in range(NO):
            cT.inc(e.matmul(out=ps_mm[:, o * 512:(o + 1) * 512], lhsT=oc_sb[:],
                            rhs=r0[0:1, o * 512:(o + 1) * 512], start=True, stop=True))
    T(p_r1c)
    m_r1c = cT.n

    def p_BR3(e):
        e.wait_ge(cT.sem, m_r1c)
        e.tensor_tensor(out=BRt[:, :], in0=brstage, in1=ps_mm[:, :], op=ALU.mult)
    V(p_BR3)
    m_BR3 = cV.n
    m_psmm_free_rows = m_BR3

    def p_fb1(e):
        e.wait_ge(cV.sem, m_BR3)
        sCst.inc(e.dma_start(out=r0, in_=dr_vfb))
    Y(p_fb1)
    m_fb1 = sCst.n

    def p_fb2(e):
        e.wait_ge(sCst.sem, m_fb1)
        e.tensor_scalar(out=r1, in0=r0, scalar1=0.5, scalar2=None,
                        op0=ALU.mult, op1=ALU.bypass)
    V(p_fb2)
    m_fb2 = cV.n

    def p_fb3(e):
        e.wait_ge(cV.sem, m_fb2)
        e.activation(out=biasF[0:1, :], in_=r1, func=AF.Identity)
    S(p_fb3)
    m_fb3 = cS.n

    def p_fb4(e):
        e.wait_ge(cS.sem, m_fb3)
        e.tensor_tensor(out=rb16, in0=r1, in1=biasF[0:1, :], op=ALU.subtract)
    V(p_fb4)
    m_fb4 = cV.n

    def p_fb5(e):
        e.wait_ge(cV.sem, m_fb4)
        sG.inc(e.dma_start(out=biasF[1:2, :], in_=rb16))
    G(p_fb5)
    m_fb5 = sG.n

    def p_gb1(e):
        e.wait_ge(sG.sem, m_fb5)
        sCst.inc(e.dma_start(out=r0, in_=dr_vgb))
    Y(p_gb1)
    m_gb1 = sCst.n

    def p_gb2(e):
        e.wait_ge(sCst.sem, m_gb1)
        e.activation(out=biasG[0:1, :], in_=r0, func=AF.Identity)
    S(p_gb2)
    m_gb2 = cS.n

    def p_gb3(e):
        e.wait_ge(cS.sem, m_gb2)
        e.tensor_tensor(out=rb16, in0=r0, in1=biasG[0:1, :], op=ALU.subtract)
    V(p_gb3)
    m_gb3 = cV.n

    def p_gb4(e):
        e.wait_ge(cV.sem, m_gb3)
        sG.inc(e.dma_start(out=biasG[1:2, :], in_=rb16))
    G(p_gb4)
    m_gb4 = sG.n
    m_rows_done_S = m_gb2
    m_rows_done_V = m_gb3

    m_wchunk2 = [0] * NCH
    tern_marks = [0] * NCH
    for k in range(NCH):
        def p_wld2(e, k=k):
            if k >= 2:
                e.wait_ge(cV.sem, tern_marks[k - 2])
            else:
                e.wait_ge(cV.sem, m_absred)
            sW[k % 2].inc(e.dma_start(out=wtmp[:, (k % 2) * D:(k % 2 + 1) * D],
                                      in_=dr_wfT[128 * k:128 * (k + 1), :]))
            m_wchunk2[k] = sW[k % 2].n
        G(p_wld2)

        def p_tern(e, k=k):
            sl = wtmp[:, (k % 2) * D:(k % 2 + 1) * D]
            e.wait_ge(sW[k % 2].sem, m_wchunk2[k])
            e.wait_ge(cS.sem, m_thr_c)
            if k == 0:
                e.wait_ge(cV.sem, m_rows_done_V)
                e.wait_ge(cS.sem, m_rows_done_S)
                e.wait_ge(sG.sem, m_gb4)
                e.wait_ge(sCst.sem, sCst.n)
            e.tensor_scalar(out=ugob[:, 0:D], in0=sl,
                            scalar1=scol[:, 0:1], scalar2=scol[:, 2:3],
                            op0=ALU.is_ge, op1=ALU.mult)
            e.tensor_scalar(out=ugob[:, D:2 * D], in0=sl,
                            scalar1=scol[:, 1:2], scalar2=scol[:, 2:3],
                            op0=ALU.is_le, op1=ALU.mult)
            e.tensor_tensor(out=W16[:, k * D:(k + 1) * D], in0=ugob[:, 0:D],
                            in1=ugob[:, D:2 * D], op=ALU.subtract)
            tern_marks[k] = cV.n
        V(p_tern)

    def p_wspill(e):
        e.wait_ge(cV.sem, tern_marks[NCH - 1])
        sG.inc(e.dma_start(out=dr_wf16, in_=W16[:, :]))
    G(p_wspill)
    m_wspill = sG.n

    for j in range(NT):
        if j + 2 < NT:
            a_pre(j + 2)
        a_mm(j)
        if j + 2 < NT:
            a_pre_t(j + 2)
            mk_h1_next[0] = cS.n
        a_post(j)

    m_gchunk = [0] * NCH
    gcast_marks = [0] * NCH
    for k in range(NCH):
        def i_wld(e, k=k):
            if k >= 2:
                e.wait_ge(cS.sem, gcast_marks[k - 2])
            else:
                e.wait_ge(cV.sem, tern_marks[NCH - 1])
            sW[k % 2].inc(e.dma_start(out=wtmp[:, (k % 2) * D:(k % 2 + 1) * D],
                                      in_=dr_wgT[128 * k:128 * (k + 1), :]))
            m_gchunk[k] = sW[k % 2].n
        G(i_wld)

        def i_gcast(e, k=k):
            e.wait_ge(sW[k % 2].sem, m_gchunk[k])
            e.wait_ge(sG.sem, m_wspill)
            e.wait_ge(cT.sem, mk_mmA[NT - 1])
            e.activation(out=W16[:, k * D:(k + 1) * D],
                         in_=wtmp[:, (k % 2) * D:(k % 2 + 1) * D], func=AF.Identity)
            gcast_marks[k] = cS.n
        S(i_gcast)

    sclose("wtmp")
    gbufB = sopen("gbufB", [128, 4 * 512], F32)
    T16sb = sopen("T16sb", [16, D], F32)
    nhalf = 2 if NO > 1 else 1

    def b_t16z(e):
        e.wait_ge(cS.sem, gcast_marks[NCH - 1])
        e.memset(T16sb[:, :], 0.0)
    V(b_t16z)
    m_t16z = cV.n

    mk_u32re = [0] * NT
    mk_ug = [0] * NT
    mk_ug_spill = [0] * NT
    mk_gmm = [0] * NT
    mk_gmmc = [[0] * NO for _ in range(NT)]
    mk_sig = [0] * NT
    mk_sigc = [[0] * NO for _ in range(NT)]
    mk_uT = [0] * NT
    mk_uTev = [0] * NT
    mk_u16 = [0] * NT
    zh_marks = [0] * NQ
    zl_marks = [0] * NQ
    zmm_marks = [0] * NQ
    tadd_marks = [0] * NQ

    def b_l01(e):
        e.wait_ge(sPx[0].sem, sPx[0].n)
        e.wait_ge(sPx[1].sem, sPx[1].n)
        for jj in range(min(3, NT)):
            sU[jj % 3].inc(e.dma_start(out=ubuf[:, jj * D:(jj + 1) * D],
                                       in_=dr_u32[128 * jj:128 * (jj + 1), :]))
            mk_u32re[jj] = sU[jj % 3].n
    Y(b_l01)

    def b_pre(j):
        s = (j % 2) * D
        su = (j % 3) * D

        def b_u16(e, j=j, s=s, su=su):
            e.wait_ge(sU[j % 3].sem, mk_u32re[j])
            if j <= 1:
                e.wait_ge(cT.sem, mk_hT[NT - 1])
            if j >= 2:
                e.wait_ge(cT.sem, mk_uT[j - 2])
            e.activation(out=h16b[:, s:s + D], in_=ubuf[:, su:su + D], func=AF.Identity)

        if j < 3:
            evac_ct = mk_mmA[NT - 1]
        else:
            evac_ct = mk_gmmc[j - 3][NO - 1]
        mk_uT[j], mk_uTev[j] = emit_transpose_pass(
            j, (j % 2) * D, (j % 3) * D, evac_ct_wait=evac_ct, s_prefix=b_u16, n_prefix=1)
        mk_u16[j] = mk_uTev[j]

    def b_mm(j):
        s = (j % 3) * D
        for o in range(NO):
            def b_mmo(e, j=j, s=s, o=o):
                if o == 0:
                    e.wait_ge(cS.sem, mk_uTev[j])
                    e.wait_ge(cS.sem, gcast_marks[NCH - 1])
                    e.wait_ge(sG.sem, max(m_gb4, m_wspill))
                if j == 0:
                    e.wait_ge(cV.sem, mk_x1c[NT - 1][o])
                else:
                    e.wait_ge(cS.sem, mk_sigc[j - 1][o])
                for i in range(NCH):
                    e.matmul(out=ps_mm[:, o * 512:(o + 1) * 512],
                             lhsT=hTb[:, s + 128 * i:s + 128 * (i + 1)],
                             rhs=W16[:, i * D + o * 512:i * D + o * 512 + 512],
                             start=(i == 0), stop=False)
                cT.inc(e.matmul(out=ps_mm[:, o * 512:(o + 1) * 512], lhsT=ones2_sb[:],
                                rhs=biasG[:, o * 512:(o + 1) * 512], start=False, stop=True))
            T(b_mmo)
            mk_gmmc[j][o] = cT.n
        mk_gmm[j] = mk_gmmc[j][NO - 1]

    def b_sig_f(j):
        s = (j % 2) * D
        cv_ug_base = cV.n

        def b_sig(e, j=j, s=s, cv_ug_base=cv_ug_base):
            if j >= 1:
                e.wait_ge(cV.sem, mk_ug[j - 1])
            if j == 0:
                e.wait_ge(cV.sem, tern_marks[NCH - 1])
            for o in range(NO):
                e.wait_ge(cT.sem, mk_gmmc[j][o])
                e.activation(out=gbufB[:, o * 512:o * 512 + 512],
                             in_=ps_mm[:, o * 512:(o + 1) * 512], func=AF.Sigmoid)
        S(b_sig)
        mk_sig[j] = cS.n
        for o in range(NO):
            mk_sigc[j][o] = mk_sig[j] - (NO - 1 - o)

    def b_post(j):
        s = (j % 2) * D

        def b_ug(e, j=j, s=s):
            if j >= 2:
                e.wait_ge(sPg[j % 2].sem, mk_ug_spill[j - 2])
                e.wait_ge(cT.sem, zmm_marks[(j - 2) * NO + NO - 1])
            su = (j % 3) * D
            for o in range(NO):
                e.wait_ge(cS.sem, mk_sigc[j][o])
                e.tensor_tensor(out=ugob[:, s + o * 512:s + o * 512 + 512],
                                in0=ubuf[:, su + o * 512:su + o * 512 + 512],
                                in1=gbufB[:, o * 512:o * 512 + 512],
                                op=ALU.mult)
        V(b_ug)
        mk_ug[j] = cV.n

        def b_sp(e, j=j, s=s):
            e.wait_ge(cV.sem, mk_ug[j])
            sPg[j % 2].inc(e.dma_start(out=dr_ug[128 * j:128 * (j + 1), :],
                                       in_=ugob[:, s:s + D]))
        G(b_sp)
        mk_ug_spill[j] = sPg[j % 2].n

        if j + 3 < NT:
            def b_upre(e, j=j):
                e.wait_ge(cV.sem, mk_ug[j])
                sU[(j + 3) % 3].inc(e.dma_start(
                    out=ubuf[:, (j % 3) * D:(j % 3) * D + D],
                    in_=dr_u32[128 * (j + 3):128 * (j + 4), :]))
                mk_u32re[j + 3] = sU[(j + 3) % 3].n
            Y(b_upre)



    def b_zchain(j):
        s = (j % 2) * D
        for o in range(NO):
            q = j * NO + o
            zs = (q % 2) * 512

            def b_z(e, j=j, s=s, o=o, q=q, zs=zs):
                e.wait_ge(cV.sem, mk_ug[j])
                if q >= 2:
                    e.wait_ge(cT.sem, zmm_marks[q - 2])
                e.tensor_tensor(out=zbuf32[:, zs:zs + 512],
                                in0=ugob[:, s + o * 512:s + o * 512 + 512],
                                in1=BRt[:, o * 512:o * 512 + 512], op=ALU.mult)
                if j == 0:
                    e.wait_ge(cS.sem, m_br0s)
                    e.tensor_tensor(out=zbuf32[0:1, zs:zs + 512],
                                    in0=ugob[0:1, s + o * 512:s + o * 512 + 512],
                                    in1=row_br0[0:1, o * 512:o * 512 + 512],
                                    op=ALU.mult)
            V(b_z)
            m_z32 = cV.n

            def b_zh(e, q=q, zs=zs, m_z32=m_z32):
                if q >= 2:
                    e.wait_ge(cT.sem, zmm_marks[q - 2])
                e.tensor_copy(out=zbufh[:, zs:zs + 512], in_=zbuf32[:, zs:zs + 512])
                e.tensor_tensor(out=zbufl[:, zs:zs + 512], in0=zbuf32[:, zs:zs + 512],
                                in1=zbufh[:, zs:zs + 512], op=ALU.subtract)
            V(b_zh)
            zh_marks[q] = cV.n
            zl_marks[q] = cV.n

            def b_zmm(e, j=j, s=s, o=o, q=q, zs=zs):
                e.wait_ge(cV.sem, zl_marks[q])
                if q >= 2:
                    e.wait_ge(cV.sem, tadd_marks[q - 2])
                else:
                    e.wait_ge(cS.sem, m_thr_c)
                pa = ps_aux[0:16, zs:zs + 512]
                e.matmul(out=pa, lhsT=embZ_sb[:, 16 * j:16 * j + 16],
                         rhs=zbufh[:, zs:zs + 512], start=True, stop=False)
                e.matmul(out=pa, lhsT=embZ_sb[:, 16 * j:16 * j + 16],
                         rhs=zbufl[:, zs:zs + 512], start=False, stop=False)
                cT.inc(e.matmul(out=pa, lhsT=embU_sb[:, 16 * j:16 * j + 16],
                                rhs=ugob[:, s + o * 512:s + o * 512 + 512],
                                start=False, stop=True))
            T(b_zmm)
            zmm_marks[q] = cT.n

            def b_tadd(e, j=j, o=o, q=q, zs=zs):
                e.wait_ge(cT.sem, zmm_marks[q])
                if q == 0:
                    e.wait_ge(cV.sem, m_t16z)
                e.tensor_tensor(out=T16sb[:, o * 512:(o + 1) * 512],
                                in0=T16sb[:, o * 512:(o + 1) * 512],
                                in1=ps_aux[0:16, zs:zs + 512], op=ALU.add)
            V(b_tadd)
            tadd_marks[q] = cV.n
    b_pre(0)
    if NT > 1:
        b_pre(1)
    for j in range(NT):
        b_mm(j)
        b_sig_f(j)
        if j >= 1:
            b_zchain(j - 1)
        if j + 2 < NT:
            b_pre(j + 2)
        b_post(j)
    b_zchain(NT - 1)

    m16mm_marks = [0] * NO
    m16exp_marks = [0] * NO
    m16tt_marks = [0] * NO
    cv_base_m16 = [0]

    def c_m16a(e):
        e.wait_ge(cV.sem, tadd_marks[NQ - 1])
        cv_base = cV.n
        cv_base_m16[0] = cv_base
        for o in range(NO):
            if o >= 2:
                e.wait_ge(cV.sem, cv_base + (o - 2) + 1)
            cT.inc(e.matmul(out=ps_aux[0:16, (o % 2) * 512:(o % 2) * 512 + 512],
                            lhsT=iotaM16_sb[:],
                            rhs=row_lnaa[0:1, o * 512:(o + 1) * 512],
                            start=True, stop=True))
            m16mm_marks[o] = cT.n
    T(c_m16a)

    def c_m16b(e):
        for o in range(NO):
            e.wait_ge(cT.sem, m16mm_marks[o])
            if o >= 2:
                e.wait_ge(cV.sem, cv_base_m16[0] + (o - 2) + 1)
            e.activation(out=ps_aux[0:16, (o % 2) * 512:(o % 2) * 512 + 512],
                         in_=ps_aux[0:16, (o % 2) * 512:(o % 2) * 512 + 512],
                         func=AF.Exp)
            m16exp_marks[o] = cS.n
    S(c_m16b)

    def c_f16(e):
        for o in range(NO):
            e.wait_ge(cS.sem, m16exp_marks[o])
            e.tensor_tensor(out=T16sb[:, o * 512:(o + 1) * 512],
                            in0=T16sb[:, o * 512:(o + 1) * 512],
                            in1=ps_aux[0:16, (o % 2) * 512:(o % 2) * 512 + 512],
                            op=ALU.mult)
            m16tt_marks[o] = cV.n
    V(c_f16)
    m_f16 = cV.n

    def c_din(e):
        e.wait_ge(cV.sem, m_f16)
        sG.inc(e.dma_start(out=dr_cc_in, in_=T16sb[:, :]))
    G(c_din)
    m_ccin = sG.n

    def c_cc(e):
        e.wait_ge(sG.sem, m_ccin)
        ccC.inc(e.collective_compute(
            "AllGather", ALU.bypass,
            replica_groups=[list(range(n_cores))],
            ins=[dr_cc_in.opt()],
            outs=[dr_cc_out.opt()],
        ))
    G(c_cc)
    m_cc = ccC.n

    sclose("T16sb")
    sclose("gbufB")
    sclose("ugob")
    bscr = sopen("bscr", [NT, 2 * D], BF16)
    deff_b = sopen("deff_b", [128, D], BF16)
    s1 = sopen("s1", [128, D], F32)
    s2 = sopen("s2", [128, D], F32)

    a64_marks = []
    for half in range(nhalf):
        os_ = tuple(range(half * (NO // nhalf), (half + 1) * (NO // nhalf)))
        def c_a64a(e, os_=os_):
            e.wait_ge(cV.sem, tadd_marks[NQ - 1])
            e.wait_ge(cV.sem, m_f16)
            if a64_marks:
                e.wait_ge(cS.sem, a64_marks[-1])
            for oi, o in enumerate(os_):
                cT.inc(e.matmul(out=ps_aux[0:NR, oi * 512:oi * 512 + 512],
                                lhsT=iota64_sb[:],
                                rhs=row_lnaa[0:1, o * 512:(o + 1) * 512],
                                start=True, stop=True))
        T(c_a64a)
        m_a64a = cT.n

        def c_a64b(e, os_=os_, m_a64a=m_a64a):
            e.wait_ge(cT.sem, m_a64a)
            e.wait_ge(sG.sem, m_ccin)
            e.wait_ge(cV.sem, m_f16)
            for oi, o in enumerate(os_):
                e.activation(out=s2[0:NR, o * 512:(o + 1) * 512],
                             in_=ps_aux[0:NR, oi * 512:oi * 512 + 512], func=AF.Exp)
        S(c_a64b)
        a64_marks.append(cS.n)
    m_a64 = a64_marks[-1]

    def c_dout(e):
        e.wait_ge(ccC.sem, m_cc)
        e.wait_ge(sPg[0].sem, sPg[0].n)
        e.wait_ge(sPg[1].sem, sPg[1].n)
        for r in range(n_cores):
            sG.inc(e.dma_start(out=s1[r * NT:(r + 1) * NT, 0:D],
                               in_=dr_cc_out[16 * r:16 * r + NT, :]))
            sG.inc(e.dma_start(out=s1[64 + r * NT:64 + (r + 1) * NT, 0:D],
                               in_=dr_cc_out[16 * r + 8:16 * r + 8 + NT, :]))
    G(c_dout)
    m_ccout = sG.n

    def c_fown(e):
        e.wait_ge(cT.sem, m_dlr7_h[0])
        sG.inc(e.dma_start(out=s1[0:NT, 0:D], in_=dr_cc_in[0:NT, :]))


    def c_w64(e):
        e.wait_ge(sG.sem, m_ccout)
        e.wait_ge(cS.sem, m_a64)
        e.tensor_tensor(out=s1[0:NR, :], in0=s1[0:NR, :], in1=s2[0:NR, :], op=ALU.mult)
    V(c_w64)
    m_w64 = cV.n
    m_dlr7_h = [0]

    def c_cpre(e):
        e.wait_ge(cV.sem, m_w64)
        e.wait_ge(cS.sem, mk_sig[NT - 1])
        for o in range(NO):
            cT.inc(e.matmul(out=ps_mm[0:1, o * 512:(o + 1) * 512], lhsT=sel64_sb[:],
                            rhs=s1[0:NR, o * 512:(o + 1) * 512], start=True, stop=True))
    T(c_cpre)
    m_cpre = cT.n

    def c_apow(e):
        e.wait_ge(cT.sem, m_cpre)
        e.wait_ge(cV.sem, m_w64)
        e.activation(out=s2[0:1, 0:D], in_=row_lnaa[:], func=AF.Exp,
                     scale=pcs_sb[0:1, 0:1])
    S(c_apow)
    m_apow = cS.n

    def c_crow(e):
        e.wait_ge(cS.sem, m_apow)
        e.wait_ge(cV.sem, mk_x1_tt[NT - 1])
        e.wait_ge(sPx[0].sem, sPx[0].n)
        e.wait_ge(sPx[1].sem, sPx[1].n)
        e.tensor_tensor(out=xbuf[0:1, 0:D], in0=ps_mm[0:1, 0:D], in1=s2[0:1, 0:D],
                        op=ALU.mult)
    V(c_crow)
    m_crow = cV.n

    def c_ummm(e):
        e.wait_ge(cV.sem, m_crow)
        for o in range(NO):
            cT.inc(e.matmul(out=ps_mm[0:1, o * 512:(o + 1) * 512],
                            lhsT=o128_sb[64:64 + NR, :],
                            rhs=s1[64:64 + NR, o * 512:(o + 1) * 512],
                            start=True, stop=True))
    T(c_ummm)
    m_ummm = cT.n

    def c_um(e):
        e.wait_ge(cT.sem, m_ummm)
        e.wait_ge(cV.sem, m_crow)
        e.activation(out=s2[0:1, 0:D], in_=ps_mm[0:1, 0:D], func=AF.Identity,
                     scale=1.0 / (n_cores * T_loc))
    S(c_um)
    m_um = cS.n

    def c_dlr0(e):
        e.wait_ge(cS.sem, m_um)
        e.wait_ge(cT.sem, m_cpre)
        e.wait_ge(cV.sem, m_crow)
        sY.inc(e.dma_start(out=s1[0:4, 0:D], in_=dr_vdlru))
    Y(c_dlr0)
    m_dlru = sY.n

    def c_dlr1(e):
        e.wait_ge(cS.sem, m_um)
        for o in range(NO):
            cT.inc(e.matmul(out=ps_mm[0:4, o * 512:(o + 1) * 512], lhsT=oc_sb[0:1, 0:4],
                            rhs=s2[0:1, o * 512:(o + 1) * 512], start=True, stop=True))
    T(c_dlr1)
    m_dlr1 = cT.n

    def c_dlr2(e):
        e.wait_ge(cT.sem, m_dlr1)
        e.wait_ge(sY.sem, m_dlru)
        e.tensor_tensor(out=s1[0:4, 0:D], in0=ps_mm[0:4, 0:D], in1=s1[0:4, 0:D],
                        op=ALU.mult)
        e.tensor_reduce(out=row_m4[:], in_=s1[0:4, 0:D], axis=AX.X, op=ALU.add)
    V(c_dlr2)
    m_dlr2 = cV.n

    def c_dlr5(e):
        e.wait_ge(cV.sem, m_dlr2)
        sY.inc(e.dma_start(out=s1[0:4, 0:D], in_=dr_vdlrv))
        sY.inc(e.dma_start(out=s2[0:1, 0:D], in_=dr_vd))
    Y(c_dlr5)
    m_dlrv2 = sY.n

    def c_dlr6(e):
        e.wait_ge(sY.sem, m_dlrv2)
        e.tensor_scalar(out=s1[0:4, 0:D], in0=s1[0:4, 0:D], scalar1=row_m4[:],
                        scalar2=0.25, op0=ALU.mult, op1=ALU.mult)
    V(c_dlr6)
    m_dlr6 = cV.n

    def c_dlr7(e):
        e.wait_ge(cV.sem, m_dlr6)
        for o in range(NO):
            cT.inc(e.matmul(out=ps_mm[0:1, o * 512:(o + 1) * 512], lhsT=o128_sb[0:4, :],
                            rhs=s1[0:4, o * 512:(o + 1) * 512], start=True, stop=True))
    T(c_dlr7)
    m_dlr7 = cT.n
    m_dlr7_h[0] = m_dlr7
    G(c_fown)
    m_fown = sG.n

    def c_deffrow(e):
        e.wait_ge(cT.sem, m_dlr7)
        e.wait_ge(sY.sem, m_dlrv2)
        e.tensor_tensor(out=s2[0:1, 0:D], in0=ps_mm[0:1, 0:D], in1=s2[0:1, 0:D],
                        op=ALU.add)
    V(c_deffrow)
    m_deffrow = cV.n

    def c_deffb(e):
        e.wait_ge(cV.sem, m_deffrow)
        for o in range(NO):
            cT.inc(e.matmul(out=ps_mm[:, o * 512:(o + 1) * 512], lhsT=oc_sb[:],
                            rhs=s2[0:1, o * 512:(o + 1) * 512], start=True, stop=True))
    T(c_deffb)
    m_deffb = cT.n

    def c_deffev(e):
        e.wait_ge(cT.sem, m_deffb)
        e.activation(out=deff_b[:, :], in_=ps_mm[:, :], func=AF.Identity)
    S(c_deffev)
    m_deffev = cS.n

    a9_marks = []
    for half in range(nhalf):
        os_ = tuple(range(half * (NO // nhalf), (half + 1) * (NO // nhalf)))
        def c_a9a(e, os_=os_):
            e.wait_ge(cV.sem, tadd_marks[NQ - 1])
            if a9_marks:
                e.wait_ge(cS.sem, a9_marks[-1])
            else:
                e.wait_ge(cS.sem, m_deffev)
            for oi, o in enumerate(os_):
                cT.inc(e.matmul(out=ps_aux[0:NT + 1, oi * 512:oi * 512 + 512],
                                lhsT=iota9_sb[:],
                                rhs=row_lnaa[0:1, o * 512:(o + 1) * 512],
                                start=True, stop=True))
        T(c_a9a)
        m_a9a = cT.n

        def c_a9b(e, os_=os_, m_a9a=m_a9a):
            e.wait_ge(cT.sem, m_a9a)
            e.wait_ge(cV.sem, m_deffrow)
            for oi, o in enumerate(os_):
                e.activation(out=s2[0:NT + 1, o * 512:(o + 1) * 512],
                             in_=ps_aux[0:NT + 1, oi * 512:oi * 512 + 512], func=AF.Exp)
        S(c_a9b)
        a9_marks.append(cS.n)
    m_a9 = a9_marks[-1]

    def c_q9(e):
        e.wait_ge(cS.sem, m_deffev)
        e.wait_ge(cV.sem, m_crow)
        e.wait_ge(sG.sem, m_fown)
        for o in range(NO):
            e.matmul(out=ps_mm[0:NT + 1, o * 512:(o + 1) * 512], lhsT=tri9_sb[:],
                     rhs=s1[0:NT, o * 512:(o + 1) * 512], start=True, stop=False)
            cT.inc(e.matmul(out=ps_mm[0:NT + 1, o * 512:(o + 1) * 512],
                            lhsT=oc_sb[0:1, 0:NT + 1],
                            rhs=xbuf[0:1, o * 512:(o + 1) * 512], start=False, stop=True))
    T(c_q9)
    m_q9 = cT.n

    def c_cfull(e):
        e.wait_ge(cT.sem, m_q9)
        e.wait_ge(cS.sem, m_a9)
        e.tensor_tensor(out=s1[0:NT, :], in0=ps_mm[0:NT, 0:D], in1=s2[0:NT, 0:D],
                        op=ALU.mult)
    V(c_cfull)
    m_cfull = cV.n

    def c_chl(e):
        e.wait_ge(cV.sem, m_cfull)
        e.activation(out=bscr[:, 0:D], in_=s1[0:NT, :], func=AF.Identity)
    S(c_chl)
    m_chl = cS.n

    def c_cll(e):
        e.wait_ge(cS.sem, m_chl)
        e.tensor_tensor(out=bscr[:, D:2 * D], in0=s1[0:NT, :], in1=bscr[:, 0:D],
                        op=ALU.subtract)
    V(c_cll)
    m_cll = cV.n

    def y_wf(e):
        e.wait_ge(cT.sem, mk_gmm[NT - 1])
        e.wait_ge(sG.sem, m_wspill)
        sY2.inc(e.dma_start(out=W16[:, :], in_=dr_wf16))
    Y(y_wf)
    m_wfre = sY2.n

    mk_x1re = [0] * NT
    mk_ugre = [0] * NT
    mk_carry = [0] * NT

    csb = None
    carryP = None

    def alloc_phase_c():
        nonlocal csb, carryP
        sclose("s2")
        sclose("s1")
        csb = sopen("csb", [128, 2 * D], F32)
        carryP = [sopen(f"carryP{p}", [2, D], BF16) for p in range(2)]
    alloc_phase_c()

    def y_carry01(e):
        e.wait_ge(cV.sem, m_cll)
        sCY[0].inc(e.dma_start(out=carryP[0][0:1, :], in_=bscr[0:1, 0:D]))
        sCY[0].inc(e.dma_start(out=carryP[0][1:2, :], in_=bscr[0:1, D:2 * D]))
        mk_carry[0] = sCY[0].n
        if NT > 1:
            sCY[1].inc(e.dma_start(out=carryP[1][0:1, :], in_=bscr[1:2, 0:D]))
            sCY[1].inc(e.dma_start(out=carryP[1][1:2, :], in_=bscr[1:2, D:2 * D]))
            mk_carry[1] = sCY[1].n
    Y(y_carry01)

    def y_l01(e):
        e.wait_ge(cV.sem, mk_ug[NT - 1])
        e.wait_ge(sPg[0].sem, sPg[0].n)
        e.wait_ge(sPg[1].sem, sPg[1].n)
        e.wait_ge(cT.sem, m_q9)
        sX[0].inc(e.dma_start(out=xbuf[:, 0:D], in_=dr_x1[0:128, :]))
        mk_x1re[0] = sX[0].n
        sU[0].inc(e.dma_start(out=ubuf[:, 0:D], in_=dr_ug[0:128, :]))
        mk_ugre[0] = sU[0].n
        sX[1].inc(e.dma_start(out=xbuf[:, D:2 * D], in_=dr_x1[128:256, :]))
        mk_x1re[1] = sX[1].n
        sU[1].inc(e.dma_start(out=ubuf[:, D:2 * D], in_=dr_ug[128:256, :]))
        mk_ugre[1] = sU[1].n
    Y(y_l01)

    mk_cs = [0] * NT
    mk_x2 = [0] * NT
    mk_rb3 = [0] * NT
    mk_h3act = [0] * NT
    mk_h3T = [0] * NT
    mk_h3Tev = [0] * NT
    mk_mmC = [0] * NT
    mk_mmCc = [[0] * NO for _ in range(NT)]
    mk_out = [0] * NT
    mk_outc = [[0] * NO for _ in range(NT)]
    mk_outdma = [0] * NT
    yz_marks = [0] * NQ
    ysh_marks = [0] * NQ
    ysl_marks = [0] * NQ
    ysmm_marks = [0] * NQ
    ycs_marks = [0] * NQ

    def y_pre_chunk(j, o):
        s = (j % 2) * D
        if True:
            q = j * NO + o
            zs = (q % 2) * 512

            def y_z(e, j=j, s=s, o=o, q=q, zs=zs):
                e.wait_ge(sU[j % 3].sem, mk_ugre[j])
                if q >= 2:
                    e.wait_ge(cG.sem, ysl_marks[q - 2])
                else:
                    e.wait_ge(cV.sem, zl_marks[NQ - 1])
                e.tensor_tensor(out=zbuf32[:, zs:zs + 512],
                                in0=ubuf[:, s + o * 512:s + o * 512 + 512],
                                in1=BRt[:, o * 512:o * 512 + 512], op=ALU.mult)
                if j == 0:
                    e.tensor_tensor(out=zbuf32[0:1, zs:zs + 512],
                                    in0=ubuf[0:1, s + o * 512:s + o * 512 + 512],
                                    in1=row_br0[0:1, o * 512:o * 512 + 512],
                                    op=ALU.mult)
            V(y_z)
            m_z32 = cV.n
            yz_marks[q] = m_z32

            def y_zh(e, q=q, zs=zs):
                if q >= 2:
                    e.wait_ge(cT.sem, ysmm_marks[q - 2])
                else:
                    e.wait_ge(cT.sem, zmm_marks[NQ - 1])
                e.tensor_copy(out=zbufh[:, zs:zs + 512], in_=zbuf32[:, zs:zs + 512])
            V(y_zh)
            ysh_marks[q] = cV.n

            def y_zl(e, q=q, zs=zs):
                e.wait_ge(cV.sem, ysh_marks[q])
                e.tensor_tensor(out=zbufl[:, zs:zs + 512], in0=zbuf32[:, zs:zs + 512],
                                in1=zbufh[:, zs:zs + 512], op=ALU.subtract)
            GC(y_zl)
            ysl_marks[q] = cG.n

            def y_smm(e, j=j, o=o, q=q, zs=zs):
                e.wait_ge(cG.sem, ysl_marks[q])
                e.wait_ge(sCY[j % 2].sem, mk_carry[j])
                if q >= 2:
                    e.wait_ge(cV.sem, ycs_marks[q - 2])
                else:
                    e.wait_ge(cS.sem, m_a9)
                pa = ps_aux[:, zs:zs + 512]
                e.matmul(out=pa, lhsT=tri_sb[:], rhs=zbufh[:, zs:zs + 512],
                         start=True, stop=False)
                e.matmul(out=pa, lhsT=tri_sb[:], rhs=zbufl[:, zs:zs + 512],
                         start=False, stop=False)
                cT.inc(e.matmul(out=pa, lhsT=ones2_sb[:],
                                rhs=carryP[j % 2][:, o * 512:(o + 1) * 512],
                                start=False, stop=True))
            T(y_smm)
            ysmm_marks[q] = cT.n

            def y_cs(e, j=j, s=s, o=o, q=q, zs=zs):
                e.wait_ge(cT.sem, ysmm_marks[q])
                if j >= 2 and o == 0:
                    e.wait_ge(cS.sem, mk_h3act[j - 2])
                e.tensor_tensor(out=csb[:, s + o * 512:s + o * 512 + 512],
                                in0=Pt[:, o * 512:(o + 1) * 512],
                                in1=ps_aux[:, zs:zs + 512], op=ALU.mult)
                if o == 0:
                    e.wait_ge(sX[j % 2].sem, mk_x1re[j])
                e.tensor_tensor(out=csb[:, s + o * 512:s + o * 512 + 512],
                                in0=csb[:, s + o * 512:s + o * 512 + 512],
                                in1=xbuf[:, s + o * 512:s + o * 512 + 512],
                                op=ALU.add)
            V(y_cs)
            ycs_marks[q] = cV.n

    def y_pre_tail(j):
        s = (j % 2) * D
        mk_cs[j] = ycs_marks[j * NO + NO - 1]

        if j + 2 < NT:
            def y_cpre(e, j=j):
                e.wait_ge(cT.sem, ysmm_marks[j * NO + NO - 1])
                sCY[j % 2].inc(e.dma_start(out=carryP[j % 2][0:1, :],
                                           in_=bscr[j + 2:j + 3, 0:D]))
                sCY[j % 2].inc(e.dma_start(out=carryP[j % 2][1:2, :],
                                           in_=bscr[j + 2:j + 3, D:2 * D]))
                mk_carry[j + 2] = sCY[j % 2].n
            Y(y_cpre)

        def y_dg(e, j=j, s=s):
            e.wait_ge(cV.sem, yz_marks[j * NO + NO - 1])
            e.wait_ge(cS.sem, m_deffev)
            e.tensor_tensor(out=ubuf[:, s:s + D], in0=ubuf[:, s:s + D],
                            in1=deff_b[:, :], op=ALU.mult)
        GC(y_dg)
        mk_dg = cG.n

        def y_x2(e, j=j, s=s):
            e.wait_ge(cV.sem, mk_cs[j])
            e.wait_ge(cG.sem, mk_dg)
            e.tensor_tensor(out=csb[:, s:s + D], in0=csb[:, s:s + D],
                            in1=ubuf[:, s:s + D], op=ALU.add)
        V(y_x2)
        mk_x2[j] = cV.n

        def y_stats(e, j=j, s=s):
            if j >= 2:
                e.wait_ge(cG.sem, mk_rb3[j - 2])
            else:
                e.wait_ge(cG.sem, mk_rb2[NT - 1])
            so = (j % 2) * 6 * NO
            for c in range(NO):
                e.bn_stats(out=statb[:, so + 6 * c:so + 6 * c + 6],
                           in_=csb[:, s + 512 * c:s + 512 * (c + 1)])
            e.bn_aggr(out=mvb[:, (j % 2) * 2:(j % 2) * 2 + 2],
                      in_=statb[:, so:so + 6 * NO])
        V(y_stats)
        m_st3 = cV.n

        def y_rb(e, j=j):
            e.wait_ge(cV.sem, m_st3)
            if j >= 2:
                e.wait_ge(cS.sem, mk_h3act[j - 2])
            else:
                e.wait_ge(cV.sem, mk_u32[NT - 1])
            c2 = (j % 2)
            e.tensor_scalar(out=rcol[:, c2:c2 + 1], in0=mvb[:, 2 * c2 + 1:2 * c2 + 2],
                            scalar1=1e-5, scalar2=None, op0=ALU.add, op1=ALU.bypass)
            e.tensor_tensor(out=rcol[:, c2:c2 + 1], in0=rcol[:, c2:c2 + 1],
                            in1=mhalf[:, :], op=ALU.pow)
        GC(y_rb)
        mk_rb3[j] = cG.n

        def y_b(e, j=j):
            e.wait_ge(cG.sem, mk_rb3[j])
            c2 = (j % 2)
            e.tensor_scalar(out=bcol[:, c2:c2 + 1], in0=mvb[:, 2 * c2:2 * c2 + 1],
                            scalar1=rcol[:, c2:c2 + 1], scalar2=-1.0,
                            op0=ALU.mult, op1=ALU.mult)
        V(y_b)
        m_b3v = cV.n

        def y_h3(e, j=j, s=s):
            e.wait_ge(cV.sem, m_b3v)
            if j == 0:
                e.wait_ge(cT.sem, mk_uT[NT - 1])
            if j >= 2:
                e.wait_ge(cT.sem, mk_h3T[j - 2])
            c2 = (j % 2)
            e.activation(out=h16b[:, s:s + D], in_=csb[:, s:s + D],
                         func=AF.Identity, bias=bcol[:, c2:c2 + 1],
                         scale=rcol[:, c2:c2 + 1])

        evac_ct = (mk_gmm[NT - 1] if j < 3 else mk_mmCc[j - 3][NO - 1])
        mk_h3T[j], mk_h3Tev[j] = emit_transpose_pass(
            j, (j % 2) * D, (j % 3) * D, evac_ct_wait=evac_ct, s_prefix=y_h3, n_prefix=1)
        mk_h3act[j] = mk_h3Tev[j]

    def y_mmo_f(j, o):
        s = (j % 3) * D
        if True:
            def y_mmo(e, j=j, s=s, o=o):
                if o == 0:
                    e.wait_ge(cS.sem, mk_h3Tev[j])
                    e.wait_ge(sY2.sem, m_wfre)
                if j == 0:
                    e.wait_ge(cV.sem, m_cfull)
                    e.wait_ge(cS.sem, m_deffev)
                else:
                    e.wait_ge(cV.sem, mk_outc[j - 1][o])
                for i in range(NCH):
                    e.matmul(out=ps_mm[:, o * 512:(o + 1) * 512],
                             lhsT=hTb[:, s + 128 * i:s + 128 * (i + 1)],
                             rhs=W16[:, i * D + o * 512:i * D + o * 512 + 512],
                             start=(i == 0), stop=False)
                cT.inc(e.matmul(out=ps_mm[:, o * 512:(o + 1) * 512], lhsT=ones2_sb[:],
                                rhs=biasF[:, o * 512:(o + 1) * 512], start=False, stop=True))
            T(y_mmo)
            mk_mmCc[j][o] = cT.n

    def y_outo_f(j, o):
        s = (j % 2) * D

        def y_outo(e, j=j, s=s, o=o):
            e.wait_ge(cT.sem, mk_mmCc[j][o])
            if j >= 2 and o == 0:
                e.wait_ge(sST[j % 2].sem, mk_outdma[j - 2])
            e.tensor_tensor(out=xbuf[:, s + o * 512:s + o * 512 + 512],
                            in0=csb[:, s + o * 512:s + o * 512 + 512],
                            in1=ps_mm[:, o * 512:(o + 1) * 512], op=ALU.add)
        V(y_outo)
        mk_outc[j][o] = cV.n

    def y_post(j):
        s = (j % 2) * D
        mk_out[j] = mk_outc[j][NO - 1]

        def y_store(e, j=j, s=s):
            e.wait_ge(cV.sem, mk_out[j])
            sST[j % 2].inc(e.dma_start(out=dr_y[128 * j:128 * (j + 1), :],
                                       in_=xbuf[:, s:s + D]))
        Y(y_store)
        mk_outdma[j] = sST[j % 2].n

        if j + 2 < NT:
            def y_ldpre(e, j=j, s=s):
                e.wait_ge(sST[j % 2].sem, mk_outdma[j])
                sX[j % 2].inc(e.dma_start(out=xbuf[:, s:s + D],
                                          in_=dr_x1[128 * (j + 2):128 * (j + 3), :]))
                mk_x1re[j + 2] = sX[j % 2].n
                e.wait_ge(cV.sem, mk_x2[j])
                sU[(j + 2) % 3].inc(e.dma_start(out=ubuf[:, s:s + D],
                                                in_=dr_ug[128 * (j + 2):128 * (j + 3), :]))
                mk_ugre[j + 2] = sU[(j + 2) % 3].n
            Y(y_ldpre)

    for o in range(NO):
        y_pre_chunk(0, o)
    y_pre_tail(0)
    for j in range(NT):
        for o in range(NO):
            y_mmo_f(j, o)
            y_outo_f(j, o)
            if j + 1 < NT:
                y_pre_chunk(j + 1, o)
        mk_mmC[j] = mk_mmCc[j][NO - 1]
        if j + 1 < NT:
            y_pre_tail(j + 1)
        y_post(j)

    def fin(e):
        e.wait_ge(sST[0].sem, sST[0].n)
        e.wait_ge(sST[1].sem, sST[1].n)
    G(fin)

    for nm in reversed(list(_open)):
        sclose(nm)
    es.close()
    return nc


def _consts(n_cores, T_loc, D, core_i):
    NT = T_loc // 128
    NR = n_cores * NT
    f32 = np.float32
    c = {}
    c["tri16"] = np.triu(np.ones((128, 128), dtype=f32)).astype(BF)
    c["id16"] = np.eye(128, dtype=f32).astype(BF)
    c["ones2_16"] = np.ones((2, 128), dtype=f32).astype(BF)
    embZ = np.zeros((128, 16 * NT), dtype=f32)
    embU = np.zeros((128, 16 * NT), dtype=f32)
    for j in range(NT):
        embZ[:, 16 * j + j] = 1.0
        embU[:, 16 * j + 8 + j] = 1.0
    c["embZ16"] = embZ.astype(BF)
    c["embU32"] = embU
    t9 = np.zeros((NT, NT + 1), dtype=f32)
    for k in range(NT):
        for jj in range(NT + 1):
            if k < jj:
                t9[k, jj] = 1.0
    c["tri9"] = t9
    c["onescol"] = np.ones((1, 128), dtype=f32)
    c["ones128"] = np.ones((128, 1), dtype=f32)
    c["iotaP"] = np.arange(1, 129, dtype=f32).reshape(1, 128)
    im16 = np.zeros((1, 16), dtype=f32)
    im16[0, :8] = -128.0 * np.arange(8)
    c["iotaM16"] = im16
    c["iota9"] = (128.0 * np.arange(NT + 1, dtype=f32)).reshape(1, NT + 1)
    c["iota64"] = np.array([-float(T_loc) * (r // NT) for r in range(NR)],
                           dtype=f32).reshape(1, NR)
    sel = np.zeros((NR, 1), dtype=f32)
    for r in range(NR):
        if (r // NT) < core_i:
            sel[r, 0] = 1.0
    c["sel64"] = sel
    pcs = np.zeros((1, 4), dtype=f32)
    pcs[0, 0] = float(T_loc) * core_i
    pcs[0, 1] = 1e-6 if core_i == 0 else -1e30
    c["pcs"] = pcs
    return c


def _run(x_full, wfT, wgT, vd, n_cores, T_loc, D):
    nc = build_core_program(n_cores, T_loc, D)
    in_maps = []
    for i in range(n_cores):
        m = dict(_consts(n_cores, T_loc, D, i))
        m["x"] = np.ascontiguousarray(x_full[i * T_loc:(i + 1) * T_loc])
        m["wfT"] = wfT
        m["wgT"] = wgT
        m.update(vd)
        in_maps.append(m)
    res = run_bass_kernel_spmd(nc, in_maps, core_ids=list(range(n_cores)))
    return np.concatenate([np.asarray(r["y"]) for r in res.results], axis=0)


def kernel(L0_attn_u, L0_attn_a, L0_attn_b, L0_attn_c, L0_attn_d,
           L0_attn_gate_w, L0_attn_gate_b, L0_attn_d_lr_u, L0_attn_d_lr_v,
           L0_ffn_weight, L0_ffn_bias):
    n_cores = 8
    B, Tn, D = L0_attn_u.shape
    T_loc = Tn // n_cores
    x = np.ascontiguousarray(np.asarray(L0_attn_u, dtype=np.float32).reshape(Tn, D))
    wfT = np.ascontiguousarray(np.asarray(L0_ffn_weight, dtype=np.float32).T)
    wgT = np.ascontiguousarray(np.asarray(L0_attn_gate_w, dtype=np.float32).T)
    vd = {
        "va": np.asarray(L0_attn_a, dtype=np.float32).reshape(1, D),
        "vb": np.asarray(L0_attn_b, dtype=np.float32).reshape(1, D),
        "vc": np.asarray(L0_attn_c, dtype=np.float32).reshape(1, D),
        "vd": np.asarray(L0_attn_d, dtype=np.float32).reshape(1, D),
        "vgb": np.asarray(L0_attn_gate_b, dtype=np.float32).reshape(1, D),
        "vfb": np.asarray(L0_ffn_bias, dtype=np.float32).reshape(1, D),
        "vdlrv": np.ascontiguousarray(np.asarray(L0_attn_d_lr_v, dtype=np.float32)),
        "vdlru": np.ascontiguousarray(np.asarray(L0_attn_d_lr_u, dtype=np.float32).T),
    }
    out = _run(x, wfT, wgT, vd, n_cores, T_loc, D)
    return out.reshape(B, Tn, D).astype(np.float32)


# version 11
# speedup vs baseline: 1.0743x; 1.0253x over previous
import contextlib
import numpy as np
import ml_dtypes

import concourse.bass as bass
import concourse.mybir as mybir
from concourse.bass_utils import run_bass_kernel_spmd

F32 = mybir.dt.float32
BF16 = mybir.dt.bfloat16
AF = mybir.ActivationFunctionType
ALU = mybir.AluOpType
AX = mybir.AxisListType
BF = ml_dtypes.bfloat16


class Ctr:
    def __init__(self, sem, step=1):
        self.sem = sem
        self.n = 0
        self.step = step

    def inc(self, instr, k=None):
        k = self.step if k is None else k
        instr.then_inc(self.sem, k)
        self.n += k
        return self.n


def build_core_program(n_cores, T_loc, D):
    NT = T_loc // 128
    NCH = D // 128
    NO = D // 512
    NQ = NT * NO
    NR = n_cores * NT

    nc = bass.Bass("TRN2", target_bir_lowering=False)

    dr_x = nc.dram_tensor("x", [T_loc, D], F32, kind="ExternalInput").ap()
    dr_wfT = nc.dram_tensor("wfT", [D, D], F32, kind="ExternalInput").ap()
    dr_wfW = nc.dram_tensor("wfW", [max(1, D // n_cores), D], F32, kind="ExternalInput").ap()
    dr_wgT = nc.dram_tensor("wgT", [D, D], F32, kind="ExternalInput").ap()
    dr_va = nc.dram_tensor("va", [1, D], F32, kind="ExternalInput").ap()
    dr_vb = nc.dram_tensor("vb", [1, D], F32, kind="ExternalInput").ap()
    dr_vc = nc.dram_tensor("vc", [1, D], F32, kind="ExternalInput").ap()
    dr_vd = nc.dram_tensor("vd", [1, D], F32, kind="ExternalInput").ap()
    dr_vgb = nc.dram_tensor("vgb", [1, D], F32, kind="ExternalInput").ap()
    dr_vfb = nc.dram_tensor("vfb", [1, D], F32, kind="ExternalInput").ap()
    dr_vdlrv = nc.dram_tensor("vdlrv", [4, D], F32, kind="ExternalInput").ap()
    dr_vdlru = nc.dram_tensor("vdlru", [4, D], F32, kind="ExternalInput").ap()
    dr_tri = nc.dram_tensor("tri16", [128, 128], BF16, kind="ExternalInput").ap()
    dr_id = nc.dram_tensor("id16", [128, 128], BF16, kind="ExternalInput").ap()
    dr_ones2 = nc.dram_tensor("ones2_16", [2, 128], BF16, kind="ExternalInput").ap()
    dr_embZ = nc.dram_tensor("embZ16", [128, 16 * NT], BF16, kind="ExternalInput").ap()
    dr_embU = nc.dram_tensor("embU32", [128, 16 * NT], F32, kind="ExternalInput").ap()
    dr_tri9 = nc.dram_tensor("tri9", [NT, NT + 1], F32, kind="ExternalInput").ap()
    dr_onescol = nc.dram_tensor("onescol", [1, 128], F32, kind="ExternalInput").ap()
    dr_ones128 = nc.dram_tensor("ones128", [128, 1], F32, kind="ExternalInput").ap()
    dr_iotaP = nc.dram_tensor("iotaP", [1, 128], F32, kind="ExternalInput").ap()
    dr_iotaM16 = nc.dram_tensor("iotaM16", [1, 16], F32, kind="ExternalInput").ap()
    dr_iota9 = nc.dram_tensor("iota9", [1, NT + 1], F32, kind="ExternalInput").ap()
    dr_iota64 = nc.dram_tensor("iota64", [1, NR], F32, kind="ExternalInput").ap()
    dr_sel64 = nc.dram_tensor("sel64", [NR, 1], F32, kind="ExternalInput").ap()
    dr_pcs = nc.dram_tensor("pcs", [1, 4], F32, kind="ExternalInput").ap()

    dr_x1 = nc.dram_tensor("x1_sp", [T_loc, D], F32).ap()
    dr_u32 = nc.dram_tensor("u32_sp", [T_loc, D], F32).ap()
    dr_ug = nc.dram_tensor("ug_sp", [T_loc, D], F32).ap()
    dr_wf16 = nc.dram_tensor("wf16_sp", [128, NCH * D], BF16).ap()
    dr_cc2_in = nc.dram_tensor("cc2_in", [1, 16], F32).ap()
    dr_cc2_out = nc.dram_tensor("cc2_out", [1, 16], F32).ap()
    dr_cc_in = nc.dram_tensor("cc_in", [16, D], F32).ap()
    dr_cc_out = nc.dram_tensor("cc_out", [n_cores * 16, D], F32).ap()
    dr_y = nc.dram_tensor("y", [T_loc, D], F32, kind="ExternalOutput").ap()

    es = contextlib.ExitStack()
    sbt = lambda n, s, d: es.enter_context(nc.sbuf_tensor(n, s, d))
    pst = lambda n, s, d: es.enter_context(nc.psum_tensor(n, s, d))

    _open = {}

    def sopen(name, shape, dt):
        cm = nc.sbuf_tensor(name, shape, dt)
        t = cm.__enter__()
        _open[name] = cm
        return t

    def sclose(name):
        _open.pop(name).__exit__(None, None, None)

    W16 = sbt("W16", [128, NCH * D], BF16)
    xbuf = sbt("xbuf", [128, 3 * D], F32)
    ubuf = sbt("ubuf", [128, 3 * D], F32)
    h16b = sbt("h16b", [128, 2 * D], BF16)
    hTb = sbt("hTb", [128, 3 * D], BF16)
    zbuf32 = sbt("zbuf32", [128, 2 * 512], F32)
    zbufh = sbt("zbufh", [128, 2 * 512], BF16)
    zbufl = sbt("zbufl", [128, 2 * 512], BF16)
    Pt = sbt("Pt", [128, D], BF16)
    BRt = sbt("BRt", [128, D], BF16)
    row_lnaa = sbt("row_lnaa", [1, D], F32)
    row_br0 = sbt("row_br0", [1, D], BF16)
    biasF = sbt("biasF", [2, D], BF16)
    biasG = sbt("biasG", [2, D], BF16)
    statb = sbt("statb", [128, 3 * 6 * NO], F32)
    mvb = sbt("mvb", [128, 3 * 2], F32)
    rcol = sbt("rcol", [128, 3], F32)
    bcol = sbt("bcol", [128, 3], F32)
    abscol = sbt("abscol", [128, NCH + 1], F32)
    scol = sbt("scol", [128, 4], F32)
    row_m4 = sbt("row_m4", [4, 1], F32)
    mhalf = sbt("mhalf", [128, 1], F32)
    cc2row = sbt("cc2row", [1, 16], F32)
    tri_sb = sbt("tri_sb", [128, 128], BF16)
    id_sb = sbt("id_sb", [128, 128], BF16)
    ones2_sb = sbt("ones2_sb", [2, 128], BF16)
    embZ_sb = sbt("embZ_sb", [128, 16 * NT], BF16)
    embU_sb = sbt("embU_sb", [128, 16 * NT], F32)
    tri9_sb = sbt("tri9_sb", [NT, NT + 1], F32)
    oc_sb = sbt("oc_sb", [1, 128], F32)
    o128_sb = sbt("o128_sb", [128, 1], F32)
    iotaP_sb = sbt("iotaP_sb", [1, 128], F32)
    iotaM16_sb = sbt("iotaM16_sb", [1, 16], F32)
    iota9_sb = sbt("iota9_sb", [1, NT + 1], F32)
    iota64_sb = sbt("iota64_sb", [1, NR], F32)
    sel64_sb = sbt("sel64_sb", [NR, 1], F32)
    pcs_sb = sbt("pcs_sb", [1, 4], F32)

    ps_mm = pst("ps_mm", [128, D], F32)
    ps_aux = pst("ps_aux", [128, 1024], F32)
    ps_tr = pst("ps_tr", [128, 512], BF16)
    ps_tr2 = pst("ps_tr2", [128, 512], BF16)

    sCst = Ctr(es.enter_context(nc.semaphore("sCst")), 16)
    sW = [Ctr(es.enter_context(nc.semaphore(f"sW{p}")), 16) for p in range(2)]
    sX = [Ctr(es.enter_context(nc.semaphore(f"sX{p}")), 16) for p in range(3)]
    sU = [Ctr(es.enter_context(nc.semaphore(f"sU{p}")), 16) for p in range(3)]
    sPx = [Ctr(es.enter_context(nc.semaphore(f"sPx{p}")), 16) for p in range(2)]
    sPg = [Ctr(es.enter_context(nc.semaphore(f"sPg{p}")), 16) for p in range(2)]
    sST = [Ctr(es.enter_context(nc.semaphore(f"sST{p}")), 16) for p in range(2)]
    sCY = [Ctr(es.enter_context(nc.semaphore(f"sCY{p}")), 16) for p in range(2)]
    sG = Ctr(es.enter_context(nc.semaphore("sG")), 16)
    sY = Ctr(es.enter_context(nc.semaphore("sY")), 16)
    sY2 = Ctr(es.enter_context(nc.semaphore("sY2")), 16)
    cT = Ctr(es.enter_context(nc.semaphore("cT")))
    cG = Ctr(es.enter_context(nc.semaphore("cG")))
    cV = Ctr(es.enter_context(nc.semaphore("cV")))
    cS = Ctr(es.enter_context(nc.semaphore("cS")))
    ccC = Ctr(es.enter_context(nc.semaphore("ccsem")))

    blk = es.enter_context(nc.Block())
    T, G, Y = blk.tensor, blk.gpsimd, blk.sync

    class _Chain:
        def __init__(self, eng, ctr):
            self._e = eng
            self._c = ctr

        def wait_ge(self, sem, n):
            self._e.wait_ge(sem, n)

        def __getattr__(self, name):
            fn = getattr(self._e, name)

            def call(*a, **k):
                self._e.wait_ge(self._c.sem, self._c.n)
                r = fn(*a, **k)
                self._c.inc(r)
                return r
            return call

    def V(f):
        blk.vector(lambda e: f(_Chain(e, cV)))

    def S(f):
        blk.scalar(lambda e: f(_Chain(e, cS)))

    def GC(f):
        blk.gpsimd(lambda e: f(_Chain(e, cG)))

    NG = max(1, NCH // 4)

    def emit_transpose_pass(j, s_src, s_dst, evac_ct_wait=None,
                            s_prefix=None, n_prefix=0, s_suffix=None, n_suffix=0):
        sbase = cS.n
        tbase = cT.n
        m_src = sbase + n_prefix
        ev_mark = lambda g: sbase + n_prefix + g + 1
        tr_mark = lambda g: tbase + min((g + 1) * 4, NCH)

        def tr_all(e):
            e.wait_ge(sCst.sem, m_consts)
            for g in range(NG):
                e.wait_ge(cS.sem, m_src if g < 2 else ev_mark(g - 2))
                pt = ps_tr if (g % 2) == 0 else ps_tr2
                for c4 in range(4):
                    c = g * 4 + c4
                    if c >= NCH:
                        break
                    cT.inc(e.transpose(out=pt[:, c4 * 128:(c4 + 1) * 128],
                                       in_=h16b[:, s_src + 128 * c:s_src + 128 * (c + 1)],
                                       identity=id_sb[:]))

        def merged(e):
            if s_prefix is not None:
                s_prefix(e)
            assert cS.n == m_src, f"prefix op count mismatch {cS.n} != {m_src}"
            if evac_ct_wait is not None:
                e.wait_ge(cT.sem, evac_ct_wait)
            for g in range(NG):
                e.wait_ge(cT.sem, tr_mark(g))
                pt = ps_tr if (g % 2) == 0 else ps_tr2
                n = min(4, NCH - g * 4) * 128
                e.activation(out=hTb[:, s_dst + g * 512:s_dst + g * 512 + n],
                             in_=pt[:, 0:n], func=AF.Identity)
            if s_suffix is not None:
                s_suffix(e)

        T(tr_all)
        assert cT.n == tbase + NCH
        S(merged)
        assert cS.n == sbase + n_prefix + NG + n_suffix, \
            f"scalar body count mismatch {cS.n} vs {sbase}+{n_prefix}+{NG}+{n_suffix}"
        return cT.n, sbase + n_prefix + NG

    ugob = sopen("ugob", [128, 2 * D], F32)
    wtmp = sopen("wtmp", [128, 2 * D], F32)
    r0 = ugob[0:1, 0:D]
    r1 = ugob[0:1, D:2 * D]
    rb16 = h16b[0:1, 0:D]

    def p_loads(e):
        for src, dst in [
            (dr_tri, tri_sb[:]), (dr_id, id_sb[:]), (dr_ones2, ones2_sb[:]),
            (dr_embZ, embZ_sb[:]), (dr_embU, embU_sb[:]), (dr_tri9, tri9_sb[:]),
            (dr_onescol, oc_sb[:]), (dr_ones128, o128_sb[:]),
            (dr_iotaP, iotaP_sb[:]),
            (dr_iotaM16, iotaM16_sb[:]), (dr_iota9, iota9_sb[:]),
            (dr_iota64, iota64_sb[:]), (dr_sel64, sel64_sb[:]), (dr_pcs, pcs_sb[:]),
            (dr_va, r0),
        ]:
            sCst.inc(e.dma_start(out=dst, in_=src))
    Y(p_loads)
    m_consts = sCst.n

    x_marks = [0] * NT

    def p_x01(e):
        sX[0].inc(e.dma_start(out=xbuf[:, 0:D], in_=dr_x[0:128, :]))
        x_marks[0] = sX[0].n
        sX[1].inc(e.dma_start(out=xbuf[:, D:2 * D], in_=dr_x[128:256, :]))
        x_marks[1] = sX[1].n
    Y(p_x01)

    def p_mh(e):
        e.memset(mhalf[:, :], -0.5)
    V(p_mh)
    m_mhalf = cV.n

    mk_x1_tt = [0] * NT
    mk_u32 = [0] * NT
    mk_rb = [0] * NT
    mk_rb2 = [0] * NT
    mk_spill = [0] * NT
    mk_hT = [0] * NT
    mk_trevac = [0] * NT
    mk_mmA = [0] * NT
    mk_mmAc = [[0] * NO for _ in range(NT)]
    mk_x1c = [[0] * NO for _ in range(NT)]
    mk_bcol = [0] * NT

    def a_pre(j):
        s = (j % 3) * D
        c3 = j % 3

        def a_stats(e, j=j, s=s, c3=c3):
            e.wait_ge(sX[j % 3].sem, x_marks[j])
            if j >= 3:
                e.wait_ge(cG.sem, mk_rb2[j - 3])
            so = c3 * 6 * NO
            for c in range(NO):
                e.bn_stats(out=statb[:, so + 6 * c:so + 6 * c + 6],
                           in_=xbuf[:, s + 512 * c:s + 512 * (c + 1)])
            e.bn_aggr(out=mvb[:, c3 * 2:c3 * 2 + 2],
                      in_=statb[:, so:so + 6 * NO])
        V(a_stats)
        m_st = cV.n

        def a_rb(e, j=j, c3=c3):
            e.wait_ge(cV.sem, m_st)
            e.wait_ge(cV.sem, m_mhalf)
            if j >= 3:
                e.wait_ge(cV.sem, mk_u32[j - 3])
            e.tensor_scalar(out=rcol[:, c3:c3 + 1], in0=mvb[:, 2 * c3 + 1:2 * c3 + 2],
                            scalar1=1e-5, scalar2=None, op0=ALU.add, op1=ALU.bypass)
            e.tensor_tensor(out=rcol[:, c3:c3 + 1], in0=rcol[:, c3:c3 + 1],
                            in1=mhalf[:, :], op=ALU.pow)
        GC(a_rb)
        mk_rb[j] = cG.n

        def a_b(e, j=j, c3=c3):
            e.wait_ge(cG.sem, mk_rb[j])
            e.tensor_scalar(out=bcol[:, c3:c3 + 1], in0=mvb[:, 2 * c3:2 * c3 + 1],
                            scalar1=rcol[:, c3:c3 + 1], scalar2=-1.0,
                            op0=ALU.mult, op1=ALU.mult)
        V(a_b)
        mk_bcol[j] = cV.n

    def a_pre_t(j):
        s = (j % 3) * D
        c3 = j % 3

        def h1_prefix(e, j=j, s=s, c3=c3):
            e.wait_ge(cV.sem, mk_bcol[j])
            if j >= 2:
                e.wait_ge(cT.sem, mk_hT[j - 2])
            e.activation(out=h16b[:, (j % 2) * D:(j % 2) * D + D],
                         in_=xbuf[:, s:s + D],
                         func=AF.Identity, bias=bcol[:, c3:c3 + 1],
                         scale=rcol[:, c3:c3 + 1])

        evac_ct = mk_mmAc[j - 3][NO - 1] if j >= 3 else None
        mk_hT[j], mk_trevac[j] = emit_transpose_pass(
            j, (j % 2) * D, (j % 3) * D, evac_ct_wait=evac_ct,
            s_prefix=h1_prefix, n_prefix=1)

    def a_mm(j):
        s2 = (j % 3) * D
        for o in range(NO):
            def a_mmo(e, j=j, s2=s2, o=o):
                if o == 0:
                    e.wait_ge(cS.sem, mk_trevac[j])
                    e.wait_ge(cV.sem, tern_marks[NCH - 1])
                    e.wait_ge(cV.sem, m_psmm_free_rows)
                    e.wait_ge(cS.sem, m_P)
                    e.wait_ge(sG.sem, m_fb5)
                if j >= 1:
                    e.wait_ge(cV.sem, mk_x1c[j - 1][o])
                for i in range(NCH):
                    e.matmul(out=ps_mm[:, o * 512:(o + 1) * 512],
                             lhsT=hTb[:, s2 + 128 * i:s2 + 128 * (i + 1)],
                             rhs=W16[:, i * D + o * 512:i * D + o * 512 + 512],
                             start=(i == 0), stop=False)
                cT.inc(e.matmul(out=ps_mm[:, o * 512:(o + 1) * 512], lhsT=ones2_sb[:],
                                rhs=biasF[:, o * 512:(o + 1) * 512], start=False, stop=True))
            T(a_mmo)
            mk_mmAc[j][o] = cT.n
        mk_mmA[j] = mk_mmAc[j][NO - 1]

    def a_post(j):
        s = (j % 3) * D
        c3 = j % 3
        for o in range(NO):
            def a_x1o(e, j=j, s=s, o=o):
                e.wait_ge(cT.sem, mk_mmAc[j][o])
                e.tensor_tensor(out=xbuf[:, s + o * 512:s + o * 512 + 512],
                                in0=xbuf[:, s + o * 512:s + o * 512 + 512],
                                in1=ps_mm[:, o * 512:(o + 1) * 512], op=ALU.add)
            V(a_x1o)
            mk_x1c[j][o] = cV.n
        mk_x1_tt[j] = mk_x1c[j][NO - 1]

        def a_stats2(e, j=j, s=s, c3=c3):
            e.wait_ge(cG.sem, mk_rb[j])
            so = c3 * 6 * NO
            for c in range(NO):
                e.bn_stats(out=statb[:, so + 6 * c:so + 6 * c + 6],
                           in_=xbuf[:, s + 512 * c:s + 512 * (c + 1)])
            e.bn_aggr(out=mvb[:, c3 * 2:c3 * 2 + 2],
                      in_=statb[:, so:so + 6 * NO])
        V(a_stats2)
        m_st2 = cV.n

        def a_rb2(e, j=j, c3=c3):
            e.wait_ge(cV.sem, m_st2)
            e.wait_ge(cS.sem, mk_h1_next[0])
            e.tensor_scalar(out=rcol[:, c3:c3 + 1], in0=mvb[:, 2 * c3 + 1:2 * c3 + 2],
                            scalar1=1e-5, scalar2=None, op0=ALU.add, op1=ALU.bypass)
            e.tensor_tensor(out=rcol[:, c3:c3 + 1], in0=rcol[:, c3:c3 + 1],
                            in1=mhalf[:, :], op=ALU.pow)
        GC(a_rb2)
        mk_rb2[j] = cG.n

        def a_b2(e, j=j, c3=c3):
            e.wait_ge(cG.sem, mk_rb2[j])
            e.tensor_scalar(out=bcol[:, c3:c3 + 1], in0=mvb[:, 2 * c3:2 * c3 + 1],
                            scalar1=rcol[:, c3:c3 + 1], scalar2=-1.0,
                            op0=ALU.mult, op1=ALU.mult)
        V(a_b2)
        m_bb2 = cV.n

        def a_u32(e, j=j, s=s, c3=c3):
            if j >= 2:
                e.wait_ge(sPx[j % 2].sem, mk_spill[j - 2])
            e.tensor_scalar(out=ubuf[:, (j % 2) * D:(j % 2) * D + D],
                            in0=xbuf[:, s:s + D],
                            scalar1=rcol[:, c3:c3 + 1], scalar2=bcol[:, c3:c3 + 1],
                            op0=ALU.mult, op1=ALU.add)
        V(a_u32)
        mk_u32[j] = cV.n

        def a_spill(e, j=j, s=s):
            e.wait_ge(cV.sem, mk_u32[j])
            sPx[j % 2].inc(e.dma_start(out=dr_x1[128 * j:128 * (j + 1), :],
                                       in_=xbuf[:, s:s + D]))
            sPx[j % 2].inc(e.dma_start(out=dr_u32[128 * j:128 * (j + 1), :],
                                       in_=ubuf[:, (j % 2) * D:(j % 2) * D + D]))
        G(a_spill)
        mk_spill[j] = sPx[j % 2].n

        if j + 3 < NT:
            def a_xpre(e, j=j):
                e.wait_ge(sPx[j % 2].sem, mk_spill[j])
                sX[(j + 3) % 3].inc(e.dma_start(
                    out=xbuf[:, ((j + 3) % 3) * D:((j + 3) % 3) * D + D],
                    in_=dr_x[128 * (j + 3):128 * (j + 4), :]))
                x_marks[j + 3] = sX[(j + 3) % 3].n
            Y(a_xpre)

    if NT > 2:
        def p_x2(e):
            sX[2].inc(e.dma_start(out=xbuf[:, 2 * D:3 * D], in_=dr_x[256:384, :]))
            x_marks[2] = sX[2].n
        Y(p_x2)

    mk_h1_next = [0]
    a_pre(0)
    if NT > 1:
        a_pre(1)
    a_pre_t(0)
    if NT > 1:
        a_pre_t(1)
    mk_h1_next[0] = cS.n

    RW = D // n_cores
    NW = (RW + 127) // 128
    m_wchunk = [0] * max(NW, 1)
    absred_marks = [0] * max(NW, 1)

    def p_absz(e):
        e.memset(abscol[:, :], 0.0)
    V(p_absz)
    m_absz = cV.n

    for kk in range(NW):
        rr = min(128, RW - 128 * kk)

        def p_wld(e, kk=kk, rr=rr):
            if kk >= 2:
                e.wait_ge(cV.sem, absred_marks[kk - 2])
            sW[kk % 2].inc(e.dma_start(out=wtmp[0:rr, (kk % 2) * D:(kk % 2 + 1) * D],
                                       in_=dr_wfW[128 * kk:128 * kk + rr, :]))
            m_wchunk[kk] = sW[kk % 2].n
        G(p_wld)

        def p_absk(e, kk=kk, rr=rr):
            e.wait_ge(sW[kk % 2].sem, m_wchunk[kk])
            e.wait_ge(cV.sem, m_absz)
            e.tensor_reduce(out=abscol[0:rr, kk:kk + 1],
                            in_=wtmp[0:rr, (kk % 2) * D:(kk % 2 + 1) * D],
                            axis=AX.X, op=ALU.add, apply_absolute_value=True)
            absred_marks[kk] = cV.n
        V(p_absk)

    def p_absred(e):
        e.tensor_reduce(out=abscol[:, NCH:NCH + 1], in_=abscol[:, 0:max(NW, 1)],
                        axis=AX.X, op=ALU.add)
        e.memset(cc2row[0:1, 1:16], 0.0)
        e.memset(scol[0:1, 3:4], 1e-5)
    V(p_absred)
    m_absred = cV.n

    def p_scale_mm(e):
        e.wait_ge(sCst.sem, m_consts)
        e.wait_ge(cV.sem, m_absred)
        cT.inc(e.matmul(out=ps_aux[0:1, 0:1], lhsT=o128_sb[:], rhs=abscol[:, NCH:NCH + 1],
                        start=True, stop=True))
    T(p_scale_mm)
    m_scale_mm = cT.n

    def p_scale0(e):
        e.wait_ge(cT.sem, m_scale_mm)
        e.activation(out=cc2row[0:1, 0:1], in_=ps_aux[0:1, 0:1], func=AF.Identity)
    S(p_scale0)
    m_part = cS.n

    def p_cc2a(e):
        e.wait_ge(cS.sem, m_part)
        e.wait_ge(cV.sem, m_absred)
        sG.inc(e.dma_start(out=dr_cc2_in, in_=cc2row[:, :]))
    G(p_cc2a)
    m_cc2a = sG.n

    def p_cc2(e):
        e.wait_ge(sG.sem, m_cc2a)
        ccC.inc(e.collective_compute(
            "AllReduce", ALU.add,
            replica_groups=[list(range(n_cores))],
            ins=[dr_cc2_in.opt()],
            outs=[dr_cc2_out.opt()],
        ))
    G(p_cc2)
    m_cc2 = ccC.n

    def p_cc2b(e):
        e.wait_ge(ccC.sem, m_cc2)
        sG.inc(e.dma_start(out=cc2row[:, :], in_=dr_cc2_out))
    G(p_cc2b)
    m_cc2b = sG.n

    def p_scale(e):
        e.wait_ge(sG.sem, m_cc2b)
        e.activation(out=scol[0:1, 2:3], in_=cc2row[0:1, 0:1], func=AF.Identity,
                     scale=0.5 / (D * D))
    S(p_scale)
    m_hs = cS.n

    def p_thr(e):
        e.wait_ge(cS.sem, m_hs)
        e.tensor_scalar(out=scol[0:1, 0:1], in0=scol[0:1, 2:3],
                        scalar1=5e-11, scalar2=None, op0=ALU.add, op1=ALU.bypass)
        e.tensor_scalar(out=scol[0:1, 1:2], in0=scol[0:1, 0:1],
                        scalar1=-1.0, scalar2=None, op0=ALU.mult, op1=ALU.bypass)
    V(p_thr)
    m_thr = cV.n

    def p_thr_b(e):
        e.wait_ge(cV.sem, m_thr)
        cT.inc(e.matmul(out=ps_aux[:, 4:8], lhsT=oc_sb[:], rhs=scol[0:1, 0:4],
                        start=True, stop=True))
    T(p_thr_b)
    m_thr_b = cT.n

    def p_thr_c(e):
        e.wait_ge(cT.sem, m_thr_b)
        e.activation(out=scol[:, 0:4], in_=ps_aux[:, 4:8], func=AF.Identity)
    S(p_thr_c)
    m_thr_c = cS.n

    def p_rows1(e):
        e.wait_ge(sCst.sem, m_consts)
        e.activation(out=r1, in_=r0, func=AF.Tanh)
        e.activation(out=row_lnaa[:], in_=r1, func=AF.Ln)
    S(p_rows1)
    m_rows1 = cS.n

    def p_br0a(e):
        e.wait_ge(cS.sem, m_rows1)
        e.tensor_scalar(out=r0, in0=r1, scalar1=pcs_sb[0:1, 1:2],
                        scalar2=None, op0=ALU.max, op1=ALU.bypass)
        e.reciprocal(out=r1, in_=r0)
    V(p_br0a)
    m_br0a = cV.n

    def p_br0b(e):
        e.wait_ge(cV.sem, m_br0a)
        sCst.inc(e.dma_start(out=r0, in_=dr_vb))
    Y(p_br0b)
    m_vb1 = sCst.n

    def p_br0c(e):
        e.wait_ge(sCst.sem, m_vb1)
        e.tensor_tensor(out=r1, in0=r1, in1=r0, op=ALU.mult)
    V(p_br0c)
    m_br0c = cV.n

    def p_br0d(e):
        e.wait_ge(cV.sem, m_br0c)
        sCst.inc(e.dma_start(out=r0, in_=dr_vc))
    Y(p_br0d)
    m_vc1 = sCst.n

    def p_br0e(e):
        e.wait_ge(sCst.sem, m_vc1)
        e.tensor_tensor(out=r1, in0=r1, in1=r0, op=ALU.mult)
    V(p_br0e)
    m_br0e = cV.n

    def p_br0f(e):
        e.wait_ge(cV.sem, m_br0e)
        e.activation(out=row_br0[:], in_=r1, func=AF.Identity)
    S(p_br0f)
    m_br0s = cS.n
    m_br0 = m_br0e

    def p_r1P(e):
        e.wait_ge(cS.sem, m_rows1)
        for o in range(NO):
            cT.inc(e.matmul(out=ps_mm[:, o * 512:(o + 1) * 512], lhsT=iotaP_sb[:],
                            rhs=row_lnaa[0:1, o * 512:(o + 1) * 512], start=True, stop=True))
    T(p_r1P)
    m_r1P = cT.n

    def p_P(e):
        e.wait_ge(cT.sem, m_r1P)
        e.activation(out=Pt[:, :], in_=ps_mm[:, :], func=AF.Exp)
    S(p_P)
    m_P = cS.n

    brstage = ubuf[:, 0:D]

    def p_BR1(e):
        e.wait_ge(cT.sem, m_r1P)
        e.activation(out=brstage, in_=ps_mm[:, :], func=AF.Exp, scale=-1.0)
    S(p_BR1)
    m_BR1 = cS.n

    def p_bload(e):
        e.wait_ge(cS.sem, m_br0s)
        sCst.inc(e.dma_start(out=r1, in_=dr_vb))
    Y(p_bload)
    m_vb2 = sCst.n

    def p_r1b2(e):
        e.wait_ge(cS.sem, m_BR1)
        e.wait_ge(sCst.sem, m_vb2)
        for o in range(NO):
            cT.inc(e.matmul(out=ps_mm[:, o * 512:(o + 1) * 512], lhsT=oc_sb[:],
                            rhs=r1[0:1, o * 512:(o + 1) * 512], start=True, stop=True))
    T(p_r1b2)
    m_r1b = cT.n

    def p_BR2(e):
        e.wait_ge(cT.sem, m_r1b)
        e.tensor_tensor(out=brstage, in0=brstage, in1=ps_mm[:, :], op=ALU.mult)
    V(p_BR2)
    m_BR2 = cV.n

    def p_r1c(e):
        e.wait_ge(cV.sem, m_BR2)
        for o in range(NO):
            cT.inc(e.matmul(out=ps_mm[:, o * 512:(o + 1) * 512], lhsT=oc_sb[:],
                            rhs=r0[0:1, o * 512:(o + 1) * 512], start=True, stop=True))
    T(p_r1c)
    m_r1c = cT.n

    def p_BR3(e):
        e.wait_ge(cT.sem, m_r1c)
        e.tensor_tensor(out=BRt[:, :], in0=brstage, in1=ps_mm[:, :], op=ALU.mult)
    V(p_BR3)
    m_BR3 = cV.n
    m_psmm_free_rows = m_BR3

    def p_fb1(e):
        e.wait_ge(cV.sem, m_BR3)
        sCst.inc(e.dma_start(out=r0, in_=dr_vfb))
    Y(p_fb1)
    m_fb1 = sCst.n

    def p_fb2(e):
        e.wait_ge(sCst.sem, m_fb1)
        e.tensor_scalar(out=r1, in0=r0, scalar1=0.5, scalar2=None,
                        op0=ALU.mult, op1=ALU.bypass)
    V(p_fb2)
    m_fb2 = cV.n

    def p_fb3(e):
        e.wait_ge(cV.sem, m_fb2)
        e.activation(out=biasF[0:1, :], in_=r1, func=AF.Identity)
    S(p_fb3)
    m_fb3 = cS.n

    def p_fb4(e):
        e.wait_ge(cS.sem, m_fb3)
        e.tensor_tensor(out=rb16, in0=r1, in1=biasF[0:1, :], op=ALU.subtract)
    V(p_fb4)
    m_fb4 = cV.n

    def p_fb5(e):
        e.wait_ge(cV.sem, m_fb4)
        sG.inc(e.dma_start(out=biasF[1:2, :], in_=rb16))
    G(p_fb5)
    m_fb5 = sG.n

    def p_gb1(e):
        e.wait_ge(sG.sem, m_fb5)
        sCst.inc(e.dma_start(out=r0, in_=dr_vgb))
    Y(p_gb1)
    m_gb1 = sCst.n

    def p_gb2(e):
        e.wait_ge(sCst.sem, m_gb1)
        e.activation(out=biasG[0:1, :], in_=r0, func=AF.Identity)
    S(p_gb2)
    m_gb2 = cS.n

    def p_gb3(e):
        e.wait_ge(cS.sem, m_gb2)
        e.tensor_tensor(out=rb16, in0=r0, in1=biasG[0:1, :], op=ALU.subtract)
    V(p_gb3)
    m_gb3 = cV.n

    def p_gb4(e):
        e.wait_ge(cV.sem, m_gb3)
        sG.inc(e.dma_start(out=biasG[1:2, :], in_=rb16))
    G(p_gb4)
    m_gb4 = sG.n
    m_rows_done_S = m_gb2
    m_rows_done_V = m_gb3

    m_wchunk2 = [0] * NCH
    tern_marks = [0] * NCH
    for k in range(NCH):
        def p_wld2(e, k=k):
            if k >= 2:
                e.wait_ge(cV.sem, tern_marks[k - 2])
            else:
                e.wait_ge(cV.sem, m_absred)
            sW[k % 2].inc(e.dma_start(out=wtmp[:, (k % 2) * D:(k % 2 + 1) * D],
                                      in_=dr_wfT[128 * k:128 * (k + 1), :]))
            m_wchunk2[k] = sW[k % 2].n
        G(p_wld2)

        def p_tern(e, k=k):
            sl = wtmp[:, (k % 2) * D:(k % 2 + 1) * D]
            e.wait_ge(sW[k % 2].sem, m_wchunk2[k])
            e.wait_ge(cS.sem, m_thr_c)
            if k == 0:
                e.wait_ge(cV.sem, m_rows_done_V)
                e.wait_ge(cS.sem, m_rows_done_S)
                e.wait_ge(sG.sem, m_gb4)
                e.wait_ge(sCst.sem, sCst.n)
            e.tensor_scalar(out=ugob[:, 0:D], in0=sl,
                            scalar1=scol[:, 0:1], scalar2=scol[:, 2:3],
                            op0=ALU.is_ge, op1=ALU.mult)
            e.tensor_scalar(out=ugob[:, D:2 * D], in0=sl,
                            scalar1=scol[:, 1:2], scalar2=scol[:, 2:3],
                            op0=ALU.is_le, op1=ALU.mult)
            e.tensor_tensor(out=W16[:, k * D:(k + 1) * D], in0=ugob[:, 0:D],
                            in1=ugob[:, D:2 * D], op=ALU.subtract)
            tern_marks[k] = cV.n
        V(p_tern)

    def p_wspill(e):
        e.wait_ge(cV.sem, tern_marks[NCH - 1])
        sG.inc(e.dma_start(out=dr_wf16, in_=W16[:, :]))
    G(p_wspill)
    m_wspill = sG.n

    for j in range(NT):
        if j + 2 < NT:
            a_pre(j + 2)
        a_mm(j)
        if j + 2 < NT:
            a_pre_t(j + 2)
            mk_h1_next[0] = cS.n
        a_post(j)

    m_gchunk = [0] * NCH
    gcast_marks = [0] * NCH
    for k in range(NCH):
        def i_wld(e, k=k):
            if k >= 2:
                e.wait_ge(cS.sem, gcast_marks[k - 2])
            else:
                e.wait_ge(cV.sem, tern_marks[NCH - 1])
            sW[k % 2].inc(e.dma_start(out=wtmp[:, (k % 2) * D:(k % 2 + 1) * D],
                                      in_=dr_wgT[128 * k:128 * (k + 1), :]))
            m_gchunk[k] = sW[k % 2].n
        G(i_wld)

        def i_gcast(e, k=k):
            e.wait_ge(sW[k % 2].sem, m_gchunk[k])
            e.wait_ge(sG.sem, m_wspill)
            e.wait_ge(cT.sem, mk_mmA[NT - 1])
            e.activation(out=W16[:, k * D:(k + 1) * D],
                         in_=wtmp[:, (k % 2) * D:(k % 2 + 1) * D], func=AF.Identity)
            gcast_marks[k] = cS.n
        S(i_gcast)

    sclose("wtmp")
    gbufB = sopen("gbufB", [128, 4 * 512], F32)
    T16sb = sopen("T16sb", [16, D], F32)
    nhalf = 2 if NO > 1 else 1

    def b_t16z(e):
        e.wait_ge(cS.sem, gcast_marks[NCH - 1])
        e.memset(T16sb[:, :], 0.0)
    V(b_t16z)
    m_t16z = cV.n

    mk_u32re = [0] * NT
    mk_ug = [0] * NT
    mk_ug_spill = [0] * NT
    mk_gmm = [0] * NT
    mk_gmmc = [[0] * NO for _ in range(NT)]
    mk_sig = [0] * NT
    mk_sigc = [[0] * NO for _ in range(NT)]
    mk_uT = [0] * NT
    mk_uTev = [0] * NT
    mk_u16 = [0] * NT
    zh_marks = [0] * NQ
    zl_marks = [0] * NQ
    zmm_marks = [0] * NQ
    tadd_marks = [0] * NQ

    def b_l01(e):
        e.wait_ge(sPx[0].sem, sPx[0].n)
        e.wait_ge(sPx[1].sem, sPx[1].n)
        for jj in range(min(3, NT)):
            sU[jj % 3].inc(e.dma_start(out=ubuf[:, jj * D:(jj + 1) * D],
                                       in_=dr_u32[128 * jj:128 * (jj + 1), :]))
            mk_u32re[jj] = sU[jj % 3].n
    Y(b_l01)

    def b_pre(j):
        s = (j % 2) * D
        su = (j % 3) * D

        def b_u16(e, j=j, s=s, su=su):
            e.wait_ge(sU[j % 3].sem, mk_u32re[j])
            if j <= 1:
                e.wait_ge(cT.sem, mk_hT[NT - 1])
            if j >= 2:
                e.wait_ge(cT.sem, mk_uT[j - 2])
            e.activation(out=h16b[:, s:s + D], in_=ubuf[:, su:su + D], func=AF.Identity)

        if j < 3:
            evac_ct = mk_mmA[NT - 1]
        else:
            evac_ct = mk_gmmc[j - 3][NO - 1]
        mk_uT[j], mk_uTev[j] = emit_transpose_pass(
            j, (j % 2) * D, (j % 3) * D, evac_ct_wait=evac_ct, s_prefix=b_u16, n_prefix=1)
        mk_u16[j] = mk_uTev[j]

    def b_mm(j):
        s = (j % 3) * D
        for o in range(NO):
            def b_mmo(e, j=j, s=s, o=o):
                if o == 0:
                    e.wait_ge(cS.sem, mk_uTev[j])
                    e.wait_ge(cS.sem, gcast_marks[NCH - 1])
                    e.wait_ge(sG.sem, max(m_gb4, m_wspill))
                if j == 0:
                    e.wait_ge(cV.sem, mk_x1c[NT - 1][o])
                else:
                    e.wait_ge(cS.sem, mk_sigc[j - 1][o])
                for i in range(NCH):
                    e.matmul(out=ps_mm[:, o * 512:(o + 1) * 512],
                             lhsT=hTb[:, s + 128 * i:s + 128 * (i + 1)],
                             rhs=W16[:, i * D + o * 512:i * D + o * 512 + 512],
                             start=(i == 0), stop=False)
                cT.inc(e.matmul(out=ps_mm[:, o * 512:(o + 1) * 512], lhsT=ones2_sb[:],
                                rhs=biasG[:, o * 512:(o + 1) * 512], start=False, stop=True))
            T(b_mmo)
            mk_gmmc[j][o] = cT.n
        mk_gmm[j] = mk_gmmc[j][NO - 1]

    def b_sig_f(j):
        s = (j % 2) * D
        cv_ug_base = cV.n

        def b_sig(e, j=j, s=s, cv_ug_base=cv_ug_base):
            if j >= 1:
                e.wait_ge(cV.sem, mk_ug[j - 1])
            if j == 0:
                e.wait_ge(cV.sem, tern_marks[NCH - 1])
            for o in range(NO):
                e.wait_ge(cT.sem, mk_gmmc[j][o])
                e.activation(out=gbufB[:, o * 512:o * 512 + 512],
                             in_=ps_mm[:, o * 512:(o + 1) * 512], func=AF.Sigmoid)
        S(b_sig)
        mk_sig[j] = cS.n
        for o in range(NO):
            mk_sigc[j][o] = mk_sig[j] - (NO - 1 - o)

    def b_post(j):
        s = (j % 2) * D

        def b_ug(e, j=j, s=s):
            if j >= 2:
                e.wait_ge(sPg[j % 2].sem, mk_ug_spill[j - 2])
                e.wait_ge(cT.sem, zmm_marks[(j - 2) * NO + NO - 1])
            su = (j % 3) * D
            for o in range(NO):
                e.wait_ge(cS.sem, mk_sigc[j][o])
                e.tensor_tensor(out=ugob[:, s + o * 512:s + o * 512 + 512],
                                in0=ubuf[:, su + o * 512:su + o * 512 + 512],
                                in1=gbufB[:, o * 512:o * 512 + 512],
                                op=ALU.mult)
        V(b_ug)
        mk_ug[j] = cV.n

        def b_sp(e, j=j, s=s):
            e.wait_ge(cV.sem, mk_ug[j])
            sPg[j % 2].inc(e.dma_start(out=dr_ug[128 * j:128 * (j + 1), :],
                                       in_=ugob[:, s:s + D]))
        G(b_sp)
        mk_ug_spill[j] = sPg[j % 2].n

        if j + 3 < NT:
            def b_upre(e, j=j):
                e.wait_ge(cV.sem, mk_ug[j])
                sU[(j + 3) % 3].inc(e.dma_start(
                    out=ubuf[:, (j % 3) * D:(j % 3) * D + D],
                    in_=dr_u32[128 * (j + 3):128 * (j + 4), :]))
                mk_u32re[j + 3] = sU[(j + 3) % 3].n
            Y(b_upre)



    def b_zchain(j):
        s = (j % 2) * D
        for o in range(NO):
            q = j * NO + o
            zs = (q % 2) * 512

            def b_z(e, j=j, s=s, o=o, q=q, zs=zs):
                e.wait_ge(cV.sem, mk_ug[j])
                if q >= 2:
                    e.wait_ge(cT.sem, zmm_marks[q - 2])
                e.tensor_tensor(out=zbuf32[:, zs:zs + 512],
                                in0=ugob[:, s + o * 512:s + o * 512 + 512],
                                in1=BRt[:, o * 512:o * 512 + 512], op=ALU.mult)
                if j == 0:
                    e.wait_ge(cS.sem, m_br0s)
                    e.tensor_tensor(out=zbuf32[0:1, zs:zs + 512],
                                    in0=ugob[0:1, s + o * 512:s + o * 512 + 512],
                                    in1=row_br0[0:1, o * 512:o * 512 + 512],
                                    op=ALU.mult)
            V(b_z)
            m_z32 = cV.n

            def b_zh(e, q=q, zs=zs, m_z32=m_z32):
                if q >= 2:
                    e.wait_ge(cT.sem, zmm_marks[q - 2])
                e.tensor_copy(out=zbufh[:, zs:zs + 512], in_=zbuf32[:, zs:zs + 512])
                e.tensor_tensor(out=zbufl[:, zs:zs + 512], in0=zbuf32[:, zs:zs + 512],
                                in1=zbufh[:, zs:zs + 512], op=ALU.subtract)
            V(b_zh)
            zh_marks[q] = cV.n
            zl_marks[q] = cV.n

            def b_zmm(e, j=j, s=s, o=o, q=q, zs=zs):
                e.wait_ge(cV.sem, zl_marks[q])
                if q >= 2:
                    e.wait_ge(cV.sem, tadd_marks[q - 2])
                else:
                    e.wait_ge(cS.sem, m_thr_c)
                pa = ps_aux[0:16, zs:zs + 512]
                e.matmul(out=pa, lhsT=embZ_sb[:, 16 * j:16 * j + 16],
                         rhs=zbufh[:, zs:zs + 512], start=True, stop=False)
                e.matmul(out=pa, lhsT=embZ_sb[:, 16 * j:16 * j + 16],
                         rhs=zbufl[:, zs:zs + 512], start=False, stop=False)
                cT.inc(e.matmul(out=pa, lhsT=embU_sb[:, 16 * j:16 * j + 16],
                                rhs=ugob[:, s + o * 512:s + o * 512 + 512],
                                start=False, stop=True))
            T(b_zmm)
            zmm_marks[q] = cT.n

            def b_tadd(e, j=j, o=o, q=q, zs=zs):
                e.wait_ge(cT.sem, zmm_marks[q])
                if q == 0:
                    e.wait_ge(cV.sem, m_t16z)
                e.tensor_tensor(out=T16sb[:, o * 512:(o + 1) * 512],
                                in0=T16sb[:, o * 512:(o + 1) * 512],
                                in1=ps_aux[0:16, zs:zs + 512], op=ALU.add)
            V(b_tadd)
            tadd_marks[q] = cV.n
    b_pre(0)
    if NT > 1:
        b_pre(1)
    for j in range(NT):
        b_mm(j)
        b_sig_f(j)
        if j >= 1:
            b_zchain(j - 1)
        if j + 2 < NT:
            b_pre(j + 2)
        b_post(j)
    b_zchain(NT - 1)

    m16mm_marks = [0] * NO
    m16exp_marks = [0] * NO
    m16tt_marks = [0] * NO
    cv_base_m16 = [0]

    def c_m16a(e):
        e.wait_ge(cV.sem, tadd_marks[NQ - 1])
        cv_base = cV.n
        cv_base_m16[0] = cv_base
        for o in range(NO):
            if o >= 2:
                e.wait_ge(cV.sem, cv_base + (o - 2) + 1)
            cT.inc(e.matmul(out=ps_aux[0:16, (o % 2) * 512:(o % 2) * 512 + 512],
                            lhsT=iotaM16_sb[:],
                            rhs=row_lnaa[0:1, o * 512:(o + 1) * 512],
                            start=True, stop=True))
            m16mm_marks[o] = cT.n
    T(c_m16a)

    def c_m16b(e):
        for o in range(NO):
            e.wait_ge(cT.sem, m16mm_marks[o])
            if o >= 2:
                e.wait_ge(cV.sem, cv_base_m16[0] + (o - 2) + 1)
            e.activation(out=ps_aux[0:16, (o % 2) * 512:(o % 2) * 512 + 512],
                         in_=ps_aux[0:16, (o % 2) * 512:(o % 2) * 512 + 512],
                         func=AF.Exp)
            m16exp_marks[o] = cS.n
    S(c_m16b)

    def c_f16(e):
        for o in range(NO):
            e.wait_ge(cS.sem, m16exp_marks[o])
            e.tensor_tensor(out=T16sb[:, o * 512:(o + 1) * 512],
                            in0=T16sb[:, o * 512:(o + 1) * 512],
                            in1=ps_aux[0:16, (o % 2) * 512:(o % 2) * 512 + 512],
                            op=ALU.mult)
            m16tt_marks[o] = cV.n
    V(c_f16)
    m_f16 = cV.n

    def c_din(e):
        e.wait_ge(cV.sem, m_f16)
        sG.inc(e.dma_start(out=dr_cc_in, in_=T16sb[:, :]))
    G(c_din)
    m_ccin = sG.n

    def c_cc(e):
        e.wait_ge(sG.sem, m_ccin)
        ccC.inc(e.collective_compute(
            "AllGather", ALU.bypass,
            replica_groups=[list(range(n_cores))],
            ins=[dr_cc_in.opt()],
            outs=[dr_cc_out.opt()],
        ))
    G(c_cc)
    m_cc = ccC.n

    sclose("T16sb")
    sclose("gbufB")
    sclose("ugob")
    bscr = sopen("bscr", [NT, 2 * D], BF16)
    deff_b = sopen("deff_b", [128, D], BF16)
    s1 = sopen("s1", [128, D], F32)
    s2 = sopen("s2", [128, D], F32)

    a64_marks = []
    for half in range(nhalf):
        os_ = tuple(range(half * (NO // nhalf), (half + 1) * (NO // nhalf)))
        def c_a64a(e, os_=os_):
            e.wait_ge(cV.sem, tadd_marks[NQ - 1])
            e.wait_ge(cV.sem, m_f16)
            if a64_marks:
                e.wait_ge(cS.sem, a64_marks[-1])
            for oi, o in enumerate(os_):
                cT.inc(e.matmul(out=ps_aux[0:NR, oi * 512:oi * 512 + 512],
                                lhsT=iota64_sb[:],
                                rhs=row_lnaa[0:1, o * 512:(o + 1) * 512],
                                start=True, stop=True))
        T(c_a64a)
        m_a64a = cT.n

        def c_a64b(e, os_=os_, m_a64a=m_a64a):
            e.wait_ge(cT.sem, m_a64a)
            e.wait_ge(sG.sem, m_ccin)
            e.wait_ge(cV.sem, m_f16)
            for oi, o in enumerate(os_):
                e.activation(out=s2[0:NR, o * 512:(o + 1) * 512],
                             in_=ps_aux[0:NR, oi * 512:oi * 512 + 512], func=AF.Exp)
        S(c_a64b)
        a64_marks.append(cS.n)
    m_a64 = a64_marks[-1]

    def c_dout(e):
        e.wait_ge(ccC.sem, m_cc)
        e.wait_ge(sPg[0].sem, sPg[0].n)
        e.wait_ge(sPg[1].sem, sPg[1].n)
        for r in range(n_cores):
            sG.inc(e.dma_start(out=s1[r * NT:(r + 1) * NT, 0:D],
                               in_=dr_cc_out[16 * r:16 * r + NT, :]))
            sG.inc(e.dma_start(out=s1[64 + r * NT:64 + (r + 1) * NT, 0:D],
                               in_=dr_cc_out[16 * r + 8:16 * r + 8 + NT, :]))
    G(c_dout)
    m_ccout = sG.n

    def c_fown(e):
        e.wait_ge(cT.sem, m_dlr7_h[0])
        sG.inc(e.dma_start(out=s1[0:NT, 0:D], in_=dr_cc_in[0:NT, :]))


    def c_w64(e):
        e.wait_ge(sG.sem, m_ccout)
        e.wait_ge(cS.sem, m_a64)
        e.tensor_tensor(out=s1[0:NR, :], in0=s1[0:NR, :], in1=s2[0:NR, :], op=ALU.mult)
    V(c_w64)
    m_w64 = cV.n
    m_dlr7_h = [0]

    def c_cpre(e):
        e.wait_ge(cV.sem, m_w64)
        e.wait_ge(cS.sem, mk_sig[NT - 1])
        for o in range(NO):
            cT.inc(e.matmul(out=ps_mm[0:1, o * 512:(o + 1) * 512], lhsT=sel64_sb[:],
                            rhs=s1[0:NR, o * 512:(o + 1) * 512], start=True, stop=True))
    T(c_cpre)
    m_cpre = cT.n

    def c_apow(e):
        e.wait_ge(cT.sem, m_cpre)
        e.wait_ge(cV.sem, m_w64)
        e.activation(out=s2[0:1, 0:D], in_=row_lnaa[:], func=AF.Exp,
                     scale=pcs_sb[0:1, 0:1])
    S(c_apow)
    m_apow = cS.n

    def c_crow(e):
        e.wait_ge(cS.sem, m_apow)
        e.wait_ge(cV.sem, mk_x1_tt[NT - 1])
        e.wait_ge(sPx[0].sem, sPx[0].n)
        e.wait_ge(sPx[1].sem, sPx[1].n)
        e.tensor_tensor(out=xbuf[0:1, 0:D], in0=ps_mm[0:1, 0:D], in1=s2[0:1, 0:D],
                        op=ALU.mult)
    V(c_crow)
    m_crow = cV.n

    def c_ummm(e):
        e.wait_ge(cV.sem, m_crow)
        for o in range(NO):
            cT.inc(e.matmul(out=ps_mm[0:1, o * 512:(o + 1) * 512],
                            lhsT=o128_sb[64:64 + NR, :],
                            rhs=s1[64:64 + NR, o * 512:(o + 1) * 512],
                            start=True, stop=True))
    T(c_ummm)
    m_ummm = cT.n

    def c_um(e):
        e.wait_ge(cT.sem, m_ummm)
        e.wait_ge(cV.sem, m_crow)
        e.activation(out=s2[0:1, 0:D], in_=ps_mm[0:1, 0:D], func=AF.Identity,
                     scale=1.0 / (n_cores * T_loc))
    S(c_um)
    m_um = cS.n

    def c_dlr0(e):
        e.wait_ge(cS.sem, m_um)
        e.wait_ge(cT.sem, m_cpre)
        e.wait_ge(cV.sem, m_crow)
        sY.inc(e.dma_start(out=s1[0:4, 0:D], in_=dr_vdlru))
    Y(c_dlr0)
    m_dlru = sY.n

    def c_dlr1(e):
        e.wait_ge(cS.sem, m_um)
        for o in range(NO):
            cT.inc(e.matmul(out=ps_mm[0:4, o * 512:(o + 1) * 512], lhsT=oc_sb[0:1, 0:4],
                            rhs=s2[0:1, o * 512:(o + 1) * 512], start=True, stop=True))
    T(c_dlr1)
    m_dlr1 = cT.n

    def c_dlr2(e):
        e.wait_ge(cT.sem, m_dlr1)
        e.wait_ge(sY.sem, m_dlru)
        e.tensor_tensor(out=s1[0:4, 0:D], in0=ps_mm[0:4, 0:D], in1=s1[0:4, 0:D],
                        op=ALU.mult)
        e.tensor_reduce(out=row_m4[:], in_=s1[0:4, 0:D], axis=AX.X, op=ALU.add)
    V(c_dlr2)
    m_dlr2 = cV.n

    def c_dlr5(e):
        e.wait_ge(cV.sem, m_dlr2)
        sY.inc(e.dma_start(out=s1[0:4, 0:D], in_=dr_vdlrv))
        sY.inc(e.dma_start(out=s2[0:1, 0:D], in_=dr_vd))
    Y(c_dlr5)
    m_dlrv2 = sY.n

    def c_dlr6(e):
        e.wait_ge(sY.sem, m_dlrv2)
        e.tensor_scalar(out=s1[0:4, 0:D], in0=s1[0:4, 0:D], scalar1=row_m4[:],
                        scalar2=0.25, op0=ALU.mult, op1=ALU.mult)
    V(c_dlr6)
    m_dlr6 = cV.n

    def c_dlr7(e):
        e.wait_ge(cV.sem, m_dlr6)
        for o in range(NO):
            cT.inc(e.matmul(out=ps_mm[0:1, o * 512:(o + 1) * 512], lhsT=o128_sb[0:4, :],
                            rhs=s1[0:4, o * 512:(o + 1) * 512], start=True, stop=True))
    T(c_dlr7)
    m_dlr7 = cT.n
    m_dlr7_h[0] = m_dlr7
    G(c_fown)
    m_fown = sG.n

    def c_deffrow(e):
        e.wait_ge(cT.sem, m_dlr7)
        e.wait_ge(sY.sem, m_dlrv2)
        e.tensor_tensor(out=s2[0:1, 0:D], in0=ps_mm[0:1, 0:D], in1=s2[0:1, 0:D],
                        op=ALU.add)
    V(c_deffrow)
    m_deffrow = cV.n

    def c_deffb(e):
        e.wait_ge(cV.sem, m_deffrow)
        for o in range(NO):
            cT.inc(e.matmul(out=ps_mm[:, o * 512:(o + 1) * 512], lhsT=oc_sb[:],
                            rhs=s2[0:1, o * 512:(o + 1) * 512], start=True, stop=True))
    T(c_deffb)
    m_deffb = cT.n

    def c_deffev(e):
        e.wait_ge(cT.sem, m_deffb)
        e.activation(out=deff_b[:, :], in_=ps_mm[:, :], func=AF.Identity)
    S(c_deffev)
    m_deffev = cS.n

    a9_marks = []
    for half in range(nhalf):
        os_ = tuple(range(half * (NO // nhalf), (half + 1) * (NO // nhalf)))
        def c_a9a(e, os_=os_):
            e.wait_ge(cV.sem, tadd_marks[NQ - 1])
            if a9_marks:
                e.wait_ge(cS.sem, a9_marks[-1])
            else:
                e.wait_ge(cS.sem, m_deffev)
            for oi, o in enumerate(os_):
                cT.inc(e.matmul(out=ps_aux[0:NT + 1, oi * 512:oi * 512 + 512],
                                lhsT=iota9_sb[:],
                                rhs=row_lnaa[0:1, o * 512:(o + 1) * 512],
                                start=True, stop=True))
        T(c_a9a)
        m_a9a = cT.n

        def c_a9b(e, os_=os_, m_a9a=m_a9a):
            e.wait_ge(cT.sem, m_a9a)
            e.wait_ge(cV.sem, m_deffrow)
            for oi, o in enumerate(os_):
                e.activation(out=s2[0:NT + 1, o * 512:(o + 1) * 512],
                             in_=ps_aux[0:NT + 1, oi * 512:oi * 512 + 512], func=AF.Exp)
        S(c_a9b)
        a9_marks.append(cS.n)
    m_a9 = a9_marks[-1]

    def c_q9(e):
        e.wait_ge(cS.sem, m_deffev)
        e.wait_ge(cV.sem, m_crow)
        e.wait_ge(sG.sem, m_fown)
        for o in range(NO):
            e.matmul(out=ps_mm[0:NT + 1, o * 512:(o + 1) * 512], lhsT=tri9_sb[:],
                     rhs=s1[0:NT, o * 512:(o + 1) * 512], start=True, stop=False)
            cT.inc(e.matmul(out=ps_mm[0:NT + 1, o * 512:(o + 1) * 512],
                            lhsT=oc_sb[0:1, 0:NT + 1],
                            rhs=xbuf[0:1, o * 512:(o + 1) * 512], start=False, stop=True))
    T(c_q9)
    m_q9 = cT.n

    def c_cfull(e):
        e.wait_ge(cT.sem, m_q9)
        e.wait_ge(cS.sem, m_a9)
        e.tensor_tensor(out=s1[0:NT, :], in0=ps_mm[0:NT, 0:D], in1=s2[0:NT, 0:D],
                        op=ALU.mult)
    V(c_cfull)
    m_cfull = cV.n

    def c_chl(e):
        e.wait_ge(cV.sem, m_cfull)
        e.activation(out=bscr[:, 0:D], in_=s1[0:NT, :], func=AF.Identity)
    S(c_chl)
    m_chl = cS.n

    def c_cll(e):
        e.wait_ge(cS.sem, m_chl)
        e.tensor_tensor(out=bscr[:, D:2 * D], in0=s1[0:NT, :], in1=bscr[:, 0:D],
                        op=ALU.subtract)
    V(c_cll)
    m_cll = cV.n

    def y_wf(e):
        e.wait_ge(cT.sem, mk_gmm[NT - 1])
        e.wait_ge(sG.sem, m_wspill)
        sY2.inc(e.dma_start(out=W16[:, :], in_=dr_wf16))
    Y(y_wf)
    m_wfre = sY2.n

    mk_x1re = [0] * NT
    mk_ugre = [0] * NT
    mk_carry = [0] * NT

    csb = None
    carryP = None

    def alloc_phase_c():
        nonlocal csb, carryP
        sclose("s2")
        sclose("s1")
        csb = sopen("csb", [128, 2 * D], F32)
        carryP = [sopen(f"carryP{p}", [2, D], BF16) for p in range(2)]
    alloc_phase_c()

    def y_carry01(e):
        e.wait_ge(cV.sem, m_cll)
        sCY[0].inc(e.dma_start(out=carryP[0][0:1, :], in_=bscr[0:1, 0:D]))
        sCY[0].inc(e.dma_start(out=carryP[0][1:2, :], in_=bscr[0:1, D:2 * D]))
        mk_carry[0] = sCY[0].n
        if NT > 1:
            sCY[1].inc(e.dma_start(out=carryP[1][0:1, :], in_=bscr[1:2, 0:D]))
            sCY[1].inc(e.dma_start(out=carryP[1][1:2, :], in_=bscr[1:2, D:2 * D]))
            mk_carry[1] = sCY[1].n
    Y(y_carry01)

    def y_l01(e):
        e.wait_ge(cV.sem, mk_ug[NT - 1])
        e.wait_ge(sPg[0].sem, sPg[0].n)
        e.wait_ge(sPg[1].sem, sPg[1].n)
        e.wait_ge(cT.sem, m_q9)
        sX[0].inc(e.dma_start(out=xbuf[:, 0:D], in_=dr_x1[0:128, :]))
        mk_x1re[0] = sX[0].n
        sU[0].inc(e.dma_start(out=ubuf[:, 0:D], in_=dr_ug[0:128, :]))
        mk_ugre[0] = sU[0].n
        sX[1].inc(e.dma_start(out=xbuf[:, D:2 * D], in_=dr_x1[128:256, :]))
        mk_x1re[1] = sX[1].n
        sU[1].inc(e.dma_start(out=ubuf[:, D:2 * D], in_=dr_ug[128:256, :]))
        mk_ugre[1] = sU[1].n
    Y(y_l01)

    mk_cs = [0] * NT
    mk_x2 = [0] * NT
    mk_rb3 = [0] * NT
    mk_h3act = [0] * NT
    mk_h3T = [0] * NT
    mk_h3Tev = [0] * NT
    mk_mmC = [0] * NT
    mk_mmCc = [[0] * NO for _ in range(NT)]
    mk_out = [0] * NT
    mk_outc = [[0] * NO for _ in range(NT)]
    mk_outdma = [0] * NT
    yz_marks = [0] * NQ
    ysh_marks = [0] * NQ
    ysl_marks = [0] * NQ
    ysmm_marks = [0] * NQ
    ycs_marks = [0] * NQ

    def y_pre_chunk(j, o):
        s = (j % 2) * D
        if True:
            q = j * NO + o
            zs = (q % 2) * 512

            def y_z(e, j=j, s=s, o=o, q=q, zs=zs):
                e.wait_ge(sU[j % 3].sem, mk_ugre[j])
                if q >= 2:
                    e.wait_ge(cG.sem, ysl_marks[q - 2])
                else:
                    e.wait_ge(cV.sem, zl_marks[NQ - 1])
                e.tensor_tensor(out=zbuf32[:, zs:zs + 512],
                                in0=ubuf[:, s + o * 512:s + o * 512 + 512],
                                in1=BRt[:, o * 512:o * 512 + 512], op=ALU.mult)
                if j == 0:
                    e.tensor_tensor(out=zbuf32[0:1, zs:zs + 512],
                                    in0=ubuf[0:1, s + o * 512:s + o * 512 + 512],
                                    in1=row_br0[0:1, o * 512:o * 512 + 512],
                                    op=ALU.mult)
            V(y_z)
            m_z32 = cV.n
            yz_marks[q] = m_z32

            def y_zh(e, q=q, zs=zs):
                if q >= 2:
                    e.wait_ge(cT.sem, ysmm_marks[q - 2])
                else:
                    e.wait_ge(cT.sem, zmm_marks[NQ - 1])
                e.tensor_copy(out=zbufh[:, zs:zs + 512], in_=zbuf32[:, zs:zs + 512])
            V(y_zh)
            ysh_marks[q] = cV.n

            def y_zl(e, q=q, zs=zs):
                e.wait_ge(cV.sem, ysh_marks[q])
                e.tensor_tensor(out=zbufl[:, zs:zs + 512], in0=zbuf32[:, zs:zs + 512],
                                in1=zbufh[:, zs:zs + 512], op=ALU.subtract)
            GC(y_zl)
            ysl_marks[q] = cG.n

            def y_smm(e, j=j, o=o, q=q, zs=zs):
                e.wait_ge(cG.sem, ysl_marks[q])
                e.wait_ge(sCY[j % 2].sem, mk_carry[j])
                if q >= 2:
                    e.wait_ge(cV.sem, ycs_marks[q - 2])
                else:
                    e.wait_ge(cS.sem, m_a9)
                pa = ps_aux[:, zs:zs + 512]
                e.matmul(out=pa, lhsT=tri_sb[:], rhs=zbufh[:, zs:zs + 512],
                         start=True, stop=False)
                e.matmul(out=pa, lhsT=tri_sb[:], rhs=zbufl[:, zs:zs + 512],
                         start=False, stop=False)
                cT.inc(e.matmul(out=pa, lhsT=ones2_sb[:],
                                rhs=carryP[j % 2][:, o * 512:(o + 1) * 512],
                                start=False, stop=True))
            T(y_smm)
            ysmm_marks[q] = cT.n

            def y_cs(e, j=j, s=s, o=o, q=q, zs=zs):
                e.wait_ge(cT.sem, ysmm_marks[q])
                if j >= 2 and o == 0:
                    e.wait_ge(cS.sem, mk_h3act[j - 2])
                e.tensor_tensor(out=csb[:, s + o * 512:s + o * 512 + 512],
                                in0=Pt[:, o * 512:(o + 1) * 512],
                                in1=ps_aux[:, zs:zs + 512], op=ALU.mult)
                if o == 0:
                    e.wait_ge(sX[j % 2].sem, mk_x1re[j])
                e.tensor_tensor(out=csb[:, s + o * 512:s + o * 512 + 512],
                                in0=csb[:, s + o * 512:s + o * 512 + 512],
                                in1=xbuf[:, s + o * 512:s + o * 512 + 512],
                                op=ALU.add)
            V(y_cs)
            ycs_marks[q] = cV.n

    def y_pre_tail(j):
        s = (j % 2) * D
        mk_cs[j] = ycs_marks[j * NO + NO - 1]

        if j + 2 < NT:
            def y_cpre(e, j=j):
                e.wait_ge(cT.sem, ysmm_marks[j * NO + NO - 1])
                sCY[j % 2].inc(e.dma_start(out=carryP[j % 2][0:1, :],
                                           in_=bscr[j + 2:j + 3, 0:D]))
                sCY[j % 2].inc(e.dma_start(out=carryP[j % 2][1:2, :],
                                           in_=bscr[j + 2:j + 3, D:2 * D]))
                mk_carry[j + 2] = sCY[j % 2].n
            Y(y_cpre)

        def y_dg(e, j=j, s=s):
            e.wait_ge(cV.sem, yz_marks[j * NO + NO - 1])
            e.wait_ge(cS.sem, m_deffev)
            e.tensor_tensor(out=ubuf[:, s:s + D], in0=ubuf[:, s:s + D],
                            in1=deff_b[:, :], op=ALU.mult)
        GC(y_dg)
        mk_dg = cG.n

        def y_x2(e, j=j, s=s):
            e.wait_ge(cV.sem, mk_cs[j])
            e.wait_ge(cG.sem, mk_dg)
            e.tensor_tensor(out=csb[:, s:s + D], in0=csb[:, s:s + D],
                            in1=ubuf[:, s:s + D], op=ALU.add)
        V(y_x2)
        mk_x2[j] = cV.n

        def y_stats(e, j=j, s=s):
            if j >= 2:
                e.wait_ge(cG.sem, mk_rb3[j - 2])
            else:
                e.wait_ge(cG.sem, mk_rb2[NT - 1])
            so = (j % 2) * 6 * NO
            for c in range(NO):
                e.bn_stats(out=statb[:, so + 6 * c:so + 6 * c + 6],
                           in_=csb[:, s + 512 * c:s + 512 * (c + 1)])
            e.bn_aggr(out=mvb[:, (j % 2) * 2:(j % 2) * 2 + 2],
                      in_=statb[:, so:so + 6 * NO])
        V(y_stats)
        m_st3 = cV.n

        def y_rb(e, j=j):
            e.wait_ge(cV.sem, m_st3)
            if j >= 2:
                e.wait_ge(cS.sem, mk_h3act[j - 2])
            else:
                e.wait_ge(cV.sem, mk_u32[NT - 1])
            c2 = (j % 2)
            e.tensor_scalar(out=rcol[:, c2:c2 + 1], in0=mvb[:, 2 * c2 + 1:2 * c2 + 2],
                            scalar1=1e-5, scalar2=None, op0=ALU.add, op1=ALU.bypass)
            e.tensor_tensor(out=rcol[:, c2:c2 + 1], in0=rcol[:, c2:c2 + 1],
                            in1=mhalf[:, :], op=ALU.pow)
        GC(y_rb)
        mk_rb3[j] = cG.n

        def y_b(e, j=j):
            e.wait_ge(cG.sem, mk_rb3[j])
            c2 = (j % 2)
            e.tensor_scalar(out=bcol[:, c2:c2 + 1], in0=mvb[:, 2 * c2:2 * c2 + 1],
                            scalar1=rcol[:, c2:c2 + 1], scalar2=-1.0,
                            op0=ALU.mult, op1=ALU.mult)
        V(y_b)
        m_b3v = cV.n

        def y_h3(e, j=j, s=s):
            e.wait_ge(cV.sem, m_b3v)
            if j == 0:
                e.wait_ge(cT.sem, mk_uT[NT - 1])
            if j >= 2:
                e.wait_ge(cT.sem, mk_h3T[j - 2])
            c2 = (j % 2)
            e.activation(out=h16b[:, s:s + D], in_=csb[:, s:s + D],
                         func=AF.Identity, bias=bcol[:, c2:c2 + 1],
                         scale=rcol[:, c2:c2 + 1])

        evac_ct = (mk_gmm[NT - 1] if j < 3 else mk_mmCc[j - 3][NO - 1])
        mk_h3T[j], mk_h3Tev[j] = emit_transpose_pass(
            j, (j % 2) * D, (j % 3) * D, evac_ct_wait=evac_ct, s_prefix=y_h3, n_prefix=1)
        mk_h3act[j] = mk_h3Tev[j]

    def y_mmo_f(j, o):
        s = (j % 3) * D
        if True:
            def y_mmo(e, j=j, s=s, o=o):
                if o == 0:
                    e.wait_ge(cS.sem, mk_h3Tev[j])
                    e.wait_ge(sY2.sem, m_wfre)
                if j == 0:
                    e.wait_ge(cV.sem, m_cfull)
                    e.wait_ge(cS.sem, m_deffev)
                else:
                    e.wait_ge(cV.sem, mk_outc[j - 1][o])
                for i in range(NCH):
                    e.matmul(out=ps_mm[:, o * 512:(o + 1) * 512],
                             lhsT=hTb[:, s + 128 * i:s + 128 * (i + 1)],
                             rhs=W16[:, i * D + o * 512:i * D + o * 512 + 512],
                             start=(i == 0), stop=False)
                cT.inc(e.matmul(out=ps_mm[:, o * 512:(o + 1) * 512], lhsT=ones2_sb[:],
                                rhs=biasF[:, o * 512:(o + 1) * 512], start=False, stop=True))
            T(y_mmo)
            mk_mmCc[j][o] = cT.n

    def y_outo_f(j, o):
        s = (j % 2) * D

        def y_outo(e, j=j, s=s, o=o):
            e.wait_ge(cT.sem, mk_mmCc[j][o])
            if j >= 2 and o == 0:
                e.wait_ge(sST[j % 2].sem, mk_outdma[j - 2])
            e.tensor_tensor(out=xbuf[:, s + o * 512:s + o * 512 + 512],
                            in0=csb[:, s + o * 512:s + o * 512 + 512],
                            in1=ps_mm[:, o * 512:(o + 1) * 512], op=ALU.add)
        V(y_outo)
        mk_outc[j][o] = cV.n

    def y_post(j):
        s = (j % 2) * D
        mk_out[j] = mk_outc[j][NO - 1]

        def y_store(e, j=j, s=s):
            e.wait_ge(cV.sem, mk_out[j])
            sST[j % 2].inc(e.dma_start(out=dr_y[128 * j:128 * (j + 1), :],
                                       in_=xbuf[:, s:s + D]))
        Y(y_store)
        mk_outdma[j] = sST[j % 2].n

        if j + 2 < NT:
            def y_ldpre(e, j=j, s=s):
                e.wait_ge(sST[j % 2].sem, mk_outdma[j])
                sX[j % 2].inc(e.dma_start(out=xbuf[:, s:s + D],
                                          in_=dr_x1[128 * (j + 2):128 * (j + 3), :]))
                mk_x1re[j + 2] = sX[j % 2].n
                e.wait_ge(cV.sem, mk_x2[j])
                sU[(j + 2) % 3].inc(e.dma_start(out=ubuf[:, s:s + D],
                                                in_=dr_ug[128 * (j + 2):128 * (j + 3), :]))
                mk_ugre[j + 2] = sU[(j + 2) % 3].n
            Y(y_ldpre)

    for o in range(NO):
        y_pre_chunk(0, o)
    y_pre_tail(0)
    for j in range(NT):
        for o in range(NO):
            y_mmo_f(j, o)
            y_outo_f(j, o)
            if j + 1 < NT:
                y_pre_chunk(j + 1, o)
        mk_mmC[j] = mk_mmCc[j][NO - 1]
        if j + 1 < NT:
            y_pre_tail(j + 1)
        y_post(j)

    def fin(e):
        e.wait_ge(sST[0].sem, sST[0].n)
        e.wait_ge(sST[1].sem, sST[1].n)
    G(fin)

    for nm in reversed(list(_open)):
        sclose(nm)
    es.close()
    return nc


def _consts(n_cores, T_loc, D, core_i):
    NT = T_loc // 128
    NR = n_cores * NT
    f32 = np.float32
    c = {}
    c["tri16"] = np.triu(np.ones((128, 128), dtype=f32)).astype(BF)
    c["id16"] = np.eye(128, dtype=f32).astype(BF)
    c["ones2_16"] = np.ones((2, 128), dtype=f32).astype(BF)
    embZ = np.zeros((128, 16 * NT), dtype=f32)
    embU = np.zeros((128, 16 * NT), dtype=f32)
    for j in range(NT):
        embZ[:, 16 * j + j] = 1.0
        embU[:, 16 * j + 8 + j] = 1.0
    c["embZ16"] = embZ.astype(BF)
    c["embU32"] = embU
    t9 = np.zeros((NT, NT + 1), dtype=f32)
    for k in range(NT):
        for jj in range(NT + 1):
            if k < jj:
                t9[k, jj] = 1.0
    c["tri9"] = t9
    c["onescol"] = np.ones((1, 128), dtype=f32)
    c["ones128"] = np.ones((128, 1), dtype=f32)
    c["iotaP"] = np.arange(1, 129, dtype=f32).reshape(1, 128)
    im16 = np.zeros((1, 16), dtype=f32)
    im16[0, :8] = -128.0 * np.arange(8)
    c["iotaM16"] = im16
    c["iota9"] = (128.0 * np.arange(NT + 1, dtype=f32)).reshape(1, NT + 1)
    c["iota64"] = np.array([-float(T_loc) * (r // NT) for r in range(NR)],
                           dtype=f32).reshape(1, NR)
    sel = np.zeros((NR, 1), dtype=f32)
    for r in range(NR):
        if (r // NT) < core_i:
            sel[r, 0] = 1.0
    c["sel64"] = sel
    pcs = np.zeros((1, 4), dtype=f32)
    pcs[0, 0] = float(T_loc) * core_i
    pcs[0, 1] = 1e-6 if core_i == 0 else -1e30
    c["pcs"] = pcs
    return c


def _run(x_full, wfT, wgT, vd, n_cores, T_loc, D):
    nc = build_core_program(n_cores, T_loc, D)
    in_maps = []
    for i in range(n_cores):
        m = dict(_consts(n_cores, T_loc, D, i))
        m["x"] = np.ascontiguousarray(x_full[i * T_loc:(i + 1) * T_loc])
        m["wfT"] = wfT
        RW = wfT.shape[0] // n_cores
        m["wfW"] = np.ascontiguousarray(wfT[i * RW:(i + 1) * RW])
        m["wgT"] = wgT
        m.update(vd)
        in_maps.append(m)
    res = run_bass_kernel_spmd(nc, in_maps, core_ids=list(range(n_cores)))
    return np.concatenate([np.asarray(r["y"]) for r in res.results], axis=0)


def kernel(L0_attn_u, L0_attn_a, L0_attn_b, L0_attn_c, L0_attn_d,
           L0_attn_gate_w, L0_attn_gate_b, L0_attn_d_lr_u, L0_attn_d_lr_v,
           L0_ffn_weight, L0_ffn_bias):
    n_cores = 8
    B, Tn, D = L0_attn_u.shape
    T_loc = Tn // n_cores
    x = np.ascontiguousarray(np.asarray(L0_attn_u, dtype=np.float32).reshape(Tn, D))
    wfT = np.ascontiguousarray(np.asarray(L0_ffn_weight, dtype=np.float32).T)
    wgT = np.ascontiguousarray(np.asarray(L0_attn_gate_w, dtype=np.float32).T)
    vd = {
        "va": np.asarray(L0_attn_a, dtype=np.float32).reshape(1, D),
        "vb": np.asarray(L0_attn_b, dtype=np.float32).reshape(1, D),
        "vc": np.asarray(L0_attn_c, dtype=np.float32).reshape(1, D),
        "vd": np.asarray(L0_attn_d, dtype=np.float32).reshape(1, D),
        "vgb": np.asarray(L0_attn_gate_b, dtype=np.float32).reshape(1, D),
        "vfb": np.asarray(L0_ffn_bias, dtype=np.float32).reshape(1, D),
        "vdlrv": np.ascontiguousarray(np.asarray(L0_attn_d_lr_v, dtype=np.float32)),
        "vdlru": np.ascontiguousarray(np.asarray(L0_attn_d_lr_u, dtype=np.float32).T),
    }
    out = _run(x, wfT, wgT, vd, n_cores, T_loc, D)
    return out.reshape(B, Tn, D).astype(np.float32)
